# version 20
# baseline (speedup 1.0000x reference)
import numpy as np
import ml_dtypes
from contextlib import ExitStack
from collections import defaultdict

import concourse.bass as bass
import concourse.mybir as mybir
from concourse.bass_utils import run_bass_kernel_spmd

F32 = mybir.dt.float32
BF16 = mybir.dt.bfloat16
I8 = mybir.dt.int8
I16 = mybir.dt.int16
QUANT_MODE = "cast"
DEBUG_TAP = None
AF = mybir.ActivationFunctionType
ALU = mybir.AluOpType
AX = mybir.AxisListType

D = 1024
DCH = 8
HEADS = 8
DH = 16
II = 2816
ICH = 22
NEG_BIG = -3.0e38
N_CORES = 8


class Ctl:
    def __init__(self, nc, ctx):
        self.nc = nc
        self.sems = {}
        self.cnt = defaultdict(int)
        self.wm = defaultdict(int)
        for name in ("s_ld", "s_sc", "s_gd", "s_cc", "s_te", "s_ve", "s_se", "s_gp"):
            self.sems[name] = ctx.enter_context(nc.semaphore(name))
        self.eng_sem = {"tensor": "s_te", "vector": "s_ve", "scalar": "s_se",
                        "gpsimd": "s_gp"}

    def inc(self, instr, name, n=1):
        instr.then_inc(self.sems[name], n)
        self.cnt[name] += n
        return (name, self.cnt[name])

    def inc_dma(self, instr, name):
        return self.inc(instr, name, 16)

    def tok(self, name):
        return (name, self.cnt[name])

    def wait(self, eng, tok, eng_name=None):
        if tok is None:
            return
        name, v = tok
        if v <= 0:
            return
        if eng_name is not None and self.eng_sem.get(eng_name) == name:
            return
        key = (id(eng), name)
        if self.wm[key] >= v:
            return
        eng.wait_ge(self.sems[name], v)
        self.wm[key] = v

    def waits(self, eng, toks, eng_name=None):
        for t in toks:
            self.wait(eng, t, eng_name)


def build_nc(T=2048):
    TCH = T // 128
    TT = T // 512
    assert T % 512 == 0

    nc = bass.Bass("TRN2", target_bir_lowering=False, debug=False,
                   detect_race_conditions=False)

    x_in = nc.dram_tensor("x", [T, D], F32, kind="ExternalInput").ap()
    gamma_in = nc.dram_tensor("gamma", [D], F32, kind="ExternalInput").ap()
    wq_in = nc.dram_tensor("wq", [D, 128], F32, kind="ExternalInput").ap()
    wkv_in = nc.dram_tensor("wkv", [D, 256], F32, kind="ExternalInput").ap()
    wout_in = nc.dram_tensor("wout", [128, D], F32, kind="ExternalInput").ap()
    clipx_in = nc.dram_tensor("clipx", [1], F32, kind="ExternalInput").ap()
    clipd_in = nc.dram_tensor("clipd", [1], F32, kind="ExternalInput").ap()
    wg_sh_in = nc.dram_tensor("wg_sh", [352, D], F32, kind="ExternalInput").ap()
    wu_sh_in = nc.dram_tensor("wu_sh", [352, D], F32, kind="ExternalInput").ap()
    wd_sh_in = nc.dram_tensor("wd_sh", [128, II], F32, kind="ExternalInput").ap()
    cvec_in = nc.dram_tensor("cvec", [8, 128], F32, kind="ExternalInput").ap()
    wmod_sh_in = nc.dram_tensor("wmod_sh", [D, 1536], F32, kind="ExternalInput").ap()
    y_out = nc.dram_tensor("y", [T, D], F32, kind="ExternalOutput").ap()

    tb_tap = nc.dram_tensor("tb_tap", [128, 4096], F32, kind="Internal").ap()
    h_spill = nc.dram_tensor("h_spill", [TCH, 128, D], F32, kind="Internal").ap()
    h2_spill = nc.dram_tensor("h2_spill", [ICH, 128, T], F32, kind="Internal").ap()

    cc_mod_in = nc.dram_tensor("cc_mod_in", [1536], F32, kind="Internal").ap()
    cc_mod_out = nc.dram_tensor("cc_mod_out", [6144], F32, kind="Internal").ap()
    cc_wg_in = nc.dram_tensor("cc_wg_in", [D, 352], BF16, kind="Internal").ap()
    cc_wg_out = nc.dram_tensor("cc_wg_out", [8 * D, 352], BF16, kind="Internal").ap()
    cc_wu_in = nc.dram_tensor("cc_wu_in", [D, 352], BF16, kind="Internal").ap()
    cc_wu_out = nc.dram_tensor("cc_wu_out", [8 * D, 352], BF16, kind="Internal").ap()
    cc_wd_in = nc.dram_tensor("cc_wd_in", [II, 128], BF16, kind="Internal").ap()
    cc_wd_out = nc.dram_tensor("cc_wd_out", [8 * II, 128], BF16, kind="Internal").ap()
    cc_sg_in = nc.dram_tensor("cc_sg_in", [352], F32, kind="Internal").ap()
    cc_sg_out = nc.dram_tensor("cc_sg_out", [II], F32, kind="Internal").ap()
    cc_su_in = nc.dram_tensor("cc_su_in", [352], F32, kind="Internal").ap()
    cc_su_out = nc.dram_tensor("cc_su_out", [II], F32, kind="Internal").ap()
    cc_sd_in = nc.dram_tensor("cc_sd_in", [128], F32, kind="Internal").ap()
    cc_sd_out = nc.dram_tensor("cc_sd_out", [D], F32, kind="Internal").ap()
    cc_st_in = nc.dram_tensor("cc_st_in", [113, 819], F32, kind="Internal").ap()
    cc_st_out = nc.dram_tensor("cc_st_out", [113, 819], F32, kind="Internal").ap()
    cc_m1_in = nc.dram_tensor("cc_m1_in", [1], F32, kind="Internal").ap()
    cc_m1_out = nc.dram_tensor("cc_m1_out", [1], F32, kind="Internal").ap()
    cc_m2_in = nc.dram_tensor("cc_m2_in", [1], F32, kind="Internal").ap()
    cc_m2_out = nc.dram_tensor("cc_m2_out", [1], F32, kind="Internal").ap()

    eyef_d = nc.inline_tensor(np.eye(128, dtype=np.float32), "eyef_c").ap()
    eyeb_d = nc.inline_tensor(np.eye(128, dtype=ml_dtypes.bfloat16), "eyeb_c").ap()
    onesf_d = nc.inline_tensor(np.ones((8, 128), dtype=np.float32), "onesf_c").ap()
    onesb_d = nc.inline_tensor(np.ones((8, 512), dtype=ml_dtypes.bfloat16), "onesb_c").ap()

    top = nc.sbuf_top
    off = [nc.sbuf_base]
    tensors = {}

    def alloc(name, shape, dtype, cursor=None):
        cur = off if cursor is None else cursor
        per_part = int(np.prod(shape[1:])) * mybir.dt.size(dtype)
        o = (cur[0] + 31) // 32 * 32
        cur[0] = o + per_part
        assert cur[0] <= top, f"SBUF overflow at {name}: {cur[0]} > {top}"
        h = nc.alloc_sbuf_tensor_at(name, list(shape), dtype, offset=o)
        tensors[name] = h
        return h

    xres = alloc("xres", [128, TCH, D], F32)
    eyef = alloc("eyef", [128, 128], F32)
    eyeb = alloc("eyeb", [128, 128], BF16)
    onesf = alloc("onesf", [8, 128], F32)
    onesb = alloc("onesb", [8, 512], BF16)
    cells = alloc("cells", [8, 16], F32)
    gcol = alloc("gcol", [128, 4], F32)
    gamcol = alloc("gamcol", [128, 16], F32)
    siluct = alloc("siluct", [128, 8], BF16)
    statsum = alloc("statsum", [128, TCH], F32)
    statssq = alloc("statssq", [128, TCH], F32)
    statmu = alloc("statmu", [128, TCH], F32)
    statrstd = alloc("statrstd", [128, TCH], F32)
    statB = alloc("statB", [128, TCH], F32)
    statr = alloc("statr", [128, TCH], F32)
    maxs = alloc("maxs", [128, TCH], F32)
    mins = alloc("mins", [128, TCH], F32)
    hmaxcol = alloc("hmaxcol", [128, 2], F32)
    sgcol = alloc("sgcol", [128, ICH], F32)
    sucol = alloc("sucol", [128, ICH], F32)
    sdrow = alloc("sdrow", [8, D], F32)
    sgxcol = alloc("sgxcol", [128, ICH], F32)
    suxcol = alloc("suxcol", [128, ICH], F32)
    wqb = alloc("wqb", [128, DCH, 128], BF16)
    wkvb = alloc("wkvb", [128, DCH, 256], BF16)
    woutb = alloc("woutb", [48, HEADS, D], BF16)
    mrow = alloc("mrow", [8, D], F32)
    garow = alloc("garow", [8, D], F32)
    modrepA = alloc("modrepA", [128, D], F32)
    modrepB = alloc("modrepB", [128, D], F32)
    qtok = alloc("qtok", [128, TCH, 128], BF16)
    stacc = alloc("stacc", [113, 819], F32)
    stA = alloc("stA", [16, HEADS, 48], BF16)
    stB = alloc("stB", [128, HEADS, 48], BF16)
    stC = alloc("stC", [128, HEADS, 48], BF16)
    stD = alloc("stD", [8, HEADS, 48], BF16)
    scratch = alloc("scratch", [128, D], F32)
    scrows = alloc("scrows", [22, 128], F32)
    junkpad = alloc("junkpad", [1, 4], F32)

    def vsp(v, n=2):
        for _ in range(n):
            v.memset(junkpad.ap(), 0.0)

    def vflush(v):
        vsp(v, 2)
        return v.memset(junkpad.ap(), 0.0)

    arena0 = off[0]

    a = [arena0]
    gamrow = alloc("gamrow", [1, D], F32, a)
    crow = alloc("crow", [8, 128], F32, a)
    wqf = alloc("wqf", [128, DCH, 128], F32, a)
    wkvf = alloc("wkvf", [128, DCH, 256], F32, a)
    wmodb = alloc("wmodb", [128, DCH, 384], BF16, a)
    wshf = alloc("wshf", [128, D], F32, a)
    wshu = alloc("wshu", [128, D], F32, a)
    wshb = alloc("wshb", [128, D], BF16, a)
    wsh8 = alloc("wsh8", [128, D], I8, a)
    wsh16 = alloc("wsh16", [128, D], I16, a)
    wshr = alloc("wshr", [128, D], F32, a)
    wdf = alloc("wdf", [128, II], F32, a)
    wdu = alloc("wdu", [128, II // 2], F32, a)
    wdb = alloc("wdb", [128, II // 2], BF16, a)
    wd8 = alloc("wd8", [128, II // 2], I8, a)
    wd16 = alloc("wd16", [128, II // 2], I16, a)
    wdr = alloc("wdr", [128, II // 2], F32, a)
    wdg = alloc("wdg", [128, II // 2], F32, a)
    modstage = alloc("modstage", [1, 1536], F32, a)
    ptile = alloc("ptile", [128, 1024], BF16, a)
    p1_end = a[0]

    a = [arena0]
    hstage = alloc("hstage", [128, 2, D], F32, a)
    hfeat = alloc("hfeat", [128, 2, DCH, 512], BF16, a)
    kfeat = alloc("kfeat", [128, 2, 512], BF16, a)
    vfeat = alloc("vfeat", [128, 2, 512], BF16, a)
    qfeatr = alloc("qfeatr", [128, 2, 512], BF16, a)
    ktok = alloc("ktok", [128, 2, 128], BF16, a)
    phiaug = alloc("phiaug", [128, 2, HEADS, 273], BF16, a)
    vaug = alloc("vaug", [128, 2, HEADS, 17], BF16, a)
    p2_end = a[0]

    a = [arena0]
    phi2q4 = alloc("phi2q4", [128, 4, 256], BF16, a)
    phiqA = alloc("phiqA", [128, 512], BF16, a)
    phiqB = alloc("phiqB", [128, 512], BF16, a)
    qf16 = alloc("qf16", [16, 512], BF16, a)
    ofd = alloc("ofd", [48, HEADS, 512], BF16, a)
    denr = alloc("denr", [8, HEADS, 512], F32, a)
    denr2 = alloc("denr2", [1, HEADS, 512], F32, a)
    p5_end = a[0]

    a = [arena0]
    xqfeat = alloc("xqfeat", [128, DCH, T], BF16, a)
    hq = alloc("hq", [128, D], F32, a)
    xq8 = alloc("xq8", [128, D], I8, a)
    xq16 = alloc("xq16", [128, D], I16, a)
    xqf = alloc("xqf", [128, D], F32, a)
    nxq = alloc("nxq", [128, D], BF16, a)
    wgs = alloc("wgs", [128, 2, DCH, 512], BF16, a)
    wus = alloc("wus", [128, 2, DCH, 512], BF16, a)
    silut = alloc("silut", [128, 512], F32, a)
    upt = alloc("upt", [128, 512], F32, a)
    h2st = alloc("h2st", [128, 512], F32, a)
    p8_end = a[0]

    a = [arena0]
    wdTs = alloc("wdTs", [128, 2, D], BF16, a)
    rld = alloc("rld", [128, 2, 512], F32, a)
    x28 = alloc("x28", [128, 2, 512], I8, a)
    x216 = alloc("x216", [128, 2, 512], I16, a)
    x2f = alloc("x2f", [128, 2, 512], F32, a)
    x2g = alloc("x2g", [128, 2, 512], F32, a)
    u2 = alloc("u2", [128, 2, 512], F32, a)
    nxq2 = alloc("nxq2", [128, 2, 512], BF16, a)
    p10_end = a[0]

    banks = [nc.alloc_psum_tensor(f"bank{i}", [128, 512], F32) for i in range(8)]

    ctx = ExitStack()
    ctl = Ctl(nc, ctx)
    block = ctx.enter_context(nc.Block())

    bank_free = [None] * 8
    toks = {}

    @block.sync
    def _(sync):
        toks["x"] = ctl.inc_dma(sync.dma_start(
            out=xres.ap(), in_=x_in.rearrange("(c p) d -> p c d", p=128)), "s_ld")
        toks["eyef"] = ctl.inc_dma(sync.dma_start(out=eyef.ap(), in_=eyef_d), "s_ld")
        toks["eyeb"] = ctl.inc_dma(sync.dma_start(out=eyeb.ap(), in_=eyeb_d), "s_ld")
        toks["onesf"] = ctl.inc_dma(sync.dma_start(out=onesf.ap(), in_=onesf_d), "s_ld")
        toks["onesb"] = ctl.inc_dma(sync.dma_start(out=onesb.ap(), in_=onesb_d), "s_ld")
        toks["gam"] = ctl.inc_dma(sync.dma_start(out=gamrow.ap(), in_=gamma_in[None, :]), "s_ld")
        toks["crow"] = ctl.inc_dma(sync.dma_start(out=crow.ap(), in_=cvec_in), "s_ld")
        toks["wqf"] = ctl.inc_dma(sync.dma_start(
            out=wqf.ap(), in_=wq_in.rearrange("(c p) m -> p c m", p=128)), "s_ld")
        toks["wkvf"] = ctl.inc_dma(sync.dma_start(
            out=wkvf.ap(), in_=wkv_in.rearrange("(c p) m -> p c m", p=128)), "s_ld")
        toks["wdf"] = ctl.inc_dma(sync.dma_start(out=wdf.ap(), in_=wd_sh_in), "s_ld")

    @block.gpsimd
    def _(g):
        toks["wmod0"] = ctl.inc_dma(g.dma_start(
            out=wmodb.ap(),
            in_=wmod_sh_in.rearrange("(c p) m -> p c m", p=128)[:, :, 0:384]), "s_gd")
        toks["woutb"] = ctl.inc_dma(g.dma_start(
            out=woutb.ap()[32:48, :, :],
            in_=wout_in.rearrange("(h e) d -> e h d", h=8)), "s_gd")

    @block.vector
    def _(v):
        v.memset(stacc.ap(), 0.0)
        v.memset(stA.ap(), 0.0)
        v.memset(stB.ap(), 0.0)
        v.memset(stC.ap(), 0.0)
        v.memset(stD.ap(), 0.0)
        v.memset(mrow.ap(), 0.0)
        v.memset(garow.ap(), 0.0)
        v.memset(sdrow.ap(), 0.0)
        v.memset(cells.ap(), 0.0)
        v.memset(hmaxcol.ap(), NEG_BIG)
        for b in (5, 6, 7):
            i = v.memset(banks[b].ap()[:, 0:512], 0.0)
        ctl.inc(i, "s_ve")
    toks["memset"] = ctl.tok("s_ve")

    @block.sync
    def _(sync):
        ctl.wait(sync, toks["memset"])
        toks["clip"] = ctl.inc_dma(sync.dma_start(out=cells.ap()[0:1, 0:1],
                                                  in_=clipx_in[None, :]), "s_ld")
        toks["clip2"] = ctl.inc_dma(sync.dma_start(out=cells.ap()[0:1, 1:2],
                                                   in_=clipd_in[None, :]), "s_ld")

    @block.scalar
    def _(se):
        ctl.waits(se, [toks["crow"], toks["clip"], toks["clip2"]], "scalar")
        se.activation(out=crow.ap(), in_=crow.ap(), func=AF.Silu)
        se.activation(out=cells.ap()[0:1, 2:3], in_=cells.ap()[0:1, 0:1], func=AF.Sigmoid)
        i = se.activation(out=cells.ap()[0:1, 3:4], in_=cells.ap()[0:1, 1:2], func=AF.Sigmoid)
        ctl.inc(i, "s_se")
    toks["siluc"] = ctl.tok("s_se")

    @block.tensor
    def _(te):
        ctl.waits(te, [toks["gam"], toks["eyef"], toks["siluc"]], "tensor")
        for ch in range(DCH):
            te.transpose(out=banks[0].ap()[:, ch:ch + 1],
                         in_=gamrow.ap()[0:1, ch * 128:(ch + 1) * 128],
                         identity=eyef.ap()[0:1, 0:1])
        i = te.transpose(out=banks[0].ap()[:, 8:16], in_=crow.ap(),
                         identity=eyef.ap()[0:8, 0:8])
        ctl.inc(i, "s_te")
    toks["cols_mm"] = ctl.tok("s_te")

    @block.scalar
    def _(se):
        ctl.wait(se, toks["cols_mm"], "scalar")
        se.activation(out=gamcol.ap()[:, 0:8], in_=banks[0].ap()[:, 0:8], func=AF.Copy)
        se.activation(out=gamcol.ap()[:, 8:16], in_=banks[0].ap()[:, 0:8],
                      func=AF.Copy, scale=0.25)
        i = se.activation(out=siluct.ap(), in_=banks[0].ap()[:, 8:16], func=AF.Copy)
        ctl.inc(i, "s_se")
    toks["cols"] = ctl.tok("s_se")
    bank_free[0] = toks["cols"]

    @block.vector
    def _(v):
        ctl.waits(v, [toks["wqf"], toks["wkvf"], toks["cols"]], "vector")
        for ch in range(DCH):
            v.tensor_scalar(out=wqb.ap()[:, ch, :], in0=wqf.ap()[:, ch, :],
                            scalar1=gamcol.ap()[:, 8 + ch:9 + ch], scalar2=None, op0=ALU.mult)
            i = v.tensor_scalar(out=wkvb.ap()[:, ch, :], in0=wkvf.ap()[:, ch, :],
                                scalar1=gamcol.ap()[:, ch:ch + 1], scalar2=None, op0=ALU.mult)
        ctl.inc(i, "s_ve")
    toks["wproj"] = ctl.tok("s_ve")

    t_mvq = toks["wmod0"]
    for qd in range(4):
        bq = 1 + (qd % 2)

        @block.tensor
        def _(te, qd=qd, bq=bq, t_mvq=t_mvq):
            ctl.waits(te, [toks["cols"], t_mvq, bank_free[bq]], "tensor")
            for ch in range(DCH):
                i = te.matmul(out=banks[bq].ap()[0:1, 0:384], lhsT=siluct.ap()[:, ch:ch + 1],
                              rhs=wmodb.ap()[:, ch, 0:384],
                              start=(ch == 0), stop=(ch == DCH - 1))
            ctl.inc(i, "s_te")
        t_mv = ctl.tok("s_te")

        @block.scalar
        def _(se, qd=qd, bq=bq, t_mv=t_mv):
            ctl.wait(se, t_mv, "scalar")
            i = se.activation(out=modstage.ap()[0:1, qd * 384:(qd + 1) * 384],
                              in_=banks[bq].ap()[0:1, 0:384], func=AF.Copy)
            ctl.inc(i, "s_se")
        bank_free[bq] = ctl.tok("s_se")

        if qd < 3:
            @block.gpsimd
            def _(g, qd=qd, t_mv=t_mv):
                ctl.wait(g, t_mv)
                toks["wmodq"] = ctl.inc_dma(g.dma_start(
                    out=wmodb.ap(),
                    in_=wmod_sh_in.rearrange("(c p) m -> p c m", p=128)[
                        :, :, (qd + 1) * 384:(qd + 2) * 384]), "s_gd")
            t_mvq = toks["wmodq"]

    @block.gpsimd
    def _(g):
        ctl.wait(g, ctl.tok("s_se"))
        toks["modst"] = ctl.inc_dma(g.dma_start(out=cc_mod_in[None, :], in_=modstage.ap()[0:1, :]), "s_gd")
        ctl.wait(g, toks["modst"])
        toks["ag_mod"] = ctl.inc(g.collective_compute(
            "AllGather", ALU.bypass, replica_groups=[[0, 1, 2, 3], [4, 5, 6, 7]],
            ins=[cc_mod_in], outs=[cc_mod_out]), "s_cc")

    prev_wsh_read = None
    prev_wsh_store = None

    def emit_wscale(rows, wt_ap, wtok, prev_store):
        @block.vector
        def _(v):
            ctl.waits(v, [wtok, prev_store], "vector")
            i = v.tensor_reduce(out=scratch.ap()[0:rows, 0:1], in_=wt_ap, axis=AX.X,
                                op=ALU.max, apply_absolute_value=True)
            ctl.inc(i, "s_ve")
        t_red = ctl.tok("s_ve")

        @block.scalar
        def _(se):
            ctl.wait(se, t_red, "scalar")
            i = se.activation(out=scratch.ap()[0:rows, 3:4], in_=scratch.ap()[0:rows, 0:1],
                              func=AF.Copy, scale=1.0 / 127.0)
            ctl.inc(i, "s_se")
        t_c3 = ctl.tok("s_se")

        @block.vector
        def _(v):
            ctl.wait(v, t_c3, "vector")
            i = v.reciprocal(out=scratch.ap()[0:rows, 2:3], in_=scratch.ap()[0:rows, 3:4])
            ctl.inc(i, "s_ve")
        return ctl.tok("s_ve")

    def emit_quant(rows, cols, wt_ap, wu_ap, wb_ap, w8_ap, w16_ap, wr_ap, wg_ap,
                   t_rs):

        @block.scalar
        def _(se):
            ctl.wait(se, t_rs, "scalar")
            i = se.activation(out=wr_ap, in_=wt_ap, func=AF.Copy,
                              scale=scratch.ap()[0:rows, 2:3])
            ctl.inc(i, "s_se")
        t_u = ctl.tok("s_se")

        if QUANT_MODE == "cast":
            @block.vector
            def _(v):
                ctl.wait(v, t_u, "vector")
                v.tensor_copy(out=w8_ap, in_=wr_ap)
                vsp(v)
                v.tensor_copy(out=wb_ap, in_=w8_ap)
                i = vflush(v)
                ctl.inc(i, "s_ve")
        else:
            @block.vector
            def _(v):
                ctl.wait(v, t_u, "vector")
                v.tensor_copy(out=w16_ap, in_=wr_ap)
                vsp(v)
                v.tensor_copy(out=wu_ap, in_=w16_ap)
                vsp(v)
                v.tensor_tensor(out=wr_ap, in0=wr_ap, in1=wu_ap, op=ALU.subtract)
                vsp(v)
                v.tensor_scalar(out=wg_ap, in0=wr_ap, scalar1=0.5, scalar2=None, op0=ALU.is_ge)
                v.tensor_scalar(out=wr_ap, in0=wr_ap, scalar1=-0.5, scalar2=-1.0,
                                op0=ALU.is_ge, op1=ALU.add)
                vsp(v)
                v.tensor_tensor(out=wg_ap, in0=wg_ap, in1=wr_ap, op=ALU.add)
                vsp(v)
                v.tensor_tensor(out=wu_ap, in0=wu_ap, in1=wg_ap, op=ALU.add)
                vsp(v)
                v.tensor_scalar(out=wb_ap, in0=wu_ap, scalar1=-128.0, scalar2=127.0,
                                op0=ALU.max, op1=ALU.min)
                i = vflush(v)
                ctl.inc(i, "s_ve")
        return t_u, ctl.tok("s_ve")

    wg_tiles = [(0, 128), (128, 256), (256, 352)]
    prev_ptile = None
    for w_in, cc_w, cc_s in ((wg_sh_in, cc_wg_in, cc_sg_in),
                             (wu_sh_in, cc_wu_in, cc_su_in)):
        for (r0, r1) in wg_tiles:
            rows = r1 - r0

            @block.sync
            def _(sync, w_in=w_in, r0=r0, r1=r1, rows=rows):
                ctl.waits(sync, [prev_wsh_read])
                toks["wsh_ld"] = ctl.inc_dma(sync.dma_start(
                    out=wshf.ap()[0:rows, :], in_=w_in[r0:r1, :]), "s_ld")

            t_rs = emit_wscale(rows, wshf.ap()[0:rows, :], toks["wsh_ld"], prev_wsh_store)
            t_u, t_q = emit_quant(rows, D, wshf.ap()[0:rows, :], wshu.ap()[0:rows, :],
                                  wshb.ap()[0:rows, :], wsh8.ap()[0:rows, :],
                                  wsh16.ap()[0:rows, :], wshr.ap()[0:rows, :],
                                  wshf.ap()[0:rows, :], t_rs)
            prev_wsh_read = t_q

            @block.tensor
            def _(te, rows=rows, t_q=t_q, prev_ptile=prev_ptile):
                ctl.waits(te, [t_q, toks["eyeb"], bank_free[3], prev_ptile], "tensor")
                bv = banks[3].ap().bitcast(BF16)
                for dch in range(DCH):
                    i = te.transpose(out=bv[:, dch * 128:dch * 128 + rows],
                                     in_=wshb.ap()[0:rows, dch * 128:(dch + 1) * 128],
                                     identity=eyeb.ap()[0:rows, 0:rows])
                ctl.inc(i, "s_te")
            t_wT = ctl.tok("s_te")

            @block.scalar
            def _(se, rows=rows, t_wT=t_wT, prev_ptile=prev_ptile):
                ctl.waits(se, [t_wT, prev_ptile], "scalar")
                bv = banks[3].ap().bitcast(BF16)
                for dch in range(DCH):
                    i = se.activation(out=ptile.ap()[:, dch * 128:dch * 128 + rows],
                                      in_=bv[:, dch * 128:dch * 128 + rows], func=AF.Copy)
                ctl.inc(i, "s_se")
            t_wTe = ctl.tok("s_se")
            bank_free[3] = t_wTe

            @block.sync
            def _(sync, cc_w=cc_w, cc_s=cc_s, r0=r0, r1=r1, rows=rows, t_q=t_q,
                  t_wTe=t_wTe):
                ctl.waits(sync, [t_q, t_wTe])
                for dch in range(DCH):
                    ctl.inc_dma(sync.dma_start(
                        out=cc_w[dch * 128:(dch + 1) * 128, r0:r1],
                        in_=ptile.ap()[:, dch * 128:dch * 128 + rows]), "s_ld")
                ctl.inc_dma(sync.dma_start(out=cc_s[r0:r1],
                                           in_=scratch.ap()[0:rows, 3:4]), "s_ld")
            prev_wsh_store = ctl.tok("s_ld")
            prev_ptile = prev_wsh_store

    t_rs = emit_wscale(128, wdf.ap(), toks["wdf"], prev_wsh_store)
    prev_half_store = None
    for half in range(2):
        hsl = slice(half * (II // 2), (half + 1) * (II // 2))
        if prev_half_store is not None:
            @block.vector
            def _(v, prev_half_store=prev_half_store):
                ctl.wait(v, prev_half_store, "vector")
                ctl.inc(v.engine_nop(), "s_ve")
            t_rs2 = ctl.tok("s_ve")
        else:
            t_rs2 = t_rs
        t_u, t_qd = emit_quant(128, II // 2, wdf.ap()[:, hsl], wdu.ap(), wdb.ap(),
                               wd8.ap(), wd16.ap(), wdr.ap(), wdg.ap(), t_rs2)

        h0 = half * (II // 2)
        for grp in range(2):
            j0, j1 = (0, 8) if grp == 0 else (8, 11)

            @block.tensor
            def _(te, j0=j0, j1=j1, t_qd=t_qd, prev_ptile=prev_ptile):
                ctl.waits(te, [t_qd, toks["eyeb"], bank_free[3], prev_ptile], "tensor")
                bv = banks[3].ap().bitcast(BF16)
                for j in range(j0, j1):
                    i = te.transpose(out=bv[:, (j - j0) * 128:(j - j0 + 1) * 128],
                                     in_=wdb.ap()[:, j * 128:(j + 1) * 128],
                                     identity=eyeb.ap())
                ctl.inc(i, "s_te")
            t_wdT = ctl.tok("s_te")

            @block.scalar
            def _(se, j0=j0, j1=j1, t_wdT=t_wdT, prev_ptile=prev_ptile):
                ctl.waits(se, [t_wdT, prev_ptile], "scalar")
                bv = banks[3].ap().bitcast(BF16)
                for j in range(j0, j1):
                    i = se.activation(out=ptile.ap()[:, (j - j0) * 128:(j - j0 + 1) * 128],
                                      in_=bv[:, (j - j0) * 128:(j - j0 + 1) * 128],
                                      func=AF.Copy)
                ctl.inc(i, "s_se")
            t_wdTe = ctl.tok("s_se")
            bank_free[3] = t_wdTe

            @block.sync
            def _(sync, h0=h0, j0=j0, j1=j1, t_wdTe=t_wdTe):
                ctl.wait(sync, t_wdTe)
                for j in range(j0, j1):
                    ctl.inc_dma(sync.dma_start(
                        out=cc_wd_in[h0 + j * 128:h0 + (j + 1) * 128, :],
                        in_=ptile.ap()[:, (j - j0) * 128:(j - j0 + 1) * 128]), "s_ld")
            prev_ptile = ctl.tok("s_ld")
        prev_half_store = ctl.tok("s_ld")

    @block.sync
    def _(sync):
        ins = sync.dma_start(out=cc_sd_in, in_=scratch.ap()[0:128, 3:4])
        ctl.inc_dma(ins, "s_ld")
    toks["wq_stores"] = ctl.tok("s_ld")

    @block.gpsimd
    def _(g):
        ctl.wait(g, toks["wq_stores"])
        toks["ag_wg"] = ctl.inc(g.collective_compute(
            "AllGather", ALU.bypass, replica_groups=[list(range(8))],
            ins=[cc_wg_in], outs=[cc_wg_out]), "s_cc")
        toks["ag_wu"] = ctl.inc(g.collective_compute(
            "AllGather", ALU.bypass, replica_groups=[list(range(8))],
            ins=[cc_wu_in], outs=[cc_wu_out]), "s_cc")
        toks["ag_wd"] = ctl.inc(g.collective_compute(
            "AllGather", ALU.bypass, replica_groups=[list(range(8))],
            ins=[cc_wd_in], outs=[cc_wd_out]), "s_cc")
        toks["ag_sg"] = ctl.inc(g.collective_compute(
            "AllGather", ALU.bypass, replica_groups=[list(range(8))],
            ins=[cc_sg_in], outs=[cc_sg_out]), "s_cc")
        toks["ag_su"] = ctl.inc(g.collective_compute(
            "AllGather", ALU.bypass, replica_groups=[list(range(8))],
            ins=[cc_su_in], outs=[cc_su_out]), "s_cc")
        toks["ag_sd"] = ctl.inc(g.collective_compute(
            "AllGather", ALU.bypass, replica_groups=[list(range(8))],
            ins=[cc_sd_in], outs=[cc_sd_out]), "s_cc")

    @block.vector
    def _(v):
        ctl.wait(v, toks["wq_stores"], "vector")
        v.memset(vaug.ap()[:, :, :, 16:17], 1.0)
        i = v.memset(phiaug.ap()[:, :, :, 272:273], 1.0)
        ctl.inc(i, "s_ve")
    toks["arena2"] = ctl.tok("s_ve")

    @block.gpsimd
    def _(g):
        ctl.waits(g, [toks["ag_mod"], toks["memset"]])
        toks["ca_row"] = ctl.inc_dma(g.dma_start(
            out=mrow.ap()[0:1, :], in_=cc_mod_out[None, 1024:2048]), "s_gd")
        toks["ga_row"] = ctl.inc_dma(g.dma_start(
            out=garow.ap()[0:1, :], in_=cc_mod_out[None, 2048:3072]), "s_gd")

    @block.scalar
    def _(se):
        ctl.wait(se, toks["ga_row"], "scalar")
        se.activation(out=scratch.ap()[0:1, 0:D], in_=garow.ap()[0:1, :], func=AF.Sign)
        i = se.activation(out=garow.ap()[0:1, :], in_=garow.ap()[0:1, :], func=AF.Abs)
        ctl.inc(i, "s_se")
    t_cl = ctl.tok("s_se")

    @block.vector
    def _(v):
        ctl.wait(v, t_cl, "vector")
        v.tensor_scalar(out=garow.ap()[0:1, :], in0=garow.ap()[0:1, :],
                        scalar1=0.25, scalar2=None, op0=ALU.max)
        vsp(v)
        i = v.tensor_tensor(out=garow.ap()[0:1, :], in0=garow.ap()[0:1, :],
                            in1=scratch.ap()[0:1, 0:D], op=ALU.mult)
        ctl.inc(i, "s_ve")
    toks["clamp"] = ctl.tok("s_ve")

    for half in range(2):
        sl = slice(half * 512, (half + 1) * 512)

        @block.tensor
        def _(te, sl=sl):
            ctl.waits(te, [toks["clamp"], toks["ca_row"], toks["onesf"],
                           bank_free[0], bank_free[2]], "tensor")
            for nh in range(2):
                ns = slice(nh * 256, (nh + 1) * 256)
                nsl = slice(sl.start + nh * 256, sl.start + (nh + 1) * 256)
                te.matmul(out=banks[0].ap()[:, ns], lhsT=onesf.ap(),
                          rhs=mrow.ap()[0:8, nsl], start=True, stop=True)
                i = te.matmul(out=banks[2].ap()[32:48, ns], lhsT=onesf.ap()[0:8, 0:16],
                              rhs=garow.ap()[0:8, nsl], start=True, stop=True)
            ctl.inc(i, "s_te")
        t_rep_mm = ctl.tok("s_te")

        @block.scalar
        def _(se, sl=sl, t_rep_mm=t_rep_mm):
            ctl.wait(se, t_rep_mm, "scalar")
            i = se.activation(out=modrepA.ap()[:, sl], in_=banks[0].ap()[:, 0:512],
                              func=AF.Copy, bias=1.0)
            ctl.inc(i, "s_se")
        t_repe = ctl.tok("s_se")

        @block.vector
        def _(v, sl=sl, t_rep_mm=t_rep_mm):
            ctl.waits(v, [t_rep_mm, toks["woutb"]], "vector")
            for hh in range(HEADS):
                i = v.tensor_tensor(out=woutb.ap()[32:48, hh, sl],
                                    in0=woutb.ap()[32:48, hh, sl],
                                    in1=banks[2].ap()[32:48, 0:512], op=ALU.mult)
            ctl.inc(i, "s_ve")
        bank_free[0] = t_repe
        bank_free[2] = ctl.tok("s_ve")

    @block.gpsimd
    def _(g):
        ctl.wait(g, ctl.tok("s_te"))
        toks["sa_row"] = ctl.inc_dma(g.dma_start(
            out=mrow.ap()[0:1, :], in_=cc_mod_out[None, 0:1024]), "s_gd")

    for half in range(2):
        sl = slice(half * 512, (half + 1) * 512)

        @block.tensor
        def _(te, sl=sl):
            ctl.waits(te, [toks["sa_row"], bank_free[1]], "tensor")
            for nh in range(2):
                i = te.matmul(out=banks[1].ap()[:, nh * 256:(nh + 1) * 256], lhsT=onesf.ap(),
                              rhs=mrow.ap()[0:8, sl.start + nh * 256:sl.start + (nh + 1) * 256],
                              start=True, stop=True)
            ctl.inc(i, "s_te")
        t_rep_mm = ctl.tok("s_te")

        @block.scalar
        def _(se, sl=sl, t_rep_mm=t_rep_mm):
            ctl.wait(se, t_rep_mm, "scalar")
            i = se.activation(out=modrepB.ap()[:, sl], in_=banks[1].ap()[:, 0:512], func=AF.Copy)
            ctl.inc(i, "s_se")
        bank_free[1] = ctl.tok("s_se")
    toks["modrep"] = ctl.tok("s_se")

    st_banks = (5, 6, 7)
    prev_state_add = None
    prev_tokuse2 = [None, None]
    prev_hT_evac = [None, None]
    prev_projevac = None
    prev_tokuse = None
    prev_hfeat_use = [None, None]

    for tt in range(TT):
        ch0 = tt * 4

        @block.scalar
        def _(se, ch0=ch0):
            ctl.waits(se, [toks["x"], toks["modrep"], toks["wq_stores"]], "scalar")
            for c in range(ch0, ch0 + 4):
                se.activation(out=scratch.ap(), in_=xres.ap()[:, c, :], func=AF.Square,
                              accum_out=statssq.ap()[:, c:c + 1])
            for c in range(ch0, ch0 + 4):
                i = se.activation(out=scratch.ap(), in_=xres.ap()[:, c, :], func=AF.Copy,
                                  accum_out=statsum.ap()[:, c:c + 1])
            ctl.inc(i, "s_se")
        t_stats = ctl.tok("s_se")

        @block.vector
        def _(v, ch0=ch0, t_stats=t_stats):
            ctl.wait(v, t_stats, "vector")
            sl = slice(ch0, ch0 + 4)
            v.tensor_scalar(out=statmu.ap()[:, sl], in0=statsum.ap()[:, sl],
                            scalar1=1.0 / D, scalar2=None, op0=ALU.mult)
            vsp(v)
            for cc_ in range(ch0, ch0 + 4):
                v.tensor_scalar(out=statB.ap()[:, cc_:cc_ + 1],
                                in0=statmu.ap()[:, cc_:cc_ + 1],
                                scalar1=statmu.ap()[:, cc_:cc_ + 1], scalar2=None,
                                op0=ALU.mult)
            vsp(v)
            v.scalar_tensor_tensor(out=statB.ap()[:, sl], in0=statssq.ap()[:, sl],
                                   scalar=1.0 / D, in1=statB.ap()[:, sl],
                                   op0=ALU.mult, op1=ALU.subtract)
            vsp(v)
            v.tensor_scalar(out=statB.ap()[:, sl], in0=statB.ap()[:, sl],
                            scalar1=1e-5, scalar2=None, op0=ALU.add)
            vsp(v)
            i = v.reciprocal(out=statrstd.ap()[:, sl], in_=statB.ap()[:, sl])
            ctl.inc(i, "s_ve")
        t_chain = ctl.tok("s_ve")

        @block.scalar
        def _(se, ch0=ch0, t_chain=t_chain):
            ctl.wait(se, t_chain, "scalar")
            sl = slice(ch0, ch0 + 4)
            i = se.activation(out=statrstd.ap()[:, sl], in_=statrstd.ap()[:, sl], func=AF.Sqrt)
            ctl.inc(i, "s_se")
        t_rstd = ctl.tok("s_se")

        @block.vector
        def _(v, ch0=ch0, t_rstd=t_rstd):
            ctl.wait(v, t_rstd, "vector")
            sl = slice(ch0, ch0 + 4)
            i = v.scalar_tensor_tensor(out=statB.ap()[:, sl], in0=statmu.ap()[:, sl],
                                       scalar=-1.0, in1=statrstd.ap()[:, sl],
                                       op0=ALU.mult, op1=ALU.mult)
            ctl.inc(i, "s_ve")
        t_B = ctl.tok("s_ve")

        for ci in range(4):
            c = ch0 + ci
            hb = ci % 2

            @block.vector
            def _(v, c=c, hb=hb, t_B=t_B):
                ctl.waits(v, [t_B, prev_hT_evac[hb], toks["wq_stores"]], "vector")
                hv = hstage.ap()[:, hb, :]
                vsp(v)
                v.tensor_scalar(out=hv, in0=xres.ap()[:, c, :],
                                scalar1=statrstd.ap()[:, c:c + 1],
                                scalar2=statB.ap()[:, c:c + 1],
                                op0=ALU.mult, op1=ALU.add)
                vsp(v)
                v.tensor_tensor(out=hv, in0=hv, in1=modrepA.ap(), op=ALU.mult)
                vsp(v)
                i = v.tensor_tensor(out=hv, in0=hv, in1=modrepB.ap(), op=ALU.add)
                ctl.inc(i, "s_ve")
            t_h = ctl.tok("s_ve")

            @block.scalar
            def _(se, c=c, hb=hb, t_h=t_h):
                ctl.wait(se, t_h, "scalar")
                i = se.activation(out=scratch.ap(), in_=hstage.ap()[:, hb, :],
                                  func=AF.Square, accum_out=statr.ap()[:, c:c + 1])
                ctl.inc(i, "s_se")
            t_hsq = ctl.tok("s_se")

            @block.tensor
            def _(te, c=c, hb=hb, t_h=t_h):
                ctl.waits(te, [t_h, bank_free[0], bank_free[1]], "tensor")
                for dch in range(DCH):
                    bk = 0 if dch < 4 else 1
                    i = te.transpose(out=banks[bk].ap()[:, (dch % 4) * 128:(dch % 4 + 1) * 128],
                                     in_=hstage.ap()[:, hb, dch * 128:(dch + 1) * 128],
                                     identity=eyef.ap())
                ctl.inc(i, "s_te")
            t_hT = ctl.tok("s_te")

            @block.scalar
            def _(se, c=c, ci=ci, tt=tt, t_hT=t_hT):
                ctl.waits(se, [t_hT, prev_hfeat_use[tt % 2]], "scalar")
                for dch in range(DCH):
                    bk = 0 if dch < 4 else 1
                    i = se.activation(out=hfeat.ap()[:, tt % 2, dch, ci * 128:(ci + 1) * 128],
                                      in_=banks[bk].ap()[:, (dch % 4) * 128:(dch % 4 + 1) * 128],
                                      func=AF.Copy)
                ctl.inc(i, "s_se")
            prev_hT_evac[hb] = ctl.tok("s_se")
            bank_free[0] = ctl.tok("s_se")
            bank_free[1] = ctl.tok("s_se")
        t_hfeat = ctl.tok("s_se")

        @block.scalar
        def _(se, ch0=ch0, t_hsq=t_hsq):
            ctl.wait(se, t_hsq, "scalar")
            sl = slice(ch0, ch0 + 4)
            i = se.activation(out=statr.ap()[:, sl], in_=statr.ap()[:, sl], func=AF.Sqrt)
            ctl.inc(i, "s_se")
        t_nn = ctl.tok("s_se")

        @block.vector
        def _(v, ch0=ch0, t_nn=t_nn):
            ctl.wait(v, t_nn, "vector")
            sl = slice(ch0, ch0 + 4)
            v.tensor_scalar(out=statB.ap()[:, sl], in0=statr.ap()[:, sl],
                            scalar1=1e-12, scalar2=1.0 / 32.0,
                            op0=ALU.max, op1=ALU.mult)
            vsp(v)
            i = v.reciprocal(out=statr.ap()[:, sl], in_=statB.ap()[:, sl])
            ctl.inc(i, "s_ve")
        toks[("rt", tt)] = ctl.tok("s_ve")

        @block.tensor
        def _(te, tt=tt, t_hfeat=t_hfeat):
            ctl.waits(te, [t_hfeat, toks["wproj"], prev_projevac,
                           bank_free[2], bank_free[3], bank_free[4]], "tensor")
            for (bk, w_t, c0, c1) in ((2, wqb, 0, 128), (3, wkvb, 0, 128), (4, wkvb, 128, 256)):
                for ch in range(DCH):
                    i = te.matmul(out=banks[bk].ap()[:, 0:512],
                                  lhsT=w_t.ap()[:, ch, c0:c1],
                                  rhs=hfeat.ap()[:, tt % 2, ch, :],
                                  start=(ch == 0), stop=(ch == DCH - 1))
            ctl.inc(i, "s_te")
        t_proj = ctl.tok("s_te")
        prev_hfeat_use[tt % 2] = t_proj

        @block.scalar
        def _(se, tt=tt, t_proj=t_proj):
            ctl.waits(se, [t_proj, prev_tokuse], "scalar")
            se.activation(out=qfeatr.ap()[:, tt % 2, :], in_=banks[2].ap()[:, 0:512], func=AF.Copy)
            se.activation(out=kfeat.ap()[:, tt % 2, :], in_=banks[3].ap()[:, 0:512], func=AF.Copy)
            i = se.activation(out=vfeat.ap()[:, tt % 2, :], in_=banks[4].ap()[:, 0:512],
                              func=AF.Copy)
            ctl.inc(i, "s_se")
        t_pe = ctl.tok("s_se")
        prev_projevac = t_pe
        bank_free[2] = t_pe
        bank_free[3] = t_pe
        bank_free[4] = t_pe

        for ci in range(4):
            c = ch0 + ci
            cb2 = c % 2

            @block.tensor
            def _(te, tt=tt, ci=ci, cb2=cb2, t_pe=t_pe):
                ctl.waits(te, [t_pe, toks["eyeb"], bank_free[cb2], prev_tokuse2[cb2]],
                          "tensor")
                kp = banks[cb2].ap().bitcast(BF16)
                te.transpose(out=kp[:, 0:128], in_=kfeat.ap()[:, tt % 2, ci * 128:(ci + 1) * 128],
                             identity=eyeb.ap())
                te.transpose(out=kp[:, 128:256], in_=vfeat.ap()[:, tt % 2, ci * 128:(ci + 1) * 128],
                             identity=eyeb.ap())
                i = te.transpose(out=kp[:, 256:384],
                                 in_=qfeatr.ap()[:, tt % 2, ci * 128:(ci + 1) * 128],
                                 identity=eyeb.ap())
                ctl.inc(i, "s_te")
            t_tokT = ctl.tok("s_te")

            @block.scalar
            def _(se, c=c, tt=tt, cb2=cb2, t_tokT=t_tokT):
                ctl.waits(se, [t_tokT, toks[("rt", tt)], toks["arena2"],
                               prev_tokuse2[cb2]], "scalar")
                kp = banks[cb2].ap().bitcast(BF16)
                rr = statr.ap()[:, c:c + 1]
                se.activation(out=ktok.ap()[:, cb2, :], in_=kp[:, 0:128], func=AF.Copy, scale=rr)
                se.activation(out=vaug.ap()[:, cb2, :, 0:16], in_=kp[:, 128:256],
                              func=AF.Copy, scale=rr)
                se.activation(out=phiaug.ap()[:, cb2, :, 0:16], in_=kp[:, 0:128],
                              func=AF.Copy, scale=rr)
                i = se.activation(out=qtok.ap()[:, c, :], in_=kp[:, 256:384], func=AF.Copy, scale=rr)
                ctl.inc(i, "s_se")
            t_tok = ctl.tok("s_se")
            bank_free[cb2] = t_tok

            @block.vector
            def _(v, cb2=cb2, t_tok=t_tok):
                ctl.wait(v, t_tok, "vector")
                for hh in range(HEADS):
                    ks = ktok.ap()[:, cb2, hh * 16:(hh + 1) * 16]
                    i = v.tensor_tensor(
                        out=phiaug.ap()[:, cb2, hh, 16:272].rearrange("p (i j) -> p i j", i=16),
                        in0=ks.unsqueeze(2).broadcast_to([128, 16, 16]),
                        in1=ks.unsqueeze(1).broadcast_to([128, 16, 16]), op=ALU.mult)
                ctl.inc(i, "s_ve")
            t_phi = ctl.tok("s_ve")

            @block.tensor
            def _(te, cb2=cb2, t_phi=t_phi):
                ctl.waits(te, [t_phi, prev_state_add, toks["memset"]], "tensor")
                for hh in range(HEADS):
                    bk = st_banks[hh // 3]
                    po = 32 * (hh % 3)
                    i = te.matmul(out=banks[bk].ap()[po:po + 17, 0:273],
                                  lhsT=vaug.ap()[:, cb2, hh, :],
                                  rhs=phiaug.ap()[:, cb2, hh, :], start=True, stop=True)
                ctl.inc(i, "s_te")
            t_smm = ctl.tok("s_te")
            prev_tokuse = t_smm
            prev_tokuse2[cb2] = t_smm

            @block.vector
            def _(v, t_smm=t_smm):
                ctl.wait(v, t_smm, "vector")
                for k in range(3):
                    i = v.tensor_tensor(out=stacc.ap()[:, k * 273:(k + 1) * 273],
                                        in0=stacc.ap()[:, k * 273:(k + 1) * 273],
                                        in1=banks[st_banks[k]].ap()[0:113, 0:273], op=ALU.add)
                ctl.inc(i, "s_ve")
            prev_state_add = ctl.tok("s_ve")

        if DEBUG_TAP == "ph2" and tt == 0:
            @block.gpsimd
            def _(g, prev_state_add=prev_state_add):
                ctl.wait(g, prev_state_add)
                ctl.inc_dma(g.dma_start(out=tb_tap[:, 0:1024], in_=hstage.ap()[:, 1, :]), "s_gd")
                ctl.inc_dma(g.dma_start(out=tb_tap[:, 1024:1152], in_=ktok.ap()[:, 1, :]), "s_gd")
                ctl.inc_dma(g.dma_start(out=tb_tap[:, 1152:1425], in_=phiaug.ap()[:, 1, 0, :]), "s_gd")
                ctl.inc_dma(g.dma_start(out=tb_tap[:, 1425:1442], in_=vaug.ap()[:, 1, 0, :]), "s_gd")
                ctl.inc_dma(g.dma_start(out=tb_tap[:, 1442:1446], in_=statr.ap()[:, 0:4]), "s_gd")
                ctl.inc_dma(g.dma_start(out=tb_tap[:, 1446:1450], in_=statrstd.ap()[:, 0:4]), "s_gd")
                ctl.inc_dma(g.dma_start(out=tb_tap[:, 1450:1454], in_=statsum.ap()[:, 0:4]), "s_gd")
                ctl.inc_dma(g.dma_start(out=tb_tap[:, 1454:1458], in_=statssq.ap()[:, 0:4]), "s_gd")
                ctl.inc_dma(g.dma_start(out=tb_tap[:, 2048:2560], in_=kfeat.ap()[:, 0, :]), "s_gd")
                ctl.inc_dma(g.dma_start(out=tb_tap[:, 2560:3072], in_=qfeatr.ap()[:, 0, :]), "s_gd")
                ctl.inc_dma(g.dma_start(out=tb_tap[:, 3072:3584], in_=hfeat.ap()[:, 0, 0, :]), "s_gd")
                ctl.inc_dma(g.dma_start(out=tb_tap[0:113, 3584:4096], in_=stacc.ap()[:, 0:512]), "s_gd")
            t_tap = ctl.tok("s_gd")
            prev_state_add = t_tap
            prev_tokuse = t_tap
            prev_hT_evac = [t_tap, t_tap]
            prev_projevac = t_tap
            prev_hfeat_use = [t_tap, t_tap]

    @block.gpsimd
    def _(g):
        ctl.wait(g, prev_state_add)
        toks["st_o"] = ctl.inc_dma(g.dma_start(out=cc_st_in, in_=stacc.ap()), "s_gd")
        ctl.wait(g, toks["st_o"])
        toks["ar_st"] = ctl.inc(g.collective_compute(
            "AllReduce", ALU.add, replica_groups=[[0, 1, 2, 3], [4, 5, 6, 7]],
            ins=[cc_st_in], outs=[cc_st_out]), "s_cc")
        ctl.wait(g, toks["ar_st"])
        toks["st_i"] = ctl.inc_dma(g.dma_start(out=stacc.ap(), in_=cc_st_out), "s_gd")

    for hh in range(HEADS):
        co = (hh // 3) * 273
        po = 32 * (hh % 3)
        bk = hh % 2
        idm = eyef.ap()[po:po + 17, po:po + 17]

        @block.tensor
        def _(te, hh=hh, co=co, po=po, bk=bk, idm=idm):
            ctl.waits(te, [toks["st_i"], bank_free[bk]], "tensor")
            src = stacc.ap()
            te.transpose(out=banks[bk].ap()[0:16, 0:17], in_=src[po:po + 17, co:co + 16],
                         identity=idm)
            te.transpose(out=banks[bk].ap()[0:128, 32:49], in_=src[po:po + 17, co + 16:co + 144],
                         identity=idm)
            te.transpose(out=banks[bk].ap()[0:128, 64:81], in_=src[po:po + 17, co + 144:co + 272],
                         identity=idm)
            i = te.transpose(out=banks[bk].ap()[0:1, 96:113], in_=src[po:po + 17, co + 272:co + 273],
                             identity=idm)
            ctl.inc(i, "s_te")
        t_sT = ctl.tok("s_te")

        @block.scalar
        def _(se, hh=hh, bk=bk, t_sT=t_sT):
            ctl.wait(se, t_sT, "scalar")
            se.activation(out=stA.ap()[:, hh, 0:1], in_=banks[bk].ap()[0:16, 16:17], func=AF.Copy)
            se.activation(out=stA.ap()[:, hh, 32:48], in_=banks[bk].ap()[0:16, 0:16], func=AF.Copy)
            se.activation(out=stB.ap()[:, hh, 0:1], in_=banks[bk].ap()[0:128, 48:49],
                          func=AF.Copy, scale=0.5)
            se.activation(out=stB.ap()[:, hh, 32:48], in_=banks[bk].ap()[0:128, 32:48],
                          func=AF.Copy, scale=0.5)
            se.activation(out=stC.ap()[:, hh, 0:1], in_=banks[bk].ap()[0:128, 80:81],
                          func=AF.Copy, scale=0.5)
            se.activation(out=stC.ap()[:, hh, 32:48], in_=banks[bk].ap()[0:128, 64:80],
                          func=AF.Copy, scale=0.5)
            se.activation(out=stD.ap()[0:1, hh, 0:1], in_=banks[bk].ap()[0:1, 112:113], func=AF.Copy)
            i = se.activation(out=stD.ap()[0:1, hh, 32:48], in_=banks[bk].ap()[0:1, 96:112],
                              func=AF.Copy)
            ctl.inc(i, "s_se")
        bank_free[bk] = ctl.tok("s_se")
    toks["stprep"] = ctl.tok("s_se")

    @block.vector
    def _(v):
        ctl.wait(v, toks["st_i"], "vector")
        i = v.memset(denr.ap(), 0.0)
        ctl.inc(i, "s_ve")

    prev_ofd_use = None
    prev_od = None
    prev_p2T = None
    for tt in range(TT):
        for hh in range(HEADS):
            tq = 2 + (hh % 2)
            ob = 4 + (hh % 2)

            @block.tensor
            def _(te, tt=tt, hh=hh):
                ctl.waits(te, [toks["stprep"], bank_free[7], prev_ofd_use], "tensor")
                for ci in range(4):
                    c = tt * 4 + ci
                    i = te.transpose(out=banks[7].ap().bitcast(BF16)[0:16, ci * 128:(ci + 1) * 128],
                                     in_=qtok.ap()[:, c, hh * 16:(hh + 1) * 16],
                                     identity=eyeb.ap())
                ctl.inc(i, "s_te")
            t_qfT = ctl.tok("s_te")

            @block.scalar
            def _(se, t_qfT=t_qfT, prev_od=prev_od):
                ctl.waits(se, [t_qfT, prev_od], "scalar")
                i = se.activation(out=qf16.ap(), in_=banks[7].ap().bitcast(BF16)[0:16, 0:512],
                                  func=AF.Copy)
                ctl.inc(i, "s_se")
            t_qf = ctl.tok("s_se")
            bank_free[7] = t_qf

            @block.vector
            def _(v, hh=hh, tt=tt, prev_p2T=prev_p2T):
                ctl.waits(v, [prev_p2T], "vector")
                for ci in range(4):
                    c = tt * 4 + ci
                    qs = qtok.ap()[:, c, hh * 16:(hh + 1) * 16]
                    i = v.tensor_tensor(
                        out=phi2q4.ap()[:, ci, :].rearrange("p (i j) -> p i j", i=16),
                        in0=qs.unsqueeze(2).broadcast_to([128, 16, 16]),
                        in1=qs.unsqueeze(1).broadcast_to([128, 16, 16]), op=ALU.mult)
                ctl.inc(i, "s_ve")
            t_p2 = ctl.tok("s_ve")

            @block.tensor
            def _(te, tq=tq, t_p2=t_p2):
                ctl.waits(te, [t_p2, bank_free[tq]], "tensor")
                bv = banks[tq].ap().bitcast(BF16)
                for ci in range(4):
                    te.transpose(out=bv[:, ci * 128:(ci + 1) * 128],
                                 in_=phi2q4.ap()[:, ci, 0:128], identity=eyeb.ap())
                    i = te.transpose(out=bv[:, 512 + ci * 128:512 + (ci + 1) * 128],
                                     in_=phi2q4.ap()[:, ci, 128:256], identity=eyeb.ap())
                ctl.inc(i, "s_te")
            prev_p2T = ctl.tok("s_te")

            @block.scalar
            def _(se, tq=tq, prev_p2T=prev_p2T, prev_od=prev_od):
                ctl.waits(se, [prev_p2T, prev_od], "scalar")
                bv = banks[tq].ap().bitcast(BF16)
                se.activation(out=phiqA.ap(), in_=bv[:, 0:512], func=AF.Copy)
                i = se.activation(out=phiqB.ap(), in_=bv[:, 512:1024], func=AF.Copy)
                ctl.inc(i, "s_se")
            bank_free[tq] = ctl.tok("s_se")
            t_phiq = ctl.tok("s_se")

            @block.tensor
            def _(te, hh=hh, tt=tt, ob=ob, t_phiq=t_phiq, t_qf=t_qf):
                ctl.waits(te, [t_phiq, t_qf, toks["onesb"], bank_free[ob]], "tensor")
                te.matmul(out=banks[ob].ap()[0:48, 0:512], lhsT=stA.ap()[:, hh, :],
                          rhs=qf16.ap(), start=True, stop=False)
                te.matmul(out=banks[ob].ap()[0:48, 0:512], lhsT=stB.ap()[:, hh, :],
                          rhs=phiqA.ap(), start=False, stop=False)
                te.matmul(out=banks[ob].ap()[0:48, 0:512], lhsT=stC.ap()[:, hh, :],
                          rhs=phiqB.ap(), start=False, stop=False)
                i = te.matmul(out=banks[ob].ap()[0:48, 0:512], lhsT=stD.ap()[:, hh, :],
                              rhs=onesb.ap(), start=False, stop=True)
                ctl.inc(i, "s_te")
            t_od = ctl.tok("s_te")
            prev_od = t_od

            @block.scalar
            def _(se, hh=hh, ob=ob, t_od=t_od):
                ctl.wait(se, t_od, "scalar")
                se.activation(out=denr.ap()[0:1, hh, :], in_=banks[ob].ap()[0:1, 0:512],
                              func=AF.Copy)
                i = se.activation(out=ofd.ap()[32:48, hh, :], in_=banks[ob].ap()[32:48, 0:512],
                                  func=AF.Copy)
                ctl.inc(i, "s_se")
            t_ode = ctl.tok("s_se")
            bank_free[ob] = t_ode
        t_odall = ctl.tok("s_se")

        @block.vector
        def _(v, t_odall=t_odall):
            ctl.wait(v, t_odall, "vector")
            v.tensor_scalar(out=denr2.ap()[0:1, :, :], in0=denr.ap()[0:1, :, :],
                            scalar1=1e-5, scalar2=None, op0=ALU.max)
            vsp(v)
            i = v.reciprocal(out=denr.ap()[0:1, :, :], in_=denr2.ap()[0:1, :, :])
            ctl.inc(i, "s_ve")
        t_den = ctl.tok("s_ve")

        for hh in range(HEADS):
            @block.tensor
            def _(te, hh=hh, t_den=t_den):
                ctl.waits(te, [t_den, bank_free[6]], "tensor")
                for nh in range(2):
                    i = te.matmul(out=banks[6].ap()[32:48, nh * 256:(nh + 1) * 256],
                                  lhsT=onesf.ap()[0:8, 0:16],
                                  rhs=denr.ap()[0:8, hh, nh * 256:(nh + 1) * 256],
                                  start=True, stop=True)
                ctl.inc(i, "s_te")
            t_qkr = ctl.tok("s_te")

            @block.vector
            def _(v, hh=hh, t_qkr=t_qkr):
                ctl.wait(v, t_qkr, "vector")
                i = v.tensor_tensor(out=ofd.ap()[32:48, hh, :], in0=ofd.ap()[32:48, hh, :],
                                    in1=banks[6].ap()[32:48, 0:512], op=ALU.mult)
                ctl.inc(i, "s_ve")
            bank_free[6] = ctl.tok("s_ve")
        t_qkall = ctl.tok("s_ve")

        for ci in range(4):
            c = tt * 4 + ci
            b0, b1 = (0, 1) if ci % 2 == 0 else (2, 3)

            @block.tensor
            def _(te, c=c, ci=ci, b0=b0, b1=b1, t_qkall=t_qkall):
                ctl.waits(te, [t_qkall, toks["woutb"], bank_free[b0], bank_free[b1]], "tensor")
                for hh in range(HEADS):
                    te.matmul(out=banks[b0].ap()[:, 0:512],
                              lhsT=ofd.ap()[32:48, hh, ci * 128:(ci + 1) * 128],
                              rhs=woutb.ap()[32:48, hh, 0:512],
                              start=(hh == 0), stop=(hh == HEADS - 1))
                for hh in range(HEADS):
                    i = te.matmul(out=banks[b1].ap()[:, 0:512],
                                  lhsT=ofd.ap()[32:48, hh, ci * 128:(ci + 1) * 128],
                                  rhs=woutb.ap()[32:48, hh, 512:1024],
                                  start=(hh == 0), stop=(hh == HEADS - 1))
                ctl.inc(i, "s_te")
            t_op = ctl.tok("s_te")

            @block.vector
            def _(v, c=c, b0=b0, b1=b1, t_op=t_op):
                ctl.wait(v, t_op, "vector")
                v.tensor_tensor(out=xres.ap()[:, c, 0:512], in0=xres.ap()[:, c, 0:512],
                                in1=banks[b0].ap()[:, 0:512], op=ALU.add)
                i = v.tensor_tensor(out=xres.ap()[:, c, 512:1024], in0=xres.ap()[:, c, 512:1024],
                                    in1=banks[b1].ap()[:, 0:512], op=ALU.add)
                ctl.inc(i, "s_ve")
            bank_free[b0] = ctl.tok("s_ve")
            bank_free[b1] = ctl.tok("s_ve")
        prev_ofd_use = ctl.tok("s_te")
    toks["resid1"] = ctl.tok("s_ve")

    @block.gpsimd
    def _(g):
        toks["cb_row"] = ctl.inc_dma(g.dma_start(
            out=mrow.ap()[0:1, :], in_=cc_mod_out[None, 4096:5120]), "s_gd")

    for half in range(2):
        sl = slice(half * 512, (half + 1) * 512)
        ba = 4 if half == 0 else 5

        @block.tensor
        def _(te, sl=sl, ba=ba):
            ctl.waits(te, [toks["cb_row"], toks["resid1"], bank_free[ba]], "tensor")
            for nh in range(2):
                i = te.matmul(out=banks[ba].ap()[:, nh * 256:(nh + 1) * 256], lhsT=onesf.ap(),
                              rhs=mrow.ap()[0:8, sl.start + nh * 256:sl.start + (nh + 1) * 256],
                              start=True, stop=True)
            ctl.inc(i, "s_te")
        t_rep2 = ctl.tok("s_te")

        @block.scalar
        def _(se, sl=sl, ba=ba, t_rep2=t_rep2):
            ctl.wait(se, t_rep2, "scalar")
            i = se.activation(out=modrepA.ap()[:, sl], in_=banks[ba].ap()[:, 0:512],
                              func=AF.Copy, bias=1.0)
            ctl.inc(i, "s_se")
        bank_free[ba] = ctl.tok("s_se")
    t_cb_mm = ctl.tok("s_te")

    @block.gpsimd
    def _(g):
        ctl.wait(g, t_cb_mm)
        toks["sb_row"] = ctl.inc_dma(g.dma_start(
            out=mrow.ap()[0:1, :], in_=cc_mod_out[None, 3072:4096]), "s_gd")

    for half in range(2):
        sl = slice(half * 512, (half + 1) * 512)
        ba = 6 if half == 0 else 7

        @block.tensor
        def _(te, sl=sl, ba=ba):
            ctl.waits(te, [toks["sb_row"], bank_free[ba]], "tensor")
            for nh in range(2):
                i = te.matmul(out=banks[ba].ap()[:, nh * 256:(nh + 1) * 256], lhsT=onesf.ap(),
                              rhs=mrow.ap()[0:8, sl.start + nh * 256:sl.start + (nh + 1) * 256],
                              start=True, stop=True)
            ctl.inc(i, "s_te")
        t_rep2 = ctl.tok("s_te")

        @block.scalar
        def _(se, sl=sl, ba=ba, t_rep2=t_rep2):
            ctl.wait(se, t_rep2, "scalar")
            i = se.activation(out=modrepB.ap()[:, sl], in_=banks[ba].ap()[:, 0:512], func=AF.Copy)
            ctl.inc(i, "s_se")
        bank_free[ba] = ctl.tok("s_se")
    toks["modrep2"] = ctl.tok("s_se")

    prev_spill = [None, None]
    for c in range(TCH):
        @block.scalar
        def _(se, c=c):
            ctl.waits(se, [toks["resid1"], toks["modrep2"]], "scalar")
            se.activation(out=scratch.ap(), in_=xres.ap()[:, c, :], func=AF.Square,
                          accum_out=statssq.ap()[:, c:c + 1])
            i = se.activation(out=scratch.ap(), in_=xres.ap()[:, c, :], func=AF.Copy,
                              accum_out=statsum.ap()[:, c:c + 1])
            ctl.inc(i, "s_se")
        t_s2 = ctl.tok("s_se")

        @block.vector
        def _(v, c=c, t_s2=t_s2):
            ctl.wait(v, t_s2, "vector")
            sl = slice(c, c + 1)
            v.tensor_scalar(out=statmu.ap()[:, sl], in0=statsum.ap()[:, sl],
                            scalar1=1.0 / D, scalar2=None, op0=ALU.mult)
            vsp(v)
            v.tensor_scalar(out=statB.ap()[:, sl], in0=statmu.ap()[:, sl],
                            scalar1=statmu.ap()[:, sl], scalar2=None, op0=ALU.mult)
            vsp(v)
            v.scalar_tensor_tensor(out=statB.ap()[:, sl], in0=statssq.ap()[:, sl],
                                   scalar=1.0 / D, in1=statB.ap()[:, sl],
                                   op0=ALU.mult, op1=ALU.subtract)
            vsp(v)
            v.tensor_scalar(out=statB.ap()[:, sl], in0=statB.ap()[:, sl],
                            scalar1=1e-5, scalar2=None, op0=ALU.add)
            vsp(v)
            i = v.reciprocal(out=statrstd.ap()[:, sl], in_=statB.ap()[:, sl])
            ctl.inc(i, "s_ve")
        t_c2 = ctl.tok("s_ve")

        @block.scalar
        def _(se, c=c, t_c2=t_c2):
            ctl.wait(se, t_c2, "scalar")
            i = se.activation(out=statrstd.ap()[:, c:c + 1], in_=statrstd.ap()[:, c:c + 1],
                              func=AF.Sqrt)
            ctl.inc(i, "s_se")
        t_r2 = ctl.tok("s_se")

        @block.vector
        def _(v, c=c, t_r2=t_r2):
            ctl.wait(v, t_r2, "vector")
            sl = slice(c, c + 1)
            i = v.scalar_tensor_tensor(out=statB.ap()[:, sl], in0=statmu.ap()[:, sl],
                                       scalar=-1.0, in1=statrstd.ap()[:, sl],
                                       op0=ALU.mult, op1=ALU.mult)
            ctl.inc(i, "s_ve")
        t_B2 = ctl.tok("s_ve")

        if True:
            @block.vector
            def _(v, c=c, t_B2=t_B2):
                ctl.waits(v, [t_B2, prev_spill[0]], "vector")
                hv = hq.ap()
                vsp(v)
                v.tensor_scalar(out=hv, in0=xres.ap()[:, c, :],
                                scalar1=statrstd.ap()[:, c:c + 1],
                                scalar2=statB.ap()[:, c:c + 1],
                                op0=ALU.mult, op1=ALU.add)
                vsp(v)
                v.tensor_tensor(out=hv, in0=hv, in1=modrepA.ap(), op=ALU.mult)
                vsp(v)
                v.tensor_tensor(out=hv, in0=hv, in1=modrepB.ap(), op=ALU.add)
                vsp(v)
                v.tensor_scalar(out=scratch.ap(), in0=hv, scalar1=0.0, scalar2=None,
                                op0=ALU.bypass, op1=ALU.max, accum_out=maxs.ap()[:, c:c + 1])
                i = v.tensor_scalar(out=scratch.ap(), in0=hv, scalar1=0.0, scalar2=None,
                                    op0=ALU.bypass, op1=ALU.min, accum_out=mins.ap()[:, c:c + 1])
                ctl.inc(i, "s_ve")
            t_h2 = ctl.tok("s_ve")

            @block.sync
            def _(sync, c=c, t_h2=t_h2):
                ctl.wait(sync, t_h2)
                toks[("spill", c)] = ctl.inc_dma(sync.dma_start(
                    out=h_spill[c, :, :], in_=hq.ap()), "s_ld")
            prev_spill[0] = toks[("spill", c)]
    t_ln2 = ctl.tok("s_ve")

    @block.vector
    def _(v):
        ctl.wait(v, t_ln2)
        vsp(v, 4)
        v.tensor_reduce(out=hmaxcol.ap()[:, 0:1], in_=maxs.ap(), axis=AX.X, op=ALU.max)
        i = v.tensor_reduce(out=hmaxcol.ap()[:, 1:2], in_=mins.ap(), axis=AX.X, op=ALU.min)
        ctl.inc(i, "s_ve")
    t_gmr = ctl.tok("s_ve")

    @block.scalar
    def _(se):
        ctl.wait(se, t_gmr, "scalar")
        i = se.activation(out=hmaxcol.ap()[:, 1:2], in_=hmaxcol.ap()[:, 1:2],
                          func=AF.Copy, scale=-1.0)
        ctl.inc(i, "s_se")
    t_gmse = ctl.tok("s_se")

    @block.vector
    def _(v):
        ctl.wait(v, t_gmse, "vector")
        i = v.tensor_tensor(out=hmaxcol.ap()[:, 0:1], in0=hmaxcol.ap()[:, 0:1],
                            in1=hmaxcol.ap()[:, 1:2], op=ALU.max)
        ctl.inc(i, "s_ve")
    t_gm = ctl.tok("s_ve")

    @block.tensor
    def _(te):
        ctl.waits(te, [t_gm, bank_free[6]], "tensor")
        i = te.transpose(out=banks[6].ap()[0:1, 0:128], in_=hmaxcol.ap()[:, 0:1],
                         identity=eyef.ap())
        ctl.inc(i, "s_te")
    t_gmT = ctl.tok("s_te")

    @block.vector
    def _(v):
        ctl.wait(v, t_gmT, "vector")
        i = v.tensor_reduce(out=cells.ap()[0:1, 4:5], in_=banks[6].ap()[0:1, 0:128],
                            axis=AX.X, op=ALU.max)
        ctl.inc(i, "s_ve")
    t_gmax = ctl.tok("s_ve")
    bank_free[6] = t_gmax

    @block.gpsimd
    def _(g):
        ctl.wait(g, t_gmax)
        toks["m1o"] = ctl.inc_dma(g.dma_start(out=cc_m1_in[None, :], in_=cells.ap()[0:1, 4:5]), "s_gd")
        ctl.wait(g, toks["m1o"])
        toks["ar_m1"] = ctl.inc(g.collective_compute(
            "AllReduce", ALU.max, replica_groups=[list(range(8))],
            ins=[cc_m1_in], outs=[cc_m1_out]), "s_cc")
        ctl.wait(g, toks["ar_m1"])
        toks["m1i"] = ctl.inc_dma(g.dma_start(out=cells.ap()[0:1, 5:6], in_=cc_m1_out[None, :]), "s_gd")

    @block.vector
    def _(v):
        ctl.wait(v, toks["m1i"], "vector")
        v.scalar_tensor_tensor(out=cells.ap()[0:1, 6:7], in0=cells.ap()[0:1, 2:3],
                               scalar=1.0 / 127.0, in1=cells.ap()[0:1, 5:6],
                               op0=ALU.mult, op1=ALU.mult)
        vsp(v)
        i = v.reciprocal(out=cells.ap()[0:1, 7:8], in_=cells.ap()[0:1, 6:7])
        ctl.inc(i, "s_ve")
    t_s1c = ctl.tok("s_ve")

    @block.tensor
    def _(te):
        ctl.waits(te, [t_s1c, bank_free[7]], "tensor")
        te.matmul(out=banks[7].ap()[:, 0:1], lhsT=onesf.ap(), rhs=cells.ap()[0:8, 7:8],
                  start=True, stop=True)
        i = te.matmul(out=banks[7].ap()[:, 1:2], lhsT=onesf.ap(), rhs=cells.ap()[0:8, 6:7],
                      start=True, stop=True)
        ctl.inc(i, "s_te")
    t_bc1 = ctl.tok("s_te")

    @block.scalar
    def _(se):
        ctl.wait(se, t_bc1, "scalar")
        i = se.activation(out=gcol.ap()[:, 0:2], in_=banks[7].ap()[:, 0:2], func=AF.Copy)
        ctl.inc(i, "s_se")
    toks["rs1"] = ctl.tok("s_se")
    bank_free[7] = toks["rs1"]

    @block.gpsimd
    def _(g):
        ctl.wait(g, toks["ag_sg"])
        toks["sgrows"] = ctl.inc_dma(g.dma_start(
            out=scrows.ap(), in_=cc_sg_out.rearrange("(a b) -> a b", a=22)), "s_gd")

    @block.tensor
    def _(te):
        ctl.waits(te, [toks["sgrows"], bank_free[6]], "tensor")
        i = te.transpose(out=banks[6].ap()[:, 0:22], in_=scrows.ap(),
                         identity=eyef.ap()[0:22, 0:22])
        ctl.inc(i, "s_te")
    t_sgT = ctl.tok("s_te")

    @block.scalar
    def _(se):
        ctl.wait(se, t_sgT, "scalar")
        i = se.activation(out=sgcol.ap(), in_=banks[6].ap()[:, 0:22], func=AF.Copy)
        ctl.inc(i, "s_se")
    t_sgc = ctl.tok("s_se")
    bank_free[6] = t_sgc

    @block.gpsimd
    def _(g):
        ctl.waits(g, [toks["ag_su"], t_sgc])
        toks["surows"] = ctl.inc_dma(g.dma_start(
            out=scrows.ap(), in_=cc_su_out.rearrange("(a b) -> a b", a=22)), "s_gd")

    @block.tensor
    def _(te):
        ctl.waits(te, [toks["surows"], bank_free[6]], "tensor")
        i = te.transpose(out=banks[6].ap()[:, 0:22], in_=scrows.ap(),
                         identity=eyef.ap()[0:22, 0:22])
        ctl.inc(i, "s_te")
    t_suT = ctl.tok("s_te")

    @block.scalar
    def _(se):
        ctl.wait(se, t_suT, "scalar")
        i = se.activation(out=sucol.ap(), in_=banks[6].ap()[:, 0:22], func=AF.Copy)
        ctl.inc(i, "s_se")
    t_suc = ctl.tok("s_se")
    bank_free[6] = t_suc

    @block.gpsimd
    def _(g):
        ctl.waits(g, [toks["ag_sd"], t_suc])
        toks["sdrow"] = ctl.inc_dma(g.dma_start(out=sdrow.ap()[0:1, :], in_=cc_sd_out[None, :]), "s_gd")

    @block.vector
    def _(v):
        ctl.waits(v, [t_sgc, t_suc, toks["rs1"]], "vector")
        v.tensor_scalar(out=sgxcol.ap(), in0=sgcol.ap(), scalar1=gcol.ap()[:, 1:2],
                        scalar2=None, op0=ALU.mult)
        i = v.tensor_scalar(out=suxcol.ap(), in0=sucol.ap(), scalar1=gcol.ap()[:, 1:2],
                            scalar2=None, op0=ALU.mult)
        ctl.inc(i, "s_ve")
    toks["sgx"] = ctl.tok("s_ve")

    final_spill = toks[("spill", TCH - 1)]
    prev_stt = None
    prev_xT = None
    for c in range(TCH):
        @block.sync
        def _(sync, c=c, prev_stt=prev_stt):
            ctl.waits(sync, [toks[("spill", c)], final_spill, prev_stt])
            toks[("rld", c)] = ctl.inc_dma(sync.dma_start(
                out=hq.ap(), in_=h_spill[c, :, :]), "s_ld")

        @block.scalar
        def _(se, c=c):
            ctl.waits(se, [toks[("rld", c)], toks["rs1"]], "scalar")
            i = se.activation(out=xqf.ap(), in_=hq.ap(), func=AF.Copy,
                              scale=gcol.ap()[:, 0:1])
            ctl.inc(i, "s_se")
        t_uq = ctl.tok("s_se")

        if QUANT_MODE == "cast":
            @block.vector
            def _(v, t_uq=t_uq, prev_xT=prev_xT):
                ctl.waits(v, [t_uq, prev_xT], "vector")
                v.tensor_copy(out=xq8.ap(), in_=xqf.ap())
                vsp(v)
                v.tensor_copy(out=nxq.ap(), in_=xq8.ap())
                i = vflush(v)
                ctl.inc(i, "s_ve")
        else:
            @block.vector
            def _(v, t_uq=t_uq, prev_xT=prev_xT):
                ctl.waits(v, [t_uq, prev_xT], "vector")
                v.tensor_copy(out=xq16.ap(), in_=xqf.ap())
                vsp(v)
                v.tensor_copy(out=hq.ap(), in_=xq16.ap())
                vsp(v)
                v.tensor_tensor(out=scratch.ap(), in0=xqf.ap(), in1=hq.ap(),
                                op=ALU.subtract)
                vsp(v)
                v.tensor_scalar(out=xqf.ap(), in0=scratch.ap(), scalar1=0.5, scalar2=None,
                                op0=ALU.is_ge)
                v.tensor_scalar(out=scratch.ap(), in0=scratch.ap(), scalar1=-0.5,
                                scalar2=-1.0, op0=ALU.is_ge, op1=ALU.add)
                vsp(v)
                v.tensor_tensor(out=xqf.ap(), in0=xqf.ap(), in1=scratch.ap(), op=ALU.add)
                vsp(v)
                v.tensor_tensor(out=hq.ap(), in0=hq.ap(), in1=xqf.ap(), op=ALU.add)
                vsp(v)
                v.tensor_scalar(out=nxq.ap(), in0=hq.ap(), scalar1=-128.0, scalar2=127.0,
                                op0=ALU.max, op1=ALU.min)
                i = vflush(v)
                ctl.inc(i, "s_ve")
        t_nx = ctl.tok("s_ve")
        prev_stt = t_nx

        @block.tensor
        def _(te, c=c, t_nx=t_nx):
            ctl.waits(te, [t_nx, bank_free[0], bank_free[1]], "tensor")
            for dch in range(DCH):
                bk = 0 if dch < 4 else 1
                bv = banks[bk].ap().bitcast(BF16)
                i = te.transpose(out=bv[:, (dch % 4) * 128:(dch % 4 + 1) * 128],
                                 in_=nxq.ap()[:, dch * 128:(dch + 1) * 128], identity=eyeb.ap())
            ctl.inc(i, "s_te")
        t_xT = ctl.tok("s_te")
        prev_xT = t_xT

        @block.scalar
        def _(se, c=c, t_xT=t_xT):
            ctl.wait(se, t_xT, "scalar")
            for dch in range(DCH):
                bk = 0 if dch < 4 else 1
                bv = banks[bk].ap().bitcast(BF16)
                i = se.activation(out=xqfeat.ap()[:, dch, c * 128:(c + 1) * 128],
                                  in_=bv[:, (dch % 4) * 128:(dch % 4 + 1) * 128], func=AF.Copy)
            ctl.inc(i, "s_se")
        bank_free[0] = ctl.tok("s_se")
        bank_free[1] = ctl.tok("s_se")
    toks["xqfeat"] = ctl.tok("s_se")

    sblocks = [(s * 4, min(ICH, s * 4 + 4)) for s in range(6)]
    gu_banks = [(2, 3), (4, 5), (6, 7)]
    prev_h2spill = [None, None]
    prev_ttr = None
    wbuf_free = [None, None]
    sb_ld = {}
    for sbi, (i0, i1) in enumerate(sblocks):
        wb = sbi % 2
        ncols = (i1 - i0) * 128

        @block.sync
        def _(sync, sbi=sbi, i0=i0, i1=i1, wb=wb, ncols=ncols):
            ctl.waits(sync, [toks["ag_wg"], toks["ag_wu"], wbuf_free[wb]])
            gi0, gi1 = i0 * 128, i1 * 128
            for c in range(8):
                s = max(gi0, c * 352)
                e = min(gi1, (c + 1) * 352)
                if s >= e:
                    continue
                for buf, cc_o in ((wgs, cc_wg_out), (wus, cc_wu_out)):
                    ctl.inc_dma(sync.dma_start(
                        out=buf.ap()[:, wb, :, s - gi0:e - gi0],
                        in_=cc_o[c * D:(c + 1) * D, s - c * 352:e - c * 352].rearrange(
                            "(dch p) w -> p dch w", p=128)), "s_ld")
            sb_ld[sbi] = ctl.tok("s_ld")

        for ii in range(i0, i1):
            io = (ii - i0) * 128
            for tt in range(TT):
                gb, ub = gu_banks[(ii * TT + tt) % 3]

                @block.tensor
                def _(te, sbi=sbi, io=io, tt=tt, gb=gb, ub=ub, wb=wb):
                    ctl.waits(te, [sb_ld[sbi], toks["xqfeat"], bank_free[gb], bank_free[ub]],
                              "tensor")
                    for dch in range(DCH):
                        te.matmul(out=banks[gb].ap()[:, 0:512],
                                  lhsT=wgs.ap()[:, wb, dch, io:io + 128],
                                  rhs=xqfeat.ap()[:, dch, tt * 512:(tt + 1) * 512],
                                  start=(dch == 0), stop=(dch == DCH - 1))
                    for dch in range(DCH):
                        i = te.matmul(out=banks[ub].ap()[:, 0:512],
                                      lhsT=wus.ap()[:, wb, dch, io:io + 128],
                                      rhs=xqfeat.ap()[:, dch, tt * 512:(tt + 1) * 512],
                                      start=(dch == 0), stop=(dch == DCH - 1))
                    ctl.inc(i, "s_te")
                t_gu = ctl.tok("s_te")

                @block.scalar
                def _(se, ii=ii, gb=gb, t_gu=t_gu, prev_ttr=prev_ttr):
                    ctl.waits(se, [t_gu, toks["sgx"], prev_ttr], "scalar")
                    i = se.activation(out=silut.ap(), in_=banks[gb].ap()[:, 0:512],
                                      func=AF.Silu, scale=sgxcol.ap()[:, ii:ii + 1])
                    ctl.inc(i, "s_se")
                t_si = ctl.tok("s_se")
                bank_free[gb] = t_si

                @block.vector
                def _(v, ii=ii, tt=tt, ub=ub, t_gu=t_gu, t_si=t_si):
                    ctl.waits(v, [t_gu, t_si, prev_h2spill[0]], "vector")
                    v.tensor_scalar(out=upt.ap(), in0=banks[ub].ap()[:, 0:512],
                                    scalar1=suxcol.ap()[:, ii:ii + 1], scalar2=None,
                                    op0=ALU.mult)
                    vsp(v)
                    v.tensor_tensor(out=h2st.ap(), in0=silut.ap(), in1=upt.ap(),
                                    op=ALU.mult)
                    vsp(v)
                    v.tensor_scalar(out=scratch.ap()[:, 0:512], in0=h2st.ap(),
                                    scalar1=0.0, scalar2=None, op0=ALU.bypass, op1=ALU.max,
                                    accum_out=statmu.ap()[:, 0:1])
                    v.tensor_scalar(out=scratch.ap()[:, 0:512],
                                    in0=h2st.ap(),
                                    scalar1=0.0, scalar2=None, op0=ALU.bypass, op1=ALU.min,
                                    accum_out=statmu.ap()[:, 1:2])
                    vsp(v)
                    v.tensor_tensor(out=hmaxcol.ap()[:, 0:1], in0=hmaxcol.ap()[:, 0:1],
                                    in1=statmu.ap()[:, 0:1], op=ALU.max)
                    v.tensor_scalar(out=statmu.ap()[:, 1:2], in0=statmu.ap()[:, 1:2],
                                    scalar1=-1.0, scalar2=None, op0=ALU.mult)
                    vsp(v)
                    i = v.tensor_tensor(out=hmaxcol.ap()[:, 1:2], in0=hmaxcol.ap()[:, 1:2],
                                        in1=statmu.ap()[:, 1:2], op=ALU.max)
                    ctl.inc(i, "s_ve")
                prev_ttr = ctl.tok("s_ve")
                t_q = ctl.tok("s_ve")
                bank_free[ub] = t_q

                @block.sync
                def _(sync, ii=ii, tt=tt, t_q=t_q):
                    ctl.wait(sync, t_q)
                    toks[("h2sp", ii, tt)] = ctl.inc_dma(sync.dma_start(
                        out=h2_spill[ii, :, tt * 512:(tt + 1) * 512],
                        in_=h2st.ap()), "s_ld")
                prev_h2spill[0] = toks[("h2sp", ii, tt)]
            toks[("h2sp", ii)] = toks[("h2sp", ii, TT - 1)]
        wbuf_free[wb] = ctl.tok("s_te")
    toks["gu_done"] = ctl.tok("s_te")

    @block.vector
    def _(v):
        ctl.wait(v, ctl.tok("s_ve"))
        vsp(v, 4)
        i = v.tensor_tensor(out=hmaxcol.ap()[:, 0:1], in0=hmaxcol.ap()[:, 0:1],
                            in1=hmaxcol.ap()[:, 1:2], op=ALU.max)
        ctl.inc(i, "s_ve")
    t_gm2 = ctl.tok("s_ve")

    @block.tensor
    def _(te):
        ctl.waits(te, [t_gm2, bank_free[0]], "tensor")
        i = te.transpose(out=banks[0].ap()[0:1, 0:128], in_=hmaxcol.ap()[:, 0:1],
                         identity=eyef.ap())
        ctl.inc(i, "s_te")
    t_gm2T = ctl.tok("s_te")

    @block.vector
    def _(v):
        ctl.wait(v, t_gm2T, "vector")
        i = v.tensor_reduce(out=cells.ap()[0:1, 8:9], in_=banks[0].ap()[0:1, 0:128],
                            axis=AX.X, op=ALU.max)
        ctl.inc(i, "s_ve")
    t_gmax2 = ctl.tok("s_ve")
    bank_free[0] = t_gmax2

    @block.gpsimd
    def _(g):
        ctl.wait(g, t_gmax2)
        toks["m2o"] = ctl.inc_dma(g.dma_start(out=cc_m2_in[None, :], in_=cells.ap()[0:1, 8:9]), "s_gd")
        ctl.wait(g, toks["m2o"])
        toks["ar_m2"] = ctl.inc(g.collective_compute(
            "AllReduce", ALU.max, replica_groups=[list(range(8))],
            ins=[cc_m2_in], outs=[cc_m2_out]), "s_cc")
        ctl.wait(g, toks["ar_m2"])
        toks["m2i"] = ctl.inc_dma(g.dma_start(out=cells.ap()[0:1, 9:10], in_=cc_m2_out[None, :]), "s_gd")

    @block.vector
    def _(v):
        ctl.wait(v, toks["m2i"], "vector")
        v.scalar_tensor_tensor(out=cells.ap()[0:1, 10:11], in0=cells.ap()[0:1, 3:4],
                               scalar=1.0 / 127.0, in1=cells.ap()[0:1, 9:10],
                               op0=ALU.mult, op1=ALU.mult)
        vsp(v)
        i = v.reciprocal(out=cells.ap()[0:1, 11:12], in_=cells.ap()[0:1, 10:11])
        ctl.inc(i, "s_ve")
    t_s2c = ctl.tok("s_ve")

    @block.tensor
    def _(te):
        ctl.waits(te, [t_s2c, bank_free[1]], "tensor")
        i = te.matmul(out=banks[1].ap()[:, 0:1], lhsT=onesf.ap(), rhs=cells.ap()[0:8, 11:12],
                      start=True, stop=True)
        ctl.inc(i, "s_te")
    t_bc2 = ctl.tok("s_te")

    @block.scalar
    def _(se):
        ctl.wait(se, t_bc2, "scalar")
        i = se.activation(out=gcol.ap()[:, 2:3], in_=banks[1].ap()[:, 0:1], func=AF.Copy)
        ctl.inc(i, "s_se")
    toks["rs2"] = ctl.tok("s_se")
    bank_free[1] = toks["rs2"]

    @block.gpsimd
    def _(g):
        toks["gb_row"] = ctl.inc_dma(g.dma_start(
            out=garow.ap()[0:1, :], in_=cc_mod_out[None, 5120:6144]), "s_gd")

    @block.scalar
    def _(se):
        ctl.wait(se, toks["gb_row"], "scalar")
        se.activation(out=scratch.ap()[0:1, 0:D], in_=garow.ap()[0:1, :], func=AF.Sign)
        i = se.activation(out=garow.ap()[0:1, :], in_=garow.ap()[0:1, :], func=AF.Abs)
        ctl.inc(i, "s_se")
    t_gbc = ctl.tok("s_se")

    @block.vector
    def _(v):
        ctl.waits(v, [t_gbc, toks["sdrow"], t_s2c], "vector")
        v.tensor_scalar(out=garow.ap()[0:1, :], in0=garow.ap()[0:1, :],
                        scalar1=0.25, scalar2=None, op0=ALU.max)
        vsp(v)
        v.tensor_tensor(out=garow.ap()[0:1, :], in0=garow.ap()[0:1, :],
                        in1=scratch.ap()[0:1, 0:D], op=ALU.mult)
        vsp(v)
        v.tensor_tensor(out=sdrow.ap()[0:1, :], in0=sdrow.ap()[0:1, :],
                        in1=garow.ap()[0:1, :], op=ALU.mult)
        vsp(v)
        i = v.tensor_scalar(out=sdrow.ap()[0:1, :], in0=sdrow.ap()[0:1, :],
                            scalar1=cells.ap()[0:1, 10:11], scalar2=None, op0=ALU.mult)
        ctl.inc(i, "s_ve")
    t_gbsd = ctl.tok("s_ve")

    t_gb_mm = []
    @block.tensor
    def _(te):
        ctl.waits(te, [t_gbsd, bank_free[2], bank_free[3]], "tensor")
        for nh in range(2):
            ns = slice(nh * 256, (nh + 1) * 256)
            te.matmul(out=banks[2].ap()[:, ns], lhsT=onesf.ap(),
                      rhs=sdrow.ap()[0:8, nh * 256:(nh + 1) * 256], start=True, stop=True)
            i = te.matmul(out=banks[3].ap()[:, ns], lhsT=onesf.ap(),
                          rhs=sdrow.ap()[0:8, 512 + nh * 256:512 + (nh + 1) * 256],
                          start=True, stop=True)
        ctl.inc(i, "s_te")
    t_gr = ctl.tok("s_te")

    @block.scalar
    def _(se):
        ctl.wait(se, t_gr, "scalar")
        se.activation(out=modrepA.ap()[:, 0:512], in_=banks[2].ap()[:, 0:512], func=AF.Copy)
        i = se.activation(out=modrepA.ap()[:, 512:1024], in_=banks[3].ap()[:, 0:512], func=AF.Copy)
        ctl.inc(i, "s_se")
    toks["gbsd"] = ctl.tok("s_se")
    bank_free[2] = toks["gbsd"]
    bank_free[3] = toks["gbsd"]

    prev_wd = [toks["gu_done"], toks["gu_done"]]
    prev_rld2 = [toks["gu_done"], toks["gu_done"]]
    for q in range(TT):
        for ii in range(ICH):
            rb = ii % 2
            wb = ii % 2

            @block.sync
            def _(sync, q=q, ii=ii, wb=wb):
                ctl.waits(sync, [toks["ag_wd"], prev_wd[wb]])
                toks[("wdld", q, ii)] = ctl.inc_dma(sync.dma_start(
                    out=wdTs.ap()[:, wb, :].rearrange("p (c d) -> p c d", d=128),
                    in_=cc_wd_out.rearrange("(c r) d -> r c d", r=II)[
                        ii * 128:(ii + 1) * 128, :, :]), "s_ld")

            @block.scalar
            def _(se, q=q, ii=ii, rb=rb):
                ctl.waits(se, [toks[("h2sp", ii, q)], prev_rld2[rb]], "scalar")
                t_r = ctl.inc_dma(se.dma_start(
                    out=rld.ap()[:, rb, :], in_=h2_spill[ii, :, q * 512:(q + 1) * 512]), "s_sc")
                ctl.wait(se, t_r, "scalar")
                ctl.wait(se, toks["rs2"], "scalar")
                i = se.activation(out=x2f.ap()[:, rb, :], in_=rld.ap()[:, rb, :],
                                  func=AF.Copy, scale=gcol.ap()[:, 2:3])
                ctl.inc(i, "s_se")
            t_u2q = ctl.tok("s_se")

            @block.vector
            def _(v, rb=rb, t_u2q=t_u2q, prev_mm=prev_wd[rb]):
                ctl.waits(v, [t_u2q, prev_mm], "vector")
                if QUANT_MODE == "cast":
                    v.tensor_copy(out=x28.ap()[:, rb, :], in_=x2f.ap()[:, rb, :])
                    vsp(v)
                    v.tensor_copy(out=nxq2.ap()[:, rb, :], in_=x28.ap()[:, rb, :])
                    i = vflush(v)
                else:
                    v.tensor_copy(out=x216.ap()[:, rb, :], in_=x2f.ap()[:, rb, :])
                    vsp(v)
                    v.tensor_copy(out=rld.ap()[:, rb, :], in_=x216.ap()[:, rb, :])
                    vsp(v)
                    v.tensor_tensor(out=u2.ap()[:, rb, :], in0=x2f.ap()[:, rb, :],
                                    in1=rld.ap()[:, rb, :], op=ALU.subtract)
                    vsp(v)
                    v.tensor_scalar(out=x2g.ap()[:, rb, :], in0=u2.ap()[:, rb, :],
                                    scalar1=0.5, scalar2=None, op0=ALU.is_ge)
                    v.tensor_scalar(out=u2.ap()[:, rb, :], in0=u2.ap()[:, rb, :],
                                    scalar1=-0.5, scalar2=-1.0, op0=ALU.is_ge, op1=ALU.add)
                    vsp(v)
                    v.tensor_tensor(out=x2g.ap()[:, rb, :], in0=x2g.ap()[:, rb, :],
                                    in1=u2.ap()[:, rb, :], op=ALU.add)
                    vsp(v)
                    v.tensor_tensor(out=rld.ap()[:, rb, :], in0=rld.ap()[:, rb, :],
                                    in1=x2g.ap()[:, rb, :], op=ALU.add)
                    vsp(v)
                    v.tensor_scalar(out=nxq2.ap()[:, rb, :], in0=rld.ap()[:, rb, :],
                                    scalar1=-128.0, scalar2=127.0, op0=ALU.max, op1=ALU.min)
                    i = vflush(v)
                ctl.inc(i, "s_ve")
            t_nx2 = ctl.tok("s_ve")
            prev_rld2[rb] = t_nx2

            @block.tensor
            def _(te, q=q, ii=ii, rb=rb, wb=wb, t_nx2=t_nx2):
                ctl.waits(te, [t_nx2, toks[("wdld", q, ii)]], "tensor")
                if ii == 0:
                    for b in range(8):
                        ctl.wait(te, bank_free[b], "tensor")
                for tc in range(4):
                    for hv in range(2):
                        i = te.matmul(out=banks[tc * 2 + hv].ap()[:, 0:512],
                                      lhsT=nxq2.ap()[:, rb, tc * 128:(tc + 1) * 128],
                                      rhs=wdTs.ap()[:, wb, hv * 512:(hv + 1) * 512],
                                      start=(ii == 0), stop=(ii == ICH - 1))
                ctl.inc(i, "s_te")
            prev_wd[wb] = ctl.tok("s_te")
        t_down = ctl.tok("s_te")

        for tc in range(4):
            c = q * 4 + tc

            @block.vector
            def _(v, tc=tc, c=c, t_down=t_down):
                ctl.waits(v, [t_down, toks["gbsd"]], "vector")
                for hv in range(2):
                    sl = slice(hv * 512, (hv + 1) * 512)
                    v.tensor_tensor(out=scratch.ap()[:, sl], in0=banks[tc * 2 + hv].ap()[:, 0:512],
                                    in1=modrepA.ap()[:, sl], op=ALU.mult)
                vsp(v, 1)
                for hv in range(2):
                    sl = slice(hv * 512, (hv + 1) * 512)
                    i = v.tensor_tensor(out=xres.ap()[:, c, sl], in0=xres.ap()[:, c, sl],
                                        in1=scratch.ap()[:, sl], op=ALU.add)
                ctl.inc(i, "s_ve")
            t_res = ctl.tok("s_ve")
            bank_free[tc * 2] = t_res
            bank_free[tc * 2 + 1] = t_res

            @block.gpsimd
            def _(g, c=c, t_res=t_res):
                ctl.wait(g, t_res)
                ctl.inc_dma(g.dma_start(
                    out=y_out.rearrange("(c p) d -> p c d", p=128)[:, c, :],
                    in_=xres.ap()[:, c, :]), "s_gd")

    if DEBUG_TAP == "h":
        @block.gpsimd
        def _(g):
            g.wait_ge(ctl.sems["s_gd"], ctl.cnt["s_gd"])
            ctl.wait(g, ctl.tok("s_ld"))
            ctl.inc_dma(g.dma_start(
                out=y_out.rearrange("(c p) d -> c p d", p=128), in_=h_spill), "s_gd")
    elif DEBUG_TAP == "mod":
        @block.gpsimd
        def _(g):
            g.wait_ge(ctl.sems["s_gd"], ctl.cnt["s_gd"])
            ctl.wait(g, ctl.tok("s_ld"))
            ctl.inc_dma(g.dma_start(
                out=y_out[0:6, :], in_=cc_mod_out.rearrange("(v d) -> v d", v=6)), "s_gd")
    elif DEBUG_TAP == "qtok":
        @block.gpsimd
        def _(g):
            g.wait_ge(ctl.sems["s_gd"], ctl.cnt["s_gd"])
            ctl.wait(g, ctl.tok("s_ld"))
            ctl.inc_dma(g.dma_start(
                out=y_out.rearrange("(c p) d -> p c d", p=128)[:, 0:1, :],
                in_=qtok.ap().bitcast(F32)[:, 0:16, :].rearrange("p c d -> p (c d)")[:, 0:1024][:, None, :]), "s_gd")
    elif DEBUG_TAP == "prep":
        @block.gpsimd
        def _(g):
            g.wait_ge(ctl.sems["s_gd"], ctl.cnt["s_gd"])
            ctl.wait(g, ctl.tok("s_ld"))
            ctl.inc_dma(g.dma_start(out=y_out[0:128, 0:16], in_=gamcol.ap()), "s_gd")
            ctl.inc_dma(g.dma_start(out=y_out[128:256, 0:4],
                                    in_=siluct.ap().bitcast(F32)[:, 0:4]), "s_gd")
            ctl.inc_dma(g.dma_start(out=y_out[256:384, 0:64],
                                    in_=wqb.ap().bitcast(F32)[:, 0, 0:64]), "s_gd")
            ctl.inc_dma(g.dma_start(out=y_out[384:512, 0:1024], in_=modrepA.ap()), "s_gd")
            ctl.inc_dma(g.dma_start(out=y_out[512:640, 0:1024], in_=modrepB.ap()), "s_gd")
            ctl.inc_dma(g.dma_start(out=y_out[640:656, 0:512],
                                    in_=woutb.ap().bitcast(F32)[32:48 if False else slice(32, 48), :, 0:512].rearrange("p h d -> p (h d)")[:, 0:1024] if False else woutb.ap().bitcast(F32)[32:48, 0, 0:512]), "s_gd")
    elif DEBUG_TAP == "stats":
        @block.gpsimd
        def _(g):
            g.wait_ge(ctl.sems["s_gd"], ctl.cnt["s_gd"])
            ctl.wait(g, ctl.tok("s_ld"))
            ctl.inc_dma(g.dma_start(out=y_out[0:128, 0:TCH], in_=statsum.ap()), "s_gd")
            ctl.inc_dma(g.dma_start(out=y_out[128:256, 0:TCH], in_=statssq.ap()), "s_gd")
            ctl.inc_dma(g.dma_start(out=y_out[256:384, 0:TCH], in_=statrstd.ap()), "s_gd")
            ctl.inc_dma(g.dma_start(out=y_out[384:512, 0:TCH], in_=statB.ap()), "s_gd")
            ctl.inc_dma(g.dma_start(out=y_out[512:640, 0:TCH], in_=statr.ap()), "s_gd")
    elif DEBUG_TAP == "stacc":
        @block.gpsimd
        def _(g):
            g.wait_ge(ctl.sems["s_gd"], ctl.cnt["s_gd"])
            ctl.wait(g, ctl.tok("s_ld"))
            ctl.inc_dma(g.dma_start(
                out=y_out[0:113, 0:819], in_=stacc.ap()), "s_gd")
    elif DEBUG_TAP == "ph2":
        @block.gpsimd
        def _(g):
            g.wait_ge(ctl.sems["s_gd"], ctl.cnt["s_gd"])
            ctl.wait(g, ctl.tok("s_ld"))
            ctl.inc_dma(g.dma_start(
                out=y_out.rearrange("(a p) d -> a p d", p=128)[0:4],
                in_=tb_tap.rearrange("p (a d) -> a p d", d=1024)), "s_gd")
    elif DEBUG_TAP == "ffn":
        @block.gpsimd
        def _(g):
            g.wait_ge(ctl.sems["s_gd"], ctl.cnt["s_gd"])
            ctl.wait(g, ctl.tok("s_ld"))
            ctl.inc_dma(g.dma_start(out=y_out[0:1024, 0:352], in_=cc_wg_out[0:1024, :]), "s_gd")
            ctl.inc_dma(g.dma_start(out=y_out[1024:1152, 0:22], in_=sgcol.ap()), "s_gd")
            ctl.inc_dma(g.dma_start(out=y_out[1152:1280, 0:22], in_=sucol.ap()), "s_gd")
            ctl.inc_dma(g.dma_start(out=y_out[1280:1408, 0:4], in_=gcol.ap()), "s_gd")
            ctl.inc_dma(g.dma_start(out=y_out[1408:1416, 0:16], in_=cells.ap()), "s_gd")
            ctl.inc_dma(g.dma_start(out=y_out[1416:1544, 0:22], in_=sgxcol.ap()), "s_gd")
            ctl.inc_dma(g.dma_start(out=y_out[1544:1896, 0:1024],
                                    in_=cc_wg_in.rearrange("d r -> r d")), "s_gd")
    elif DEBUG_TAP == "h2a":
        @block.gpsimd
        def _(g):
            g.wait_ge(ctl.sems["s_gd"], ctl.cnt["s_gd"])
            ctl.wait(g, ctl.tok("s_ld"))
            for u in range(2):
                ctl.inc_dma(g.dma_start(
                    out=y_out.rearrange("(a w) d -> a w d", w=256)[0:8, u * 128:(u + 1) * 128, :],
                    in_=h2_spill[0:8, :, u * 1024:(u + 1) * 1024]), "s_gd")

    @block.gpsimd
    def _(g):
        g.wait_ge(ctl.sems["s_gd"], ctl.cnt["s_gd"])

    ctx.close()
    return nc


def _shard_inputs(L0_attn_x, L0_attn_gamma, L0_attn_w_q, L0_attn_w_kv, L0_attn_w_out,
                  L0_ffn_clip_x, L0_ffn_clip_down, L0_ffn_w_gate, L0_ffn_w_up,
                  L0_ffn_w_down, L0_c, L0_wmod, T):
    f32 = lambda a: np.ascontiguousarray(np.asarray(a, dtype=np.float32))
    x = f32(L0_attn_x)
    b, n, d = x.shape
    assert n == 4 * T and d == D
    gamma = f32(L0_attn_gamma); wq = f32(L0_attn_w_q); wkv = f32(L0_attn_w_kv)
    wout = f32(L0_attn_w_out); clipx = f32(L0_ffn_clip_x); clipd = f32(L0_ffn_clip_down)
    wg = f32(L0_ffn_w_gate); wu = f32(L0_ffn_w_up); wd = f32(L0_ffn_w_down)
    cc = f32(L0_c); wmod = f32(L0_wmod)
    in_maps = []
    for core in range(N_CORES):
        bb = core // 4
        qq = core % 4
        rg = slice(core * 352, (core + 1) * 352)
        in_maps.append({
            "x": np.ascontiguousarray(x[bb, qq * T:(qq + 1) * T, :]),
            "gamma": gamma, "wq": wq, "wkv": wkv, "wout": wout,
            "clipx": clipx, "clipd": clipd,
            "wg_sh": np.ascontiguousarray(wg[rg, :]),
            "wu_sh": np.ascontiguousarray(wu[rg, :]),
            "wd_sh": np.ascontiguousarray(wd[core * 128:(core + 1) * 128, :]),
            "cvec": np.ascontiguousarray(cc[bb].reshape(8, 128)),
            "wmod_sh": np.ascontiguousarray(wmod[:, qq * 1536:(qq + 1) * 1536]),
        })
    return in_maps


def kernel(L0_attn_x, L0_attn_gamma, L0_attn_w_q, L0_attn_w_kv, L0_attn_w_out,
           L0_ffn_clip_x, L0_ffn_clip_down, L0_ffn_w_gate, L0_ffn_w_up,
           L0_ffn_w_down, L0_c, L0_wmod):
    T = 2048
    in_maps = _shard_inputs(L0_attn_x, L0_attn_gamma, L0_attn_w_q, L0_attn_w_kv,
                            L0_attn_w_out, L0_ffn_clip_x, L0_ffn_clip_down,
                            L0_ffn_w_gate, L0_ffn_w_up, L0_ffn_w_down, L0_c, L0_wmod, T)
    nc = build_nc(T)
    res = run_bass_kernel_spmd(nc, in_maps, core_ids=list(range(N_CORES)))
    out = np.empty((2, 4 * T, D), dtype=np.float32)
    for core in range(N_CORES):
        bb = core // 4
        qq = core % 4
        out[bb, qq * T:(qq + 1) * T, :] = res.results[core]["y"]
    return out


# version 21
# speedup vs baseline: 1.0211x; 1.0211x over previous
import numpy as np
import ml_dtypes
from contextlib import ExitStack
from collections import defaultdict

import concourse.bass as bass
import concourse.mybir as mybir
from concourse.bass_utils import run_bass_kernel_spmd

F32 = mybir.dt.float32
BF16 = mybir.dt.bfloat16
I8 = mybir.dt.int8
I16 = mybir.dt.int16
QUANT_MODE = "cast"
DEBUG_TAP = None
AF = mybir.ActivationFunctionType
ALU = mybir.AluOpType
AX = mybir.AxisListType

D = 1024
DCH = 8
HEADS = 8
DH = 16
II = 2816
ICH = 22
NEG_BIG = -3.0e38
N_CORES = 8


class Ctl:
    def __init__(self, nc, ctx):
        self.nc = nc
        self.sems = {}
        self.cnt = defaultdict(int)
        self.wm = defaultdict(int)
        for name in ("s_ld", "s_sc", "s_gd", "s_cc", "s_te", "s_ve", "s_se", "s_gp"):
            self.sems[name] = ctx.enter_context(nc.semaphore(name))
        self.eng_sem = {"tensor": "s_te", "vector": "s_ve", "scalar": "s_se",
                        "gpsimd": "s_gp"}

    def inc(self, instr, name, n=1):
        instr.then_inc(self.sems[name], n)
        self.cnt[name] += n
        return (name, self.cnt[name])

    def inc_dma(self, instr, name):
        return self.inc(instr, name, 16)

    def tok(self, name):
        return (name, self.cnt[name])

    def wait(self, eng, tok, eng_name=None):
        if tok is None:
            return
        name, v = tok
        if v <= 0:
            return
        if eng_name is not None and self.eng_sem.get(eng_name) == name:
            return
        key = (id(eng), name)
        if self.wm[key] >= v:
            return
        eng.wait_ge(self.sems[name], v)
        self.wm[key] = v

    def waits(self, eng, toks, eng_name=None):
        for t in toks:
            self.wait(eng, t, eng_name)


def build_nc(T=2048):
    TCH = T // 128
    TT = T // 512
    assert T % 512 == 0

    nc = bass.Bass("TRN2", target_bir_lowering=False, debug=False,
                   detect_race_conditions=False)

    x_in = nc.dram_tensor("x", [T, D], F32, kind="ExternalInput").ap()
    gamma_in = nc.dram_tensor("gamma", [D], F32, kind="ExternalInput").ap()
    wq_in = nc.dram_tensor("wq", [D, 128], F32, kind="ExternalInput").ap()
    wkv_in = nc.dram_tensor("wkv", [D, 256], F32, kind="ExternalInput").ap()
    wout_in = nc.dram_tensor("wout", [128, D], F32, kind="ExternalInput").ap()
    clipx_in = nc.dram_tensor("clipx", [1], F32, kind="ExternalInput").ap()
    clipd_in = nc.dram_tensor("clipd", [1], F32, kind="ExternalInput").ap()
    wg_sh_in = nc.dram_tensor("wg_sh", [352, D], F32, kind="ExternalInput").ap()
    wu_sh_in = nc.dram_tensor("wu_sh", [352, D], F32, kind="ExternalInput").ap()
    wd_sh_in = nc.dram_tensor("wd_sh", [128, II], F32, kind="ExternalInput").ap()
    cvec_in = nc.dram_tensor("cvec", [8, 128], F32, kind="ExternalInput").ap()
    wmod_sh_in = nc.dram_tensor("wmod_sh", [D, 1536], F32, kind="ExternalInput").ap()
    y_out = nc.dram_tensor("y", [T, D], F32, kind="ExternalOutput").ap()

    tb_tap = nc.dram_tensor("tb_tap", [128, 4096], F32, kind="Internal").ap()
    h_spill = nc.dram_tensor("h_spill", [TCH, 128, D], F32, kind="Internal").ap()
    h2_spill = nc.dram_tensor("h2_spill", [ICH, 128, T], F32, kind="Internal").ap()

    cc_mod_in = nc.dram_tensor("cc_mod_in", [1536], F32, kind="Internal").ap()
    cc_mod_out = nc.dram_tensor("cc_mod_out", [6144], F32, kind="Internal").ap()
    cc_wg_in = nc.dram_tensor("cc_wg_in", [D, 352], BF16, kind="Internal").ap()
    cc_wg_out = nc.dram_tensor("cc_wg_out", [8 * D, 352], BF16, kind="Internal").ap()
    cc_wu_in = nc.dram_tensor("cc_wu_in", [D, 352], BF16, kind="Internal").ap()
    cc_wu_out = nc.dram_tensor("cc_wu_out", [8 * D, 352], BF16, kind="Internal").ap()
    cc_wd_in = nc.dram_tensor("cc_wd_in", [II, 128], BF16, kind="Internal").ap()
    cc_wd_out = nc.dram_tensor("cc_wd_out", [8 * II, 128], BF16, kind="Internal").ap()
    cc_sg_in = nc.dram_tensor("cc_sg_in", [352], F32, kind="Internal").ap()
    cc_sg_out = nc.dram_tensor("cc_sg_out", [II], F32, kind="Internal").ap()
    cc_su_in = nc.dram_tensor("cc_su_in", [352], F32, kind="Internal").ap()
    cc_su_out = nc.dram_tensor("cc_su_out", [II], F32, kind="Internal").ap()
    cc_sd_in = nc.dram_tensor("cc_sd_in", [128], F32, kind="Internal").ap()
    cc_sd_out = nc.dram_tensor("cc_sd_out", [D], F32, kind="Internal").ap()
    cc_st_in = nc.dram_tensor("cc_st_in", [113, 819], F32, kind="Internal").ap()
    cc_st_out = nc.dram_tensor("cc_st_out", [113, 819], F32, kind="Internal").ap()
    cc_m1_in = nc.dram_tensor("cc_m1_in", [1], F32, kind="Internal").ap()
    cc_m1_out = nc.dram_tensor("cc_m1_out", [1], F32, kind="Internal").ap()
    cc_m2_in = nc.dram_tensor("cc_m2_in", [1], F32, kind="Internal").ap()
    cc_m2_out = nc.dram_tensor("cc_m2_out", [1], F32, kind="Internal").ap()

    eyef_d = nc.inline_tensor(np.eye(128, dtype=np.float32), "eyef_c").ap()
    eyeb_d = nc.inline_tensor(np.eye(128, dtype=ml_dtypes.bfloat16), "eyeb_c").ap()
    onesf_d = nc.inline_tensor(np.ones((8, 128), dtype=np.float32), "onesf_c").ap()
    onesb_d = nc.inline_tensor(np.ones((8, 512), dtype=ml_dtypes.bfloat16), "onesb_c").ap()

    top = nc.sbuf_top
    off = [nc.sbuf_base]
    tensors = {}

    def alloc(name, shape, dtype, cursor=None):
        cur = off if cursor is None else cursor
        per_part = int(np.prod(shape[1:])) * mybir.dt.size(dtype)
        o = (cur[0] + 31) // 32 * 32
        cur[0] = o + per_part
        assert cur[0] <= top, f"SBUF overflow at {name}: {cur[0]} > {top}"
        h = nc.alloc_sbuf_tensor_at(name, list(shape), dtype, offset=o)
        tensors[name] = h
        return h

    xres = alloc("xres", [128, TCH, D], F32)
    eyef = alloc("eyef", [128, 128], F32)
    eyeb = alloc("eyeb", [128, 128], BF16)
    onesf = alloc("onesf", [8, 128], F32)
    onesb = alloc("onesb", [8, 512], BF16)
    cells = alloc("cells", [8, 16], F32)
    gcol = alloc("gcol", [128, 4], F32)
    gamcol = alloc("gamcol", [128, 16], F32)
    siluct = alloc("siluct", [128, 8], BF16)
    statsum = alloc("statsum", [128, TCH], F32)
    statssq = alloc("statssq", [128, TCH], F32)
    statmu = alloc("statmu", [128, TCH], F32)
    statrstd = alloc("statrstd", [128, TCH], F32)
    statB = alloc("statB", [128, TCH], F32)
    statr = alloc("statr", [128, TCH], F32)
    maxs = alloc("maxs", [128, TCH], F32)
    mins = alloc("mins", [128, TCH], F32)
    hmaxcol = alloc("hmaxcol", [128, 2], F32)
    sgcol = alloc("sgcol", [128, ICH], F32)
    sucol = alloc("sucol", [128, ICH], F32)
    sdrow = alloc("sdrow", [8, D], F32)
    sgxcol = alloc("sgxcol", [128, ICH], F32)
    suxcol = alloc("suxcol", [128, ICH], F32)
    wqb = alloc("wqb", [128, DCH, 128], BF16)
    wkvb = alloc("wkvb", [128, DCH, 256], BF16)
    woutb = alloc("woutb", [48, HEADS, D], BF16)
    mrow = alloc("mrow", [8, D], F32)
    garow = alloc("garow", [8, D], F32)
    modrepA = alloc("modrepA", [128, D], F32)
    modrepB = alloc("modrepB", [128, D], F32)
    qtok = alloc("qtok", [128, TCH, 128], BF16)
    stacc = alloc("stacc", [113, 819], F32)
    stA = alloc("stA", [16, HEADS, 48], BF16)
    stB = alloc("stB", [128, HEADS, 48], BF16)
    stC = alloc("stC", [128, HEADS, 48], BF16)
    stD = alloc("stD", [8, HEADS, 48], BF16)
    scratch = alloc("scratch", [128, D], F32)
    scrows = alloc("scrows", [22, 128], F32)
    junkpad = alloc("junkpad", [1, 4], F32)

    def vsp(v, n=2):
        for _ in range(n):
            v.memset(junkpad.ap(), 0.0)

    def vflush(v):
        vsp(v, 2)
        return v.memset(junkpad.ap(), 0.0)

    arena0 = off[0]

    a = [arena0]
    gamrow = alloc("gamrow", [1, D], F32, a)
    crow = alloc("crow", [8, 128], F32, a)
    wqf = alloc("wqf", [128, DCH, 128], F32, a)
    wkvf = alloc("wkvf", [128, DCH, 256], F32, a)
    wmodb = alloc("wmodb", [128, DCH, 384], BF16, a)
    wshf = alloc("wshf", [128, D], F32, a)
    wshu = alloc("wshu", [128, D], F32, a)
    wshb = alloc("wshb", [128, D], BF16, a)
    wsh8 = alloc("wsh8", [128, D], I8, a)
    wsh16 = alloc("wsh16", [128, D], I16, a)
    wshr = alloc("wshr", [128, D], F32, a)
    wdf = alloc("wdf", [128, II], F32, a)
    wdu = alloc("wdu", [128, II // 2], F32, a)
    wdb = alloc("wdb", [128, II // 2], BF16, a)
    wd8 = alloc("wd8", [128, II // 2], I8, a)
    wd16 = alloc("wd16", [128, II // 2], I16, a)
    wdr = alloc("wdr", [128, II // 2], F32, a)
    wdg = alloc("wdg", [128, II // 2], F32, a)
    modstage = alloc("modstage", [1, 1536], F32, a)
    ptile = alloc("ptile", [128, 1024], BF16, a)
    p1_end = a[0]

    a = [arena0]
    hstage = alloc("hstage", [128, 2, D], F32, a)
    hfeat = alloc("hfeat", [128, 2, DCH, 512], BF16, a)
    kfeat = alloc("kfeat", [128, 2, 512], BF16, a)
    vfeat = alloc("vfeat", [128, 2, 512], BF16, a)
    qfeatr = alloc("qfeatr", [128, 2, 512], BF16, a)
    ktok = alloc("ktok", [128, 2, 128], BF16, a)
    phiaug = alloc("phiaug", [128, 2, HEADS, 273], BF16, a)
    vaug = alloc("vaug", [128, 2, HEADS, 17], BF16, a)
    p2_end = a[0]

    a = [arena0]
    phi2q4 = alloc("phi2q4", [128, 4, 256], BF16, a)
    phiqA = alloc("phiqA", [128, 512], BF16, a)
    phiqB = alloc("phiqB", [128, 512], BF16, a)
    qf16 = alloc("qf16", [16, 512], BF16, a)
    ofd = alloc("ofd", [48, HEADS, 512], BF16, a)
    denr = alloc("denr", [8, HEADS, 512], F32, a)
    denr2 = alloc("denr2", [1, HEADS, 512], F32, a)
    p5_end = a[0]

    a = [arena0]
    xqfeat = alloc("xqfeat", [128, DCH, T], BF16, a)
    hq = alloc("hq", [128, D], F32, a)
    xq8 = alloc("xq8", [128, D], I8, a)
    xq16 = alloc("xq16", [128, D], I16, a)
    xqf = alloc("xqf", [128, D], F32, a)
    nxq = alloc("nxq", [128, D], BF16, a)
    wgs = alloc("wgs", [128, 2, DCH, 512], BF16, a)
    wus = alloc("wus", [128, 2, DCH, 512], BF16, a)
    silut = alloc("silut", [128, 512], F32, a)
    upt = alloc("upt", [128, 512], F32, a)
    h2st = alloc("h2st", [128, 512], F32, a)
    p8_end = a[0]

    a = [arena0]
    wdTs = alloc("wdTs", [128, 2, D], BF16, a)
    rld = alloc("rld", [128, 2, 512], F32, a)
    x28 = alloc("x28", [128, 2, 512], I8, a)
    x216 = alloc("x216", [128, 2, 512], I16, a)
    x2f = alloc("x2f", [128, 2, 512], F32, a)
    x2g = alloc("x2g", [128, 2, 512], F32, a)
    u2 = alloc("u2", [128, 2, 512], F32, a)
    nxq2 = alloc("nxq2", [128, 2, 512], BF16, a)
    p10_end = a[0]

    banks = [nc.alloc_psum_tensor(f"bank{i}", [128, 512], F32) for i in range(8)]

    ctx = ExitStack()
    ctl = Ctl(nc, ctx)
    block = ctx.enter_context(nc.Block())

    bank_free = [None] * 8
    toks = {}

    @block.sync
    def _(sync):
        toks["x"] = ctl.inc_dma(sync.dma_start(
            out=xres.ap(), in_=x_in.rearrange("(c p) d -> p c d", p=128)), "s_ld")
        toks["eyef"] = ctl.inc_dma(sync.dma_start(out=eyef.ap(), in_=eyef_d), "s_ld")
        toks["eyeb"] = ctl.inc_dma(sync.dma_start(out=eyeb.ap(), in_=eyeb_d), "s_ld")
        toks["onesf"] = ctl.inc_dma(sync.dma_start(out=onesf.ap(), in_=onesf_d), "s_ld")
        toks["onesb"] = ctl.inc_dma(sync.dma_start(out=onesb.ap(), in_=onesb_d), "s_ld")
        toks["gam"] = ctl.inc_dma(sync.dma_start(out=gamrow.ap(), in_=gamma_in[None, :]), "s_ld")
        toks["crow"] = ctl.inc_dma(sync.dma_start(out=crow.ap(), in_=cvec_in), "s_ld")
        toks["wqf"] = ctl.inc_dma(sync.dma_start(
            out=wqf.ap(), in_=wq_in.rearrange("(c p) m -> p c m", p=128)), "s_ld")
        toks["wkvf"] = ctl.inc_dma(sync.dma_start(
            out=wkvf.ap(), in_=wkv_in.rearrange("(c p) m -> p c m", p=128)), "s_ld")
        toks["wdf"] = ctl.inc_dma(sync.dma_start(out=wdf.ap(), in_=wd_sh_in), "s_ld")

    @block.gpsimd
    def _(g):
        toks["wmod0"] = ctl.inc_dma(g.dma_start(
            out=wmodb.ap(),
            in_=wmod_sh_in.rearrange("(c p) m -> p c m", p=128)[:, :, 0:384]), "s_gd")
        toks["woutb"] = ctl.inc_dma(g.dma_start(
            out=woutb.ap()[32:48, :, :],
            in_=wout_in.rearrange("(h e) d -> e h d", h=8)), "s_gd")

    @block.vector
    def _(v):
        v.memset(stacc.ap(), 0.0)
        v.memset(stA.ap(), 0.0)
        v.memset(stB.ap(), 0.0)
        v.memset(stC.ap(), 0.0)
        v.memset(stD.ap(), 0.0)
        v.memset(mrow.ap(), 0.0)
        v.memset(garow.ap(), 0.0)
        v.memset(sdrow.ap(), 0.0)
        v.memset(cells.ap(), 0.0)
        v.memset(hmaxcol.ap(), NEG_BIG)
        for b in (5, 6, 7):
            i = v.memset(banks[b].ap()[:, 0:512], 0.0)
        ctl.inc(i, "s_ve")
    toks["memset"] = ctl.tok("s_ve")

    @block.sync
    def _(sync):
        ctl.wait(sync, toks["memset"])
        toks["clip"] = ctl.inc_dma(sync.dma_start(out=cells.ap()[0:1, 0:1],
                                                  in_=clipx_in[None, :]), "s_ld")
        toks["clip2"] = ctl.inc_dma(sync.dma_start(out=cells.ap()[0:1, 1:2],
                                                   in_=clipd_in[None, :]), "s_ld")

    @block.scalar
    def _(se):
        ctl.waits(se, [toks["crow"], toks["clip"], toks["clip2"]], "scalar")
        se.activation(out=crow.ap(), in_=crow.ap(), func=AF.Silu)
        se.activation(out=cells.ap()[0:1, 2:3], in_=cells.ap()[0:1, 0:1], func=AF.Sigmoid)
        i = se.activation(out=cells.ap()[0:1, 3:4], in_=cells.ap()[0:1, 1:2], func=AF.Sigmoid)
        ctl.inc(i, "s_se")
    toks["siluc"] = ctl.tok("s_se")

    @block.tensor
    def _(te):
        ctl.waits(te, [toks["gam"], toks["eyef"], toks["siluc"]], "tensor")
        for ch in range(DCH):
            te.transpose(out=banks[0].ap()[:, ch:ch + 1],
                         in_=gamrow.ap()[0:1, ch * 128:(ch + 1) * 128],
                         identity=eyef.ap()[0:1, 0:1])
        i = te.transpose(out=banks[0].ap()[:, 8:16], in_=crow.ap(),
                         identity=eyef.ap()[0:8, 0:8])
        ctl.inc(i, "s_te")
    toks["cols_mm"] = ctl.tok("s_te")

    @block.scalar
    def _(se):
        ctl.wait(se, toks["cols_mm"], "scalar")
        se.activation(out=gamcol.ap()[:, 0:8], in_=banks[0].ap()[:, 0:8], func=AF.Copy)
        se.activation(out=gamcol.ap()[:, 8:16], in_=banks[0].ap()[:, 0:8],
                      func=AF.Copy, scale=0.25)
        i = se.activation(out=siluct.ap(), in_=banks[0].ap()[:, 8:16], func=AF.Copy)
        ctl.inc(i, "s_se")
    toks["cols"] = ctl.tok("s_se")
    bank_free[0] = toks["cols"]

    @block.vector
    def _(v):
        ctl.waits(v, [toks["wqf"], toks["wkvf"], toks["cols"]], "vector")
        for ch in range(DCH):
            v.tensor_scalar(out=wqb.ap()[:, ch, :], in0=wqf.ap()[:, ch, :],
                            scalar1=gamcol.ap()[:, 8 + ch:9 + ch], scalar2=None, op0=ALU.mult)
            i = v.tensor_scalar(out=wkvb.ap()[:, ch, :], in0=wkvf.ap()[:, ch, :],
                                scalar1=gamcol.ap()[:, ch:ch + 1], scalar2=None, op0=ALU.mult)
        ctl.inc(i, "s_ve")
    toks["wproj"] = ctl.tok("s_ve")

    t_mvq = toks["wmod0"]
    for qd in range(4):
        bq = 1 + (qd % 2)

        @block.tensor
        def _(te, qd=qd, bq=bq, t_mvq=t_mvq):
            ctl.waits(te, [toks["cols"], t_mvq, bank_free[bq]], "tensor")
            for ch in range(DCH):
                i = te.matmul(out=banks[bq].ap()[0:1, 0:384], lhsT=siluct.ap()[:, ch:ch + 1],
                              rhs=wmodb.ap()[:, ch, 0:384],
                              start=(ch == 0), stop=(ch == DCH - 1))
            ctl.inc(i, "s_te")
        t_mv = ctl.tok("s_te")

        @block.scalar
        def _(se, qd=qd, bq=bq, t_mv=t_mv):
            ctl.wait(se, t_mv, "scalar")
            i = se.activation(out=modstage.ap()[0:1, qd * 384:(qd + 1) * 384],
                              in_=banks[bq].ap()[0:1, 0:384], func=AF.Copy)
            ctl.inc(i, "s_se")
        bank_free[bq] = ctl.tok("s_se")

        if qd < 3:
            @block.gpsimd
            def _(g, qd=qd, t_mv=t_mv):
                ctl.wait(g, t_mv)
                toks["wmodq"] = ctl.inc_dma(g.dma_start(
                    out=wmodb.ap(),
                    in_=wmod_sh_in.rearrange("(c p) m -> p c m", p=128)[
                        :, :, (qd + 1) * 384:(qd + 2) * 384]), "s_gd")
            t_mvq = toks["wmodq"]

    @block.gpsimd
    def _(g):
        ctl.wait(g, ctl.tok("s_se"))
        toks["modst"] = ctl.inc_dma(g.dma_start(out=cc_mod_in[None, :], in_=modstage.ap()[0:1, :]), "s_gd")
        ctl.wait(g, toks["modst"])
        toks["ag_mod"] = ctl.inc(g.collective_compute(
            "AllGather", ALU.bypass, replica_groups=[[0, 1, 2, 3], [4, 5, 6, 7]],
            ins=[cc_mod_in], outs=[cc_mod_out]), "s_cc")

    prev_wsh_read = None
    prev_wsh_store = None

    def emit_wscale(rows, wt_ap, wtok, prev_store):
        @block.vector
        def _(v):
            ctl.waits(v, [wtok, prev_store], "vector")
            i = v.tensor_reduce(out=scratch.ap()[0:rows, 0:1], in_=wt_ap, axis=AX.X,
                                op=ALU.max, apply_absolute_value=True)
            ctl.inc(i, "s_ve")
        t_red = ctl.tok("s_ve")

        @block.scalar
        def _(se):
            ctl.wait(se, t_red, "scalar")
            i = se.activation(out=scratch.ap()[0:rows, 3:4], in_=scratch.ap()[0:rows, 0:1],
                              func=AF.Copy, scale=1.0 / 127.0)
            ctl.inc(i, "s_se")
        t_c3 = ctl.tok("s_se")

        @block.vector
        def _(v):
            ctl.wait(v, t_c3, "vector")
            i = v.reciprocal(out=scratch.ap()[0:rows, 2:3], in_=scratch.ap()[0:rows, 3:4])
            ctl.inc(i, "s_ve")
        return ctl.tok("s_ve")

    def emit_quant(rows, cols, wt_ap, wu_ap, wb_ap, w8_ap, w16_ap, wr_ap, wg_ap,
                   t_rs):

        @block.scalar
        def _(se):
            ctl.wait(se, t_rs, "scalar")
            i = se.activation(out=wr_ap, in_=wt_ap, func=AF.Copy,
                              scale=scratch.ap()[0:rows, 2:3])
            ctl.inc(i, "s_se")
        t_u = ctl.tok("s_se")

        if QUANT_MODE == "cast":
            @block.vector
            def _(v):
                ctl.wait(v, t_u, "vector")
                v.tensor_copy(out=w8_ap, in_=wr_ap)
                vsp(v)
                v.tensor_copy(out=wb_ap, in_=w8_ap)
                i = vflush(v)
                ctl.inc(i, "s_ve")
        else:
            @block.vector
            def _(v):
                ctl.wait(v, t_u, "vector")
                v.tensor_copy(out=w16_ap, in_=wr_ap)
                vsp(v)
                v.tensor_copy(out=wu_ap, in_=w16_ap)
                vsp(v)
                v.tensor_tensor(out=wr_ap, in0=wr_ap, in1=wu_ap, op=ALU.subtract)
                vsp(v)
                v.tensor_scalar(out=wg_ap, in0=wr_ap, scalar1=0.5, scalar2=None, op0=ALU.is_ge)
                v.tensor_scalar(out=wr_ap, in0=wr_ap, scalar1=-0.5, scalar2=-1.0,
                                op0=ALU.is_ge, op1=ALU.add)
                vsp(v)
                v.tensor_tensor(out=wg_ap, in0=wg_ap, in1=wr_ap, op=ALU.add)
                vsp(v)
                v.tensor_tensor(out=wu_ap, in0=wu_ap, in1=wg_ap, op=ALU.add)
                vsp(v)
                v.tensor_scalar(out=wb_ap, in0=wu_ap, scalar1=-128.0, scalar2=127.0,
                                op0=ALU.max, op1=ALU.min)
                i = vflush(v)
                ctl.inc(i, "s_ve")
        return t_u, ctl.tok("s_ve")

    wg_tiles = [(0, 128), (128, 256), (256, 352)]
    prev_ptile = None
    for w_in, cc_w, cc_s in ((wg_sh_in, cc_wg_in, cc_sg_in),
                             (wu_sh_in, cc_wu_in, cc_su_in)):
        for (r0, r1) in wg_tiles:
            rows = r1 - r0

            @block.sync
            def _(sync, w_in=w_in, r0=r0, r1=r1, rows=rows):
                ctl.waits(sync, [prev_wsh_read])
                toks["wsh_ld"] = ctl.inc_dma(sync.dma_start(
                    out=wshf.ap()[0:rows, :], in_=w_in[r0:r1, :]), "s_ld")

            t_rs = emit_wscale(rows, wshf.ap()[0:rows, :], toks["wsh_ld"], prev_wsh_store)
            t_u, t_q = emit_quant(rows, D, wshf.ap()[0:rows, :], wshu.ap()[0:rows, :],
                                  wshb.ap()[0:rows, :], wsh8.ap()[0:rows, :],
                                  wsh16.ap()[0:rows, :], wshr.ap()[0:rows, :],
                                  wshf.ap()[0:rows, :], t_rs)
            prev_wsh_read = t_q

            @block.tensor
            def _(te, rows=rows, t_q=t_q, prev_ptile=prev_ptile):
                ctl.waits(te, [t_q, toks["eyeb"], bank_free[3], prev_ptile], "tensor")
                bv = banks[3].ap().bitcast(BF16)
                for dch in range(DCH):
                    i = te.transpose(out=bv[:, dch * 128:dch * 128 + rows],
                                     in_=wshb.ap()[0:rows, dch * 128:(dch + 1) * 128],
                                     identity=eyeb.ap()[0:rows, 0:rows])
                ctl.inc(i, "s_te")
            t_wT = ctl.tok("s_te")

            @block.scalar
            def _(se, rows=rows, t_wT=t_wT, prev_ptile=prev_ptile):
                ctl.waits(se, [t_wT, prev_ptile], "scalar")
                bv = banks[3].ap().bitcast(BF16)
                for dch in range(DCH):
                    i = se.activation(out=ptile.ap()[:, dch * 128:dch * 128 + rows],
                                      in_=bv[:, dch * 128:dch * 128 + rows], func=AF.Copy)
                ctl.inc(i, "s_se")
            t_wTe = ctl.tok("s_se")
            bank_free[3] = t_wTe

            @block.sync
            def _(sync, cc_w=cc_w, cc_s=cc_s, r0=r0, r1=r1, rows=rows, t_q=t_q,
                  t_wTe=t_wTe):
                ctl.waits(sync, [t_q, t_wTe])
                for dch in range(DCH):
                    ctl.inc_dma(sync.dma_start(
                        out=cc_w[dch * 128:(dch + 1) * 128, r0:r1],
                        in_=ptile.ap()[:, dch * 128:dch * 128 + rows]), "s_ld")
                ctl.inc_dma(sync.dma_start(out=cc_s[r0:r1],
                                           in_=scratch.ap()[0:rows, 3:4]), "s_ld")
            prev_wsh_store = ctl.tok("s_ld")
            prev_ptile = prev_wsh_store

    t_rs = emit_wscale(128, wdf.ap(), toks["wdf"], prev_wsh_store)
    prev_half_store = None
    for half in range(2):
        hsl = slice(half * (II // 2), (half + 1) * (II // 2))
        if prev_half_store is not None:
            @block.vector
            def _(v, prev_half_store=prev_half_store):
                ctl.wait(v, prev_half_store, "vector")
                ctl.inc(v.engine_nop(), "s_ve")
            t_rs2 = ctl.tok("s_ve")
        else:
            t_rs2 = t_rs
        t_u, t_qd = emit_quant(128, II // 2, wdf.ap()[:, hsl], wdu.ap(), wdb.ap(),
                               wd8.ap(), wd16.ap(), wdr.ap(), wdg.ap(), t_rs2)

        h0 = half * (II // 2)
        for grp in range(2):
            j0, j1 = (0, 8) if grp == 0 else (8, 11)

            @block.tensor
            def _(te, j0=j0, j1=j1, t_qd=t_qd, prev_ptile=prev_ptile):
                ctl.waits(te, [t_qd, toks["eyeb"], bank_free[3], prev_ptile], "tensor")
                bv = banks[3].ap().bitcast(BF16)
                for j in range(j0, j1):
                    i = te.transpose(out=bv[:, (j - j0) * 128:(j - j0 + 1) * 128],
                                     in_=wdb.ap()[:, j * 128:(j + 1) * 128],
                                     identity=eyeb.ap())
                ctl.inc(i, "s_te")
            t_wdT = ctl.tok("s_te")

            @block.scalar
            def _(se, j0=j0, j1=j1, t_wdT=t_wdT, prev_ptile=prev_ptile):
                ctl.waits(se, [t_wdT, prev_ptile], "scalar")
                bv = banks[3].ap().bitcast(BF16)
                for j in range(j0, j1):
                    i = se.activation(out=ptile.ap()[:, (j - j0) * 128:(j - j0 + 1) * 128],
                                      in_=bv[:, (j - j0) * 128:(j - j0 + 1) * 128],
                                      func=AF.Copy)
                ctl.inc(i, "s_se")
            t_wdTe = ctl.tok("s_se")
            bank_free[3] = t_wdTe

            @block.sync
            def _(sync, h0=h0, j0=j0, j1=j1, t_wdTe=t_wdTe):
                ctl.wait(sync, t_wdTe)
                for j in range(j0, j1):
                    ctl.inc_dma(sync.dma_start(
                        out=cc_wd_in[h0 + j * 128:h0 + (j + 1) * 128, :],
                        in_=ptile.ap()[:, (j - j0) * 128:(j - j0 + 1) * 128]), "s_ld")
            prev_ptile = ctl.tok("s_ld")
        prev_half_store = ctl.tok("s_ld")

    @block.sync
    def _(sync):
        ins = sync.dma_start(out=cc_sd_in, in_=scratch.ap()[0:128, 3:4])
        ctl.inc_dma(ins, "s_ld")
    toks["wq_stores"] = ctl.tok("s_ld")

    @block.gpsimd
    def _(g):
        ctl.wait(g, toks["wq_stores"])
        toks["ag_wg"] = ctl.inc(g.collective_compute(
            "AllGather", ALU.bypass, replica_groups=[list(range(8))],
            ins=[cc_wg_in], outs=[cc_wg_out]), "s_cc")
        toks["ag_wu"] = ctl.inc(g.collective_compute(
            "AllGather", ALU.bypass, replica_groups=[list(range(8))],
            ins=[cc_wu_in], outs=[cc_wu_out]), "s_cc")
        toks["ag_wd"] = ctl.inc(g.collective_compute(
            "AllGather", ALU.bypass, replica_groups=[list(range(8))],
            ins=[cc_wd_in], outs=[cc_wd_out]), "s_cc")
        toks["ag_sg"] = ctl.inc(g.collective_compute(
            "AllGather", ALU.bypass, replica_groups=[list(range(8))],
            ins=[cc_sg_in], outs=[cc_sg_out]), "s_cc")
        toks["ag_su"] = ctl.inc(g.collective_compute(
            "AllGather", ALU.bypass, replica_groups=[list(range(8))],
            ins=[cc_su_in], outs=[cc_su_out]), "s_cc")
        toks["ag_sd"] = ctl.inc(g.collective_compute(
            "AllGather", ALU.bypass, replica_groups=[list(range(8))],
            ins=[cc_sd_in], outs=[cc_sd_out]), "s_cc")

    @block.vector
    def _(v):
        ctl.wait(v, toks["wq_stores"], "vector")
        v.memset(vaug.ap()[:, :, :, 16:17], 1.0)
        i = v.memset(phiaug.ap()[:, :, :, 272:273], 1.0)
        ctl.inc(i, "s_ve")
    toks["arena2"] = ctl.tok("s_ve")

    @block.gpsimd
    def _(g):
        ctl.waits(g, [toks["ag_mod"], toks["memset"]])
        toks["ca_row"] = ctl.inc_dma(g.dma_start(
            out=mrow.ap()[0:1, :], in_=cc_mod_out[None, 1024:2048]), "s_gd")
        toks["ga_row"] = ctl.inc_dma(g.dma_start(
            out=garow.ap()[0:1, :], in_=cc_mod_out[None, 2048:3072]), "s_gd")

    @block.scalar
    def _(se):
        ctl.wait(se, toks["ga_row"], "scalar")
        se.activation(out=scratch.ap()[0:1, 0:D], in_=garow.ap()[0:1, :], func=AF.Sign)
        i = se.activation(out=garow.ap()[0:1, :], in_=garow.ap()[0:1, :], func=AF.Abs)
        ctl.inc(i, "s_se")
    t_cl = ctl.tok("s_se")

    @block.vector
    def _(v):
        ctl.wait(v, t_cl, "vector")
        v.tensor_scalar(out=garow.ap()[0:1, :], in0=garow.ap()[0:1, :],
                        scalar1=0.25, scalar2=None, op0=ALU.max)
        vsp(v)
        i = v.tensor_tensor(out=garow.ap()[0:1, :], in0=garow.ap()[0:1, :],
                            in1=scratch.ap()[0:1, 0:D], op=ALU.mult)
        ctl.inc(i, "s_ve")
    toks["clamp"] = ctl.tok("s_ve")

    for half in range(2):
        sl = slice(half * 512, (half + 1) * 512)

        @block.tensor
        def _(te, sl=sl):
            ctl.waits(te, [toks["clamp"], toks["ca_row"], toks["onesf"],
                           bank_free[0], bank_free[2]], "tensor")
            for nh in range(2):
                ns = slice(nh * 256, (nh + 1) * 256)
                nsl = slice(sl.start + nh * 256, sl.start + (nh + 1) * 256)
                te.matmul(out=banks[0].ap()[:, ns], lhsT=onesf.ap(),
                          rhs=mrow.ap()[0:8, nsl], start=True, stop=True)
                i = te.matmul(out=banks[2].ap()[32:48, ns], lhsT=onesf.ap()[0:8, 0:16],
                              rhs=garow.ap()[0:8, nsl], start=True, stop=True)
            ctl.inc(i, "s_te")
        t_rep_mm = ctl.tok("s_te")

        @block.scalar
        def _(se, sl=sl, t_rep_mm=t_rep_mm):
            ctl.wait(se, t_rep_mm, "scalar")
            i = se.activation(out=modrepA.ap()[:, sl], in_=banks[0].ap()[:, 0:512],
                              func=AF.Copy, bias=1.0)
            ctl.inc(i, "s_se")
        t_repe = ctl.tok("s_se")

        @block.vector
        def _(v, sl=sl, t_rep_mm=t_rep_mm):
            ctl.waits(v, [t_rep_mm, toks["woutb"]], "vector")
            for hh in range(HEADS):
                i = v.tensor_tensor(out=woutb.ap()[32:48, hh, sl],
                                    in0=woutb.ap()[32:48, hh, sl],
                                    in1=banks[2].ap()[32:48, 0:512], op=ALU.mult)
            ctl.inc(i, "s_ve")
        bank_free[0] = t_repe
        bank_free[2] = ctl.tok("s_ve")

    @block.gpsimd
    def _(g):
        ctl.wait(g, ctl.tok("s_te"))
        toks["sa_row"] = ctl.inc_dma(g.dma_start(
            out=mrow.ap()[0:1, :], in_=cc_mod_out[None, 0:1024]), "s_gd")

    for half in range(2):
        sl = slice(half * 512, (half + 1) * 512)

        @block.tensor
        def _(te, sl=sl):
            ctl.waits(te, [toks["sa_row"], bank_free[1]], "tensor")
            for nh in range(2):
                i = te.matmul(out=banks[1].ap()[:, nh * 256:(nh + 1) * 256], lhsT=onesf.ap(),
                              rhs=mrow.ap()[0:8, sl.start + nh * 256:sl.start + (nh + 1) * 256],
                              start=True, stop=True)
            ctl.inc(i, "s_te")
        t_rep_mm = ctl.tok("s_te")

        @block.scalar
        def _(se, sl=sl, t_rep_mm=t_rep_mm):
            ctl.wait(se, t_rep_mm, "scalar")
            i = se.activation(out=modrepB.ap()[:, sl], in_=banks[1].ap()[:, 0:512], func=AF.Copy)
            ctl.inc(i, "s_se")
        bank_free[1] = ctl.tok("s_se")
    toks["modrep"] = ctl.tok("s_se")

    st_banks = (5, 6, 7)
    prev_state_add = None
    prev_tokuse2 = [None, None]
    prev_hT_evac = [None, None]
    prev_projevac = None
    prev_tokuse = None
    prev_hfeat_use = [None, None]

    for tt in range(TT):
        ch0 = tt * 4

        @block.scalar
        def _(se, ch0=ch0):
            ctl.waits(se, [toks["x"], toks["modrep"], toks["wq_stores"]], "scalar")
            for c in range(ch0, ch0 + 4):
                se.activation(out=scratch.ap(), in_=xres.ap()[:, c, :], func=AF.Square,
                              accum_out=statssq.ap()[:, c:c + 1])
            for c in range(ch0, ch0 + 4):
                i = se.activation(out=scratch.ap(), in_=xres.ap()[:, c, :], func=AF.Copy,
                                  accum_out=statsum.ap()[:, c:c + 1])
            ctl.inc(i, "s_se")
        t_stats = ctl.tok("s_se")

        @block.vector
        def _(v, ch0=ch0, t_stats=t_stats):
            ctl.wait(v, t_stats, "vector")
            sl = slice(ch0, ch0 + 4)
            v.tensor_scalar(out=statmu.ap()[:, sl], in0=statsum.ap()[:, sl],
                            scalar1=1.0 / D, scalar2=None, op0=ALU.mult)
            vsp(v)
            for cc_ in range(ch0, ch0 + 4):
                v.tensor_scalar(out=statB.ap()[:, cc_:cc_ + 1],
                                in0=statmu.ap()[:, cc_:cc_ + 1],
                                scalar1=statmu.ap()[:, cc_:cc_ + 1], scalar2=None,
                                op0=ALU.mult)
            vsp(v)
            v.scalar_tensor_tensor(out=statB.ap()[:, sl], in0=statssq.ap()[:, sl],
                                   scalar=1.0 / D, in1=statB.ap()[:, sl],
                                   op0=ALU.mult, op1=ALU.subtract)
            vsp(v)
            v.tensor_scalar(out=statB.ap()[:, sl], in0=statB.ap()[:, sl],
                            scalar1=1e-5, scalar2=None, op0=ALU.add)
            vsp(v)
            i = v.reciprocal(out=statrstd.ap()[:, sl], in_=statB.ap()[:, sl])
            ctl.inc(i, "s_ve")
        t_chain = ctl.tok("s_ve")

        @block.scalar
        def _(se, ch0=ch0, t_chain=t_chain):
            ctl.wait(se, t_chain, "scalar")
            sl = slice(ch0, ch0 + 4)
            i = se.activation(out=statrstd.ap()[:, sl], in_=statrstd.ap()[:, sl], func=AF.Sqrt)
            ctl.inc(i, "s_se")
        t_rstd = ctl.tok("s_se")

        @block.vector
        def _(v, ch0=ch0, t_rstd=t_rstd):
            ctl.wait(v, t_rstd, "vector")
            sl = slice(ch0, ch0 + 4)
            i = v.scalar_tensor_tensor(out=statB.ap()[:, sl], in0=statmu.ap()[:, sl],
                                       scalar=-1.0, in1=statrstd.ap()[:, sl],
                                       op0=ALU.mult, op1=ALU.mult)
            ctl.inc(i, "s_ve")
        t_B = ctl.tok("s_ve")

        for ci in range(4):
            c = ch0 + ci
            hb = ci % 2

            @block.vector
            def _(v, c=c, hb=hb, t_B=t_B):
                ctl.waits(v, [t_B, prev_hT_evac[hb], toks["wq_stores"]], "vector")
                hv = hstage.ap()[:, hb, :]
                vsp(v)
                v.tensor_scalar(out=hv, in0=xres.ap()[:, c, :],
                                scalar1=statrstd.ap()[:, c:c + 1],
                                scalar2=statB.ap()[:, c:c + 1],
                                op0=ALU.mult, op1=ALU.add)
                vsp(v)
                v.tensor_tensor(out=hv, in0=hv, in1=modrepA.ap(), op=ALU.mult)
                vsp(v)
                i = v.tensor_tensor(out=hv, in0=hv, in1=modrepB.ap(), op=ALU.add)
                ctl.inc(i, "s_ve")
            t_h = ctl.tok("s_ve")

            @block.scalar
            def _(se, c=c, hb=hb, t_h=t_h):
                ctl.wait(se, t_h, "scalar")
                i = se.activation(out=scratch.ap(), in_=hstage.ap()[:, hb, :],
                                  func=AF.Square, accum_out=statr.ap()[:, c:c + 1])
                ctl.inc(i, "s_se")
            t_hsq = ctl.tok("s_se")

            @block.tensor
            def _(te, c=c, hb=hb, t_h=t_h):
                ctl.waits(te, [t_h, bank_free[0], bank_free[1]], "tensor")
                for dch in range(DCH):
                    bk = 0 if dch < 4 else 1
                    i = te.transpose(out=banks[bk].ap()[:, (dch % 4) * 128:(dch % 4 + 1) * 128],
                                     in_=hstage.ap()[:, hb, dch * 128:(dch + 1) * 128],
                                     identity=eyef.ap())
                ctl.inc(i, "s_te")
            t_hT = ctl.tok("s_te")

            @block.scalar
            def _(se, c=c, ci=ci, tt=tt, t_hT=t_hT):
                ctl.waits(se, [t_hT, prev_hfeat_use[tt % 2]], "scalar")
                for dch in range(DCH):
                    bk = 0 if dch < 4 else 1
                    i = se.activation(out=hfeat.ap()[:, tt % 2, dch, ci * 128:(ci + 1) * 128],
                                      in_=banks[bk].ap()[:, (dch % 4) * 128:(dch % 4 + 1) * 128],
                                      func=AF.Copy)
                ctl.inc(i, "s_se")
            prev_hT_evac[hb] = ctl.tok("s_se")
            bank_free[0] = ctl.tok("s_se")
            bank_free[1] = ctl.tok("s_se")
        t_hfeat = ctl.tok("s_se")

        @block.scalar
        def _(se, ch0=ch0, t_hsq=t_hsq):
            ctl.wait(se, t_hsq, "scalar")
            sl = slice(ch0, ch0 + 4)
            i = se.activation(out=statr.ap()[:, sl], in_=statr.ap()[:, sl], func=AF.Sqrt)
            ctl.inc(i, "s_se")
        t_nn = ctl.tok("s_se")

        @block.vector
        def _(v, ch0=ch0, t_nn=t_nn):
            ctl.wait(v, t_nn, "vector")
            sl = slice(ch0, ch0 + 4)
            v.tensor_scalar(out=statB.ap()[:, sl], in0=statr.ap()[:, sl],
                            scalar1=1e-12, scalar2=1.0 / 32.0,
                            op0=ALU.max, op1=ALU.mult)
            vsp(v)
            i = v.reciprocal(out=statr.ap()[:, sl], in_=statB.ap()[:, sl])
            ctl.inc(i, "s_ve")
        toks[("rt", tt)] = ctl.tok("s_ve")

        @block.tensor
        def _(te, tt=tt, t_hfeat=t_hfeat):
            ctl.waits(te, [t_hfeat, toks["wproj"], prev_projevac,
                           bank_free[2], bank_free[3], bank_free[4]], "tensor")
            for (bk, w_t, c0, c1) in ((2, wqb, 0, 128), (3, wkvb, 0, 128), (4, wkvb, 128, 256)):
                for ch in range(DCH):
                    i = te.matmul(out=banks[bk].ap()[:, 0:512],
                                  lhsT=w_t.ap()[:, ch, c0:c1],
                                  rhs=hfeat.ap()[:, tt % 2, ch, :],
                                  start=(ch == 0), stop=(ch == DCH - 1))
            ctl.inc(i, "s_te")
        t_proj = ctl.tok("s_te")
        prev_hfeat_use[tt % 2] = t_proj

        @block.scalar
        def _(se, tt=tt, t_proj=t_proj):
            ctl.waits(se, [t_proj, prev_tokuse], "scalar")
            se.activation(out=qfeatr.ap()[:, tt % 2, :], in_=banks[2].ap()[:, 0:512], func=AF.Copy)
            se.activation(out=kfeat.ap()[:, tt % 2, :], in_=banks[3].ap()[:, 0:512], func=AF.Copy)
            i = se.activation(out=vfeat.ap()[:, tt % 2, :], in_=banks[4].ap()[:, 0:512],
                              func=AF.Copy)
            ctl.inc(i, "s_se")
        t_pe = ctl.tok("s_se")
        prev_projevac = t_pe
        bank_free[2] = t_pe
        bank_free[3] = t_pe
        bank_free[4] = t_pe

        for ci in range(4):
            c = ch0 + ci
            cb2 = c % 2

            @block.tensor
            def _(te, tt=tt, ci=ci, cb2=cb2, t_pe=t_pe):
                ctl.waits(te, [t_pe, toks["eyeb"], bank_free[cb2], prev_tokuse2[cb2]],
                          "tensor")
                kp = banks[cb2].ap().bitcast(BF16)
                te.transpose(out=kp[:, 0:128], in_=kfeat.ap()[:, tt % 2, ci * 128:(ci + 1) * 128],
                             identity=eyeb.ap())
                te.transpose(out=kp[:, 128:256], in_=vfeat.ap()[:, tt % 2, ci * 128:(ci + 1) * 128],
                             identity=eyeb.ap())
                i = te.transpose(out=kp[:, 256:384],
                                 in_=qfeatr.ap()[:, tt % 2, ci * 128:(ci + 1) * 128],
                                 identity=eyeb.ap())
                ctl.inc(i, "s_te")
            t_tokT = ctl.tok("s_te")

            @block.scalar
            def _(se, c=c, tt=tt, cb2=cb2, t_tokT=t_tokT):
                ctl.waits(se, [t_tokT, toks[("rt", tt)], toks["arena2"],
                               prev_tokuse2[cb2]], "scalar")
                kp = banks[cb2].ap().bitcast(BF16)
                rr = statr.ap()[:, c:c + 1]
                se.activation(out=ktok.ap()[:, cb2, :], in_=kp[:, 0:128], func=AF.Copy, scale=rr)
                se.activation(out=vaug.ap()[:, cb2, :, 0:16], in_=kp[:, 128:256],
                              func=AF.Copy, scale=rr)
                se.activation(out=phiaug.ap()[:, cb2, :, 0:16], in_=kp[:, 0:128],
                              func=AF.Copy, scale=rr)
                i = se.activation(out=qtok.ap()[:, c, :], in_=kp[:, 256:384], func=AF.Copy, scale=rr)
                ctl.inc(i, "s_se")
            t_tok = ctl.tok("s_se")
            bank_free[cb2] = t_tok

            @block.vector
            def _(v, cb2=cb2, t_tok=t_tok):
                ctl.wait(v, t_tok, "vector")
                for hh in range(HEADS):
                    ks = ktok.ap()[:, cb2, hh * 16:(hh + 1) * 16]
                    i = v.tensor_tensor(
                        out=phiaug.ap()[:, cb2, hh, 16:272].rearrange("p (i j) -> p i j", i=16),
                        in0=ks.unsqueeze(2).broadcast_to([128, 16, 16]),
                        in1=ks.unsqueeze(1).broadcast_to([128, 16, 16]), op=ALU.mult)
                ctl.inc(i, "s_ve")
            t_phi = ctl.tok("s_ve")

            @block.tensor
            def _(te, cb2=cb2, t_phi=t_phi):
                ctl.waits(te, [t_phi, prev_state_add, toks["memset"]], "tensor")
                for hh in range(HEADS):
                    bk = st_banks[hh // 3]
                    po = 32 * (hh % 3)
                    i = te.matmul(out=banks[bk].ap()[po:po + 17, 0:273],
                                  lhsT=vaug.ap()[:, cb2, hh, :],
                                  rhs=phiaug.ap()[:, cb2, hh, :], start=True, stop=True)
                ctl.inc(i, "s_te")
            t_smm = ctl.tok("s_te")
            prev_tokuse = t_smm
            prev_tokuse2[cb2] = t_smm

            @block.vector
            def _(v, t_smm=t_smm):
                ctl.wait(v, t_smm, "vector")
                for k in range(3):
                    i = v.tensor_tensor(out=stacc.ap()[:, k * 273:(k + 1) * 273],
                                        in0=stacc.ap()[:, k * 273:(k + 1) * 273],
                                        in1=banks[st_banks[k]].ap()[0:113, 0:273], op=ALU.add)
                ctl.inc(i, "s_ve")
            prev_state_add = ctl.tok("s_ve")

        if DEBUG_TAP == "ph2" and tt == 0:
            @block.gpsimd
            def _(g, prev_state_add=prev_state_add):
                ctl.wait(g, prev_state_add)
                ctl.inc_dma(g.dma_start(out=tb_tap[:, 0:1024], in_=hstage.ap()[:, 1, :]), "s_gd")
                ctl.inc_dma(g.dma_start(out=tb_tap[:, 1024:1152], in_=ktok.ap()[:, 1, :]), "s_gd")
                ctl.inc_dma(g.dma_start(out=tb_tap[:, 1152:1425], in_=phiaug.ap()[:, 1, 0, :]), "s_gd")
                ctl.inc_dma(g.dma_start(out=tb_tap[:, 1425:1442], in_=vaug.ap()[:, 1, 0, :]), "s_gd")
                ctl.inc_dma(g.dma_start(out=tb_tap[:, 1442:1446], in_=statr.ap()[:, 0:4]), "s_gd")
                ctl.inc_dma(g.dma_start(out=tb_tap[:, 1446:1450], in_=statrstd.ap()[:, 0:4]), "s_gd")
                ctl.inc_dma(g.dma_start(out=tb_tap[:, 1450:1454], in_=statsum.ap()[:, 0:4]), "s_gd")
                ctl.inc_dma(g.dma_start(out=tb_tap[:, 1454:1458], in_=statssq.ap()[:, 0:4]), "s_gd")
                ctl.inc_dma(g.dma_start(out=tb_tap[:, 2048:2560], in_=kfeat.ap()[:, 0, :]), "s_gd")
                ctl.inc_dma(g.dma_start(out=tb_tap[:, 2560:3072], in_=qfeatr.ap()[:, 0, :]), "s_gd")
                ctl.inc_dma(g.dma_start(out=tb_tap[:, 3072:3584], in_=hfeat.ap()[:, 0, 0, :]), "s_gd")
                ctl.inc_dma(g.dma_start(out=tb_tap[0:113, 3584:4096], in_=stacc.ap()[:, 0:512]), "s_gd")
            t_tap = ctl.tok("s_gd")
            prev_state_add = t_tap
            prev_tokuse = t_tap
            prev_hT_evac = [t_tap, t_tap]
            prev_projevac = t_tap
            prev_hfeat_use = [t_tap, t_tap]

    @block.gpsimd
    def _(g):
        ctl.wait(g, prev_state_add)
        toks["st_o"] = ctl.inc_dma(g.dma_start(out=cc_st_in, in_=stacc.ap()), "s_gd")
        ctl.wait(g, toks["st_o"])
        toks["ar_st"] = ctl.inc(g.collective_compute(
            "AllReduce", ALU.add, replica_groups=[[0, 1, 2, 3], [4, 5, 6, 7]],
            ins=[cc_st_in], outs=[cc_st_out]), "s_cc")
        ctl.wait(g, toks["ar_st"])
        toks["st_i"] = ctl.inc_dma(g.dma_start(out=stacc.ap(), in_=cc_st_out), "s_gd")

    for hh in range(HEADS):
        co = (hh // 3) * 273
        po = 32 * (hh % 3)
        bk = hh % 2
        idm = eyef.ap()[po:po + 17, po:po + 17]

        @block.tensor
        def _(te, hh=hh, co=co, po=po, bk=bk, idm=idm):
            ctl.waits(te, [toks["st_i"], bank_free[bk]], "tensor")
            src = stacc.ap()
            te.transpose(out=banks[bk].ap()[0:16, 0:17], in_=src[po:po + 17, co:co + 16],
                         identity=idm)
            te.transpose(out=banks[bk].ap()[0:128, 32:49], in_=src[po:po + 17, co + 16:co + 144],
                         identity=idm)
            te.transpose(out=banks[bk].ap()[0:128, 64:81], in_=src[po:po + 17, co + 144:co + 272],
                         identity=idm)
            i = te.transpose(out=banks[bk].ap()[0:1, 96:113], in_=src[po:po + 17, co + 272:co + 273],
                             identity=idm)
            ctl.inc(i, "s_te")
        t_sT = ctl.tok("s_te")

        @block.scalar
        def _(se, hh=hh, bk=bk, t_sT=t_sT):
            ctl.wait(se, t_sT, "scalar")
            se.activation(out=stA.ap()[:, hh, 0:1], in_=banks[bk].ap()[0:16, 16:17], func=AF.Copy)
            se.activation(out=stA.ap()[:, hh, 32:48], in_=banks[bk].ap()[0:16, 0:16], func=AF.Copy)
            se.activation(out=stB.ap()[:, hh, 0:1], in_=banks[bk].ap()[0:128, 48:49],
                          func=AF.Copy, scale=0.5)
            se.activation(out=stB.ap()[:, hh, 32:48], in_=banks[bk].ap()[0:128, 32:48],
                          func=AF.Copy, scale=0.5)
            se.activation(out=stC.ap()[:, hh, 0:1], in_=banks[bk].ap()[0:128, 80:81],
                          func=AF.Copy, scale=0.5)
            se.activation(out=stC.ap()[:, hh, 32:48], in_=banks[bk].ap()[0:128, 64:80],
                          func=AF.Copy, scale=0.5)
            se.activation(out=stD.ap()[0:1, hh, 0:1], in_=banks[bk].ap()[0:1, 112:113], func=AF.Copy)
            i = se.activation(out=stD.ap()[0:1, hh, 32:48], in_=banks[bk].ap()[0:1, 96:112],
                              func=AF.Copy)
            ctl.inc(i, "s_se")
        bank_free[bk] = ctl.tok("s_se")
    toks["stprep"] = ctl.tok("s_se")

    @block.vector
    def _(v):
        ctl.wait(v, toks["st_i"], "vector")
        i = v.memset(denr.ap(), 0.0)
        ctl.inc(i, "s_ve")

    prev_ofd_use = None
    prev_od = None
    prev_p2T = None
    for tt in range(TT):
        for hh in range(HEADS):
            tq = 2 + (hh % 2)
            ob = 4 + (hh % 2)

            @block.tensor
            def _(te, tt=tt, hh=hh):
                ctl.waits(te, [toks["stprep"], bank_free[7], prev_ofd_use], "tensor")
                for ci in range(4):
                    c = tt * 4 + ci
                    i = te.transpose(out=banks[7].ap().bitcast(BF16)[0:16, ci * 128:(ci + 1) * 128],
                                     in_=qtok.ap()[:, c, hh * 16:(hh + 1) * 16],
                                     identity=eyeb.ap())
                ctl.inc(i, "s_te")
            t_qfT = ctl.tok("s_te")

            @block.scalar
            def _(se, t_qfT=t_qfT, prev_od=prev_od):
                ctl.waits(se, [t_qfT, prev_od], "scalar")
                i = se.activation(out=qf16.ap(), in_=banks[7].ap().bitcast(BF16)[0:16, 0:512],
                                  func=AF.Copy)
                ctl.inc(i, "s_se")
            t_qf = ctl.tok("s_se")
            bank_free[7] = t_qf

            @block.vector
            def _(v, hh=hh, tt=tt, prev_p2T=prev_p2T):
                ctl.waits(v, [prev_p2T], "vector")
                for ci in range(4):
                    c = tt * 4 + ci
                    qs = qtok.ap()[:, c, hh * 16:(hh + 1) * 16]
                    i = v.tensor_tensor(
                        out=phi2q4.ap()[:, ci, :].rearrange("p (i j) -> p i j", i=16),
                        in0=qs.unsqueeze(2).broadcast_to([128, 16, 16]),
                        in1=qs.unsqueeze(1).broadcast_to([128, 16, 16]), op=ALU.mult)
                ctl.inc(i, "s_ve")
            t_p2 = ctl.tok("s_ve")

            @block.tensor
            def _(te, tq=tq, t_p2=t_p2):
                ctl.waits(te, [t_p2, bank_free[tq]], "tensor")
                bv = banks[tq].ap().bitcast(BF16)
                for ci in range(4):
                    te.transpose(out=bv[:, ci * 128:(ci + 1) * 128],
                                 in_=phi2q4.ap()[:, ci, 0:128], identity=eyeb.ap())
                    i = te.transpose(out=bv[:, 512 + ci * 128:512 + (ci + 1) * 128],
                                     in_=phi2q4.ap()[:, ci, 128:256], identity=eyeb.ap())
                ctl.inc(i, "s_te")
            prev_p2T = ctl.tok("s_te")

            @block.scalar
            def _(se, tq=tq, prev_p2T=prev_p2T, prev_od=prev_od):
                ctl.waits(se, [prev_p2T, prev_od], "scalar")
                bv = banks[tq].ap().bitcast(BF16)
                se.activation(out=phiqA.ap(), in_=bv[:, 0:512], func=AF.Copy)
                i = se.activation(out=phiqB.ap(), in_=bv[:, 512:1024], func=AF.Copy)
                ctl.inc(i, "s_se")
            bank_free[tq] = ctl.tok("s_se")
            t_phiq = ctl.tok("s_se")

            @block.tensor
            def _(te, hh=hh, tt=tt, ob=ob, t_phiq=t_phiq, t_qf=t_qf):
                ctl.waits(te, [t_phiq, t_qf, toks["onesb"], bank_free[ob]], "tensor")
                te.matmul(out=banks[ob].ap()[0:48, 0:512], lhsT=stA.ap()[:, hh, :],
                          rhs=qf16.ap(), start=True, stop=False)
                te.matmul(out=banks[ob].ap()[0:48, 0:512], lhsT=stB.ap()[:, hh, :],
                          rhs=phiqA.ap(), start=False, stop=False)
                te.matmul(out=banks[ob].ap()[0:48, 0:512], lhsT=stC.ap()[:, hh, :],
                          rhs=phiqB.ap(), start=False, stop=False)
                i = te.matmul(out=banks[ob].ap()[0:48, 0:512], lhsT=stD.ap()[:, hh, :],
                              rhs=onesb.ap(), start=False, stop=True)
                ctl.inc(i, "s_te")
            t_od = ctl.tok("s_te")
            prev_od = t_od

            @block.scalar
            def _(se, hh=hh, ob=ob, t_od=t_od):
                ctl.wait(se, t_od, "scalar")
                se.activation(out=denr.ap()[0:1, hh, :], in_=banks[ob].ap()[0:1, 0:512],
                              func=AF.Copy)
                i = se.activation(out=ofd.ap()[32:48, hh, :], in_=banks[ob].ap()[32:48, 0:512],
                                  func=AF.Copy)
                ctl.inc(i, "s_se")
            t_ode = ctl.tok("s_se")
            bank_free[ob] = t_ode
        t_odall = ctl.tok("s_se")

        @block.vector
        def _(v, t_odall=t_odall):
            ctl.wait(v, t_odall, "vector")
            v.tensor_scalar(out=denr2.ap()[0:1, :, :], in0=denr.ap()[0:1, :, :],
                            scalar1=1e-5, scalar2=None, op0=ALU.max)
            vsp(v)
            i = v.reciprocal(out=denr.ap()[0:1, :, :], in_=denr2.ap()[0:1, :, :])
            ctl.inc(i, "s_ve")
        t_den = ctl.tok("s_ve")

        qk_banks = (2, 3, 6, 7)
        for rnd in range(2):
            @block.tensor
            def _(te, rnd=rnd, t_den=t_den):
                ctl.waits(te, [t_den] + [bank_free[b] for b in qk_banks], "tensor")
                for hi in range(4):
                    hh = rnd * 4 + hi
                    for nh in range(2):
                        i = te.matmul(out=banks[qk_banks[hi]].ap()[32:48, nh * 256:(nh + 1) * 256],
                                      lhsT=onesf.ap()[0:8, 0:16],
                                      rhs=denr.ap()[0:8, hh, nh * 256:(nh + 1) * 256],
                                      start=True, stop=True)
                ctl.inc(i, "s_te")
            t_qkr = ctl.tok("s_te")

            @block.vector
            def _(v, rnd=rnd, t_qkr=t_qkr):
                ctl.wait(v, t_qkr, "vector")
                for hi in range(4):
                    hh = rnd * 4 + hi
                    i = v.tensor_tensor(out=ofd.ap()[32:48, hh, :],
                                        in0=ofd.ap()[32:48, hh, :],
                                        in1=banks[qk_banks[hi]].ap()[32:48, 0:512],
                                        op=ALU.mult)
                ctl.inc(i, "s_ve")
            for b in qk_banks:
                bank_free[b] = ctl.tok("s_ve")
        t_qkall = ctl.tok("s_ve")

        for ci in range(4):
            c = tt * 4 + ci
            b0, b1 = (0, 1) if ci % 2 == 0 else (2, 3)

            @block.tensor
            def _(te, c=c, ci=ci, b0=b0, b1=b1, t_qkall=t_qkall):
                ctl.waits(te, [t_qkall, toks["woutb"], bank_free[b0], bank_free[b1]], "tensor")
                for hh in range(HEADS):
                    te.matmul(out=banks[b0].ap()[:, 0:512],
                              lhsT=ofd.ap()[32:48, hh, ci * 128:(ci + 1) * 128],
                              rhs=woutb.ap()[32:48, hh, 0:512],
                              start=(hh == 0), stop=(hh == HEADS - 1))
                for hh in range(HEADS):
                    i = te.matmul(out=banks[b1].ap()[:, 0:512],
                                  lhsT=ofd.ap()[32:48, hh, ci * 128:(ci + 1) * 128],
                                  rhs=woutb.ap()[32:48, hh, 512:1024],
                                  start=(hh == 0), stop=(hh == HEADS - 1))
                ctl.inc(i, "s_te")
            t_op = ctl.tok("s_te")

            @block.vector
            def _(v, c=c, b0=b0, b1=b1, t_op=t_op):
                ctl.wait(v, t_op, "vector")
                v.tensor_tensor(out=xres.ap()[:, c, 0:512], in0=xres.ap()[:, c, 0:512],
                                in1=banks[b0].ap()[:, 0:512], op=ALU.add)
                i = v.tensor_tensor(out=xres.ap()[:, c, 512:1024], in0=xres.ap()[:, c, 512:1024],
                                    in1=banks[b1].ap()[:, 0:512], op=ALU.add)
                ctl.inc(i, "s_ve")
            bank_free[b0] = ctl.tok("s_ve")
            bank_free[b1] = ctl.tok("s_ve")
        prev_ofd_use = ctl.tok("s_te")
    toks["resid1"] = ctl.tok("s_ve")

    @block.gpsimd
    def _(g):
        toks["cb_row"] = ctl.inc_dma(g.dma_start(
            out=mrow.ap()[0:1, :], in_=cc_mod_out[None, 4096:5120]), "s_gd")

    for half in range(2):
        sl = slice(half * 512, (half + 1) * 512)
        ba = 4 if half == 0 else 5

        @block.tensor
        def _(te, sl=sl, ba=ba):
            ctl.waits(te, [toks["cb_row"], toks["resid1"], bank_free[ba]], "tensor")
            for nh in range(2):
                i = te.matmul(out=banks[ba].ap()[:, nh * 256:(nh + 1) * 256], lhsT=onesf.ap(),
                              rhs=mrow.ap()[0:8, sl.start + nh * 256:sl.start + (nh + 1) * 256],
                              start=True, stop=True)
            ctl.inc(i, "s_te")
        t_rep2 = ctl.tok("s_te")

        @block.scalar
        def _(se, sl=sl, ba=ba, t_rep2=t_rep2):
            ctl.wait(se, t_rep2, "scalar")
            i = se.activation(out=modrepA.ap()[:, sl], in_=banks[ba].ap()[:, 0:512],
                              func=AF.Copy, bias=1.0)
            ctl.inc(i, "s_se")
        bank_free[ba] = ctl.tok("s_se")
    t_cb_mm = ctl.tok("s_te")

    @block.gpsimd
    def _(g):
        ctl.wait(g, t_cb_mm)
        toks["sb_row"] = ctl.inc_dma(g.dma_start(
            out=mrow.ap()[0:1, :], in_=cc_mod_out[None, 3072:4096]), "s_gd")

    for half in range(2):
        sl = slice(half * 512, (half + 1) * 512)
        ba = 6 if half == 0 else 7

        @block.tensor
        def _(te, sl=sl, ba=ba):
            ctl.waits(te, [toks["sb_row"], bank_free[ba]], "tensor")
            for nh in range(2):
                i = te.matmul(out=banks[ba].ap()[:, nh * 256:(nh + 1) * 256], lhsT=onesf.ap(),
                              rhs=mrow.ap()[0:8, sl.start + nh * 256:sl.start + (nh + 1) * 256],
                              start=True, stop=True)
            ctl.inc(i, "s_te")
        t_rep2 = ctl.tok("s_te")

        @block.scalar
        def _(se, sl=sl, ba=ba, t_rep2=t_rep2):
            ctl.wait(se, t_rep2, "scalar")
            i = se.activation(out=modrepB.ap()[:, sl], in_=banks[ba].ap()[:, 0:512], func=AF.Copy)
            ctl.inc(i, "s_se")
        bank_free[ba] = ctl.tok("s_se")
    toks["modrep2"] = ctl.tok("s_se")

    prev_spill = [None, None]
    for c in range(TCH):
        @block.scalar
        def _(se, c=c):
            ctl.waits(se, [toks["resid1"], toks["modrep2"]], "scalar")
            se.activation(out=scratch.ap(), in_=xres.ap()[:, c, :], func=AF.Square,
                          accum_out=statssq.ap()[:, c:c + 1])
            i = se.activation(out=scratch.ap(), in_=xres.ap()[:, c, :], func=AF.Copy,
                              accum_out=statsum.ap()[:, c:c + 1])
            ctl.inc(i, "s_se")
        t_s2 = ctl.tok("s_se")

        @block.vector
        def _(v, c=c, t_s2=t_s2):
            ctl.wait(v, t_s2, "vector")
            sl = slice(c, c + 1)
            v.tensor_scalar(out=statmu.ap()[:, sl], in0=statsum.ap()[:, sl],
                            scalar1=1.0 / D, scalar2=None, op0=ALU.mult)
            vsp(v)
            v.tensor_scalar(out=statB.ap()[:, sl], in0=statmu.ap()[:, sl],
                            scalar1=statmu.ap()[:, sl], scalar2=None, op0=ALU.mult)
            vsp(v)
            v.scalar_tensor_tensor(out=statB.ap()[:, sl], in0=statssq.ap()[:, sl],
                                   scalar=1.0 / D, in1=statB.ap()[:, sl],
                                   op0=ALU.mult, op1=ALU.subtract)
            vsp(v)
            v.tensor_scalar(out=statB.ap()[:, sl], in0=statB.ap()[:, sl],
                            scalar1=1e-5, scalar2=None, op0=ALU.add)
            vsp(v)
            i = v.reciprocal(out=statrstd.ap()[:, sl], in_=statB.ap()[:, sl])
            ctl.inc(i, "s_ve")
        t_c2 = ctl.tok("s_ve")

        @block.scalar
        def _(se, c=c, t_c2=t_c2):
            ctl.wait(se, t_c2, "scalar")
            i = se.activation(out=statrstd.ap()[:, c:c + 1], in_=statrstd.ap()[:, c:c + 1],
                              func=AF.Sqrt)
            ctl.inc(i, "s_se")
        t_r2 = ctl.tok("s_se")

        @block.vector
        def _(v, c=c, t_r2=t_r2):
            ctl.wait(v, t_r2, "vector")
            sl = slice(c, c + 1)
            i = v.scalar_tensor_tensor(out=statB.ap()[:, sl], in0=statmu.ap()[:, sl],
                                       scalar=-1.0, in1=statrstd.ap()[:, sl],
                                       op0=ALU.mult, op1=ALU.mult)
            ctl.inc(i, "s_ve")
        t_B2 = ctl.tok("s_ve")

        if True:
            @block.vector
            def _(v, c=c, t_B2=t_B2):
                ctl.waits(v, [t_B2, prev_spill[0]], "vector")
                hv = hq.ap()
                vsp(v)
                v.tensor_scalar(out=hv, in0=xres.ap()[:, c, :],
                                scalar1=statrstd.ap()[:, c:c + 1],
                                scalar2=statB.ap()[:, c:c + 1],
                                op0=ALU.mult, op1=ALU.add)
                vsp(v)
                v.tensor_tensor(out=hv, in0=hv, in1=modrepA.ap(), op=ALU.mult)
                vsp(v)
                v.tensor_tensor(out=hv, in0=hv, in1=modrepB.ap(), op=ALU.add)
                vsp(v)
                v.tensor_scalar(out=scratch.ap(), in0=hv, scalar1=0.0, scalar2=None,
                                op0=ALU.bypass, op1=ALU.max, accum_out=maxs.ap()[:, c:c + 1])
                i = v.tensor_scalar(out=scratch.ap(), in0=hv, scalar1=0.0, scalar2=None,
                                    op0=ALU.bypass, op1=ALU.min, accum_out=mins.ap()[:, c:c + 1])
                ctl.inc(i, "s_ve")
            t_h2 = ctl.tok("s_ve")

            @block.sync
            def _(sync, c=c, t_h2=t_h2):
                ctl.wait(sync, t_h2)
                toks[("spill", c)] = ctl.inc_dma(sync.dma_start(
                    out=h_spill[c, :, :], in_=hq.ap()), "s_ld")
            prev_spill[0] = toks[("spill", c)]
    t_ln2 = ctl.tok("s_ve")

    @block.vector
    def _(v):
        ctl.wait(v, t_ln2)
        vsp(v, 4)
        v.tensor_reduce(out=hmaxcol.ap()[:, 0:1], in_=maxs.ap(), axis=AX.X, op=ALU.max)
        i = v.tensor_reduce(out=hmaxcol.ap()[:, 1:2], in_=mins.ap(), axis=AX.X, op=ALU.min)
        ctl.inc(i, "s_ve")
    t_gmr = ctl.tok("s_ve")

    @block.scalar
    def _(se):
        ctl.wait(se, t_gmr, "scalar")
        i = se.activation(out=hmaxcol.ap()[:, 1:2], in_=hmaxcol.ap()[:, 1:2],
                          func=AF.Copy, scale=-1.0)
        ctl.inc(i, "s_se")
    t_gmse = ctl.tok("s_se")

    @block.vector
    def _(v):
        ctl.wait(v, t_gmse, "vector")
        i = v.tensor_tensor(out=hmaxcol.ap()[:, 0:1], in0=hmaxcol.ap()[:, 0:1],
                            in1=hmaxcol.ap()[:, 1:2], op=ALU.max)
        ctl.inc(i, "s_ve")
    t_gm = ctl.tok("s_ve")

    @block.tensor
    def _(te):
        ctl.waits(te, [t_gm, bank_free[6]], "tensor")
        i = te.transpose(out=banks[6].ap()[0:1, 0:128], in_=hmaxcol.ap()[:, 0:1],
                         identity=eyef.ap())
        ctl.inc(i, "s_te")
    t_gmT = ctl.tok("s_te")

    @block.vector
    def _(v):
        ctl.wait(v, t_gmT, "vector")
        i = v.tensor_reduce(out=cells.ap()[0:1, 4:5], in_=banks[6].ap()[0:1, 0:128],
                            axis=AX.X, op=ALU.max)
        ctl.inc(i, "s_ve")
    t_gmax = ctl.tok("s_ve")
    bank_free[6] = t_gmax

    @block.gpsimd
    def _(g):
        ctl.wait(g, t_gmax)
        toks["m1o"] = ctl.inc_dma(g.dma_start(out=cc_m1_in[None, :], in_=cells.ap()[0:1, 4:5]), "s_gd")
        ctl.wait(g, toks["m1o"])
        toks["ar_m1"] = ctl.inc(g.collective_compute(
            "AllReduce", ALU.max, replica_groups=[list(range(8))],
            ins=[cc_m1_in], outs=[cc_m1_out]), "s_cc")
        ctl.wait(g, toks["ar_m1"])
        toks["m1i"] = ctl.inc_dma(g.dma_start(out=cells.ap()[0:1, 5:6], in_=cc_m1_out[None, :]), "s_gd")

    @block.vector
    def _(v):
        ctl.wait(v, toks["m1i"], "vector")
        v.scalar_tensor_tensor(out=cells.ap()[0:1, 6:7], in0=cells.ap()[0:1, 2:3],
                               scalar=1.0 / 127.0, in1=cells.ap()[0:1, 5:6],
                               op0=ALU.mult, op1=ALU.mult)
        vsp(v)
        i = v.reciprocal(out=cells.ap()[0:1, 7:8], in_=cells.ap()[0:1, 6:7])
        ctl.inc(i, "s_ve")
    t_s1c = ctl.tok("s_ve")

    @block.tensor
    def _(te):
        ctl.waits(te, [t_s1c, bank_free[7]], "tensor")
        te.matmul(out=banks[7].ap()[:, 0:1], lhsT=onesf.ap(), rhs=cells.ap()[0:8, 7:8],
                  start=True, stop=True)
        i = te.matmul(out=banks[7].ap()[:, 1:2], lhsT=onesf.ap(), rhs=cells.ap()[0:8, 6:7],
                      start=True, stop=True)
        ctl.inc(i, "s_te")
    t_bc1 = ctl.tok("s_te")

    @block.scalar
    def _(se):
        ctl.wait(se, t_bc1, "scalar")
        i = se.activation(out=gcol.ap()[:, 0:2], in_=banks[7].ap()[:, 0:2], func=AF.Copy)
        ctl.inc(i, "s_se")
    toks["rs1"] = ctl.tok("s_se")
    bank_free[7] = toks["rs1"]

    @block.gpsimd
    def _(g):
        ctl.wait(g, toks["ag_sg"])
        toks["sgrows"] = ctl.inc_dma(g.dma_start(
            out=scrows.ap(), in_=cc_sg_out.rearrange("(a b) -> a b", a=22)), "s_gd")

    @block.tensor
    def _(te):
        ctl.waits(te, [toks["sgrows"], bank_free[6]], "tensor")
        i = te.transpose(out=banks[6].ap()[:, 0:22], in_=scrows.ap(),
                         identity=eyef.ap()[0:22, 0:22])
        ctl.inc(i, "s_te")
    t_sgT = ctl.tok("s_te")

    @block.scalar
    def _(se):
        ctl.wait(se, t_sgT, "scalar")
        i = se.activation(out=sgcol.ap(), in_=banks[6].ap()[:, 0:22], func=AF.Copy)
        ctl.inc(i, "s_se")
    t_sgc = ctl.tok("s_se")
    bank_free[6] = t_sgc

    @block.gpsimd
    def _(g):
        ctl.waits(g, [toks["ag_su"], t_sgc])
        toks["surows"] = ctl.inc_dma(g.dma_start(
            out=scrows.ap(), in_=cc_su_out.rearrange("(a b) -> a b", a=22)), "s_gd")

    @block.tensor
    def _(te):
        ctl.waits(te, [toks["surows"], bank_free[6]], "tensor")
        i = te.transpose(out=banks[6].ap()[:, 0:22], in_=scrows.ap(),
                         identity=eyef.ap()[0:22, 0:22])
        ctl.inc(i, "s_te")
    t_suT = ctl.tok("s_te")

    @block.scalar
    def _(se):
        ctl.wait(se, t_suT, "scalar")
        i = se.activation(out=sucol.ap(), in_=banks[6].ap()[:, 0:22], func=AF.Copy)
        ctl.inc(i, "s_se")
    t_suc = ctl.tok("s_se")
    bank_free[6] = t_suc

    @block.gpsimd
    def _(g):
        ctl.waits(g, [toks["ag_sd"], t_suc])
        toks["sdrow"] = ctl.inc_dma(g.dma_start(out=sdrow.ap()[0:1, :], in_=cc_sd_out[None, :]), "s_gd")

    @block.vector
    def _(v):
        ctl.waits(v, [t_sgc, t_suc, toks["rs1"]], "vector")
        v.tensor_scalar(out=sgxcol.ap(), in0=sgcol.ap(), scalar1=gcol.ap()[:, 1:2],
                        scalar2=None, op0=ALU.mult)
        i = v.tensor_scalar(out=suxcol.ap(), in0=sucol.ap(), scalar1=gcol.ap()[:, 1:2],
                            scalar2=None, op0=ALU.mult)
        ctl.inc(i, "s_ve")
    toks["sgx"] = ctl.tok("s_ve")

    final_spill = toks[("spill", TCH - 1)]
    prev_stt = None
    prev_xT = None
    for c in range(TCH):
        @block.sync
        def _(sync, c=c, prev_stt=prev_stt):
            ctl.waits(sync, [toks[("spill", c)], final_spill, prev_stt])
            toks[("rld", c)] = ctl.inc_dma(sync.dma_start(
                out=hq.ap(), in_=h_spill[c, :, :]), "s_ld")

        @block.scalar
        def _(se, c=c):
            ctl.waits(se, [toks[("rld", c)], toks["rs1"]], "scalar")
            i = se.activation(out=xqf.ap(), in_=hq.ap(), func=AF.Copy,
                              scale=gcol.ap()[:, 0:1])
            ctl.inc(i, "s_se")
        t_uq = ctl.tok("s_se")

        if QUANT_MODE == "cast":
            @block.vector
            def _(v, t_uq=t_uq, prev_xT=prev_xT):
                ctl.waits(v, [t_uq, prev_xT], "vector")
                v.tensor_copy(out=xq8.ap(), in_=xqf.ap())
                vsp(v)
                v.tensor_copy(out=nxq.ap(), in_=xq8.ap())
                i = vflush(v)
                ctl.inc(i, "s_ve")
        else:
            @block.vector
            def _(v, t_uq=t_uq, prev_xT=prev_xT):
                ctl.waits(v, [t_uq, prev_xT], "vector")
                v.tensor_copy(out=xq16.ap(), in_=xqf.ap())
                vsp(v)
                v.tensor_copy(out=hq.ap(), in_=xq16.ap())
                vsp(v)
                v.tensor_tensor(out=scratch.ap(), in0=xqf.ap(), in1=hq.ap(),
                                op=ALU.subtract)
                vsp(v)
                v.tensor_scalar(out=xqf.ap(), in0=scratch.ap(), scalar1=0.5, scalar2=None,
                                op0=ALU.is_ge)
                v.tensor_scalar(out=scratch.ap(), in0=scratch.ap(), scalar1=-0.5,
                                scalar2=-1.0, op0=ALU.is_ge, op1=ALU.add)
                vsp(v)
                v.tensor_tensor(out=xqf.ap(), in0=xqf.ap(), in1=scratch.ap(), op=ALU.add)
                vsp(v)
                v.tensor_tensor(out=hq.ap(), in0=hq.ap(), in1=xqf.ap(), op=ALU.add)
                vsp(v)
                v.tensor_scalar(out=nxq.ap(), in0=hq.ap(), scalar1=-128.0, scalar2=127.0,
                                op0=ALU.max, op1=ALU.min)
                i = vflush(v)
                ctl.inc(i, "s_ve")
        t_nx = ctl.tok("s_ve")
        prev_stt = t_nx

        @block.tensor
        def _(te, c=c, t_nx=t_nx):
            ctl.waits(te, [t_nx, bank_free[0], bank_free[1]], "tensor")
            for dch in range(DCH):
                bk = 0 if dch < 4 else 1
                bv = banks[bk].ap().bitcast(BF16)
                i = te.transpose(out=bv[:, (dch % 4) * 128:(dch % 4 + 1) * 128],
                                 in_=nxq.ap()[:, dch * 128:(dch + 1) * 128], identity=eyeb.ap())
            ctl.inc(i, "s_te")
        t_xT = ctl.tok("s_te")
        prev_xT = t_xT

        @block.scalar
        def _(se, c=c, t_xT=t_xT):
            ctl.wait(se, t_xT, "scalar")
            for dch in range(DCH):
                bk = 0 if dch < 4 else 1
                bv = banks[bk].ap().bitcast(BF16)
                i = se.activation(out=xqfeat.ap()[:, dch, c * 128:(c + 1) * 128],
                                  in_=bv[:, (dch % 4) * 128:(dch % 4 + 1) * 128], func=AF.Copy)
            ctl.inc(i, "s_se")
        bank_free[0] = ctl.tok("s_se")
        bank_free[1] = ctl.tok("s_se")
    toks["xqfeat"] = ctl.tok("s_se")

    sblocks = [(s * 4, min(ICH, s * 4 + 4)) for s in range(6)]
    gu_banks = [(2, 3), (4, 5), (6, 7)]
    prev_h2spill = [None, None]
    prev_ttr = None
    wbuf_free = [None, None]
    sb_ld = {}
    for sbi, (i0, i1) in enumerate(sblocks):
        wb = sbi % 2
        ncols = (i1 - i0) * 128

        @block.sync
        def _(sync, sbi=sbi, i0=i0, i1=i1, wb=wb, ncols=ncols):
            ctl.waits(sync, [toks["ag_wg"], toks["ag_wu"], wbuf_free[wb]])
            gi0, gi1 = i0 * 128, i1 * 128
            for c in range(8):
                s = max(gi0, c * 352)
                e = min(gi1, (c + 1) * 352)
                if s >= e:
                    continue
                for buf, cc_o in ((wgs, cc_wg_out), (wus, cc_wu_out)):
                    ctl.inc_dma(sync.dma_start(
                        out=buf.ap()[:, wb, :, s - gi0:e - gi0],
                        in_=cc_o[c * D:(c + 1) * D, s - c * 352:e - c * 352].rearrange(
                            "(dch p) w -> p dch w", p=128)), "s_ld")
            sb_ld[sbi] = ctl.tok("s_ld")

        for ii in range(i0, i1):
            io = (ii - i0) * 128
            for tt in range(TT):
                gb, ub = gu_banks[(ii * TT + tt) % 3]

                @block.tensor
                def _(te, sbi=sbi, io=io, tt=tt, gb=gb, ub=ub, wb=wb):
                    ctl.waits(te, [sb_ld[sbi], toks["xqfeat"], bank_free[gb], bank_free[ub]],
                              "tensor")
                    for dch in range(DCH):
                        te.matmul(out=banks[gb].ap()[:, 0:512],
                                  lhsT=wgs.ap()[:, wb, dch, io:io + 128],
                                  rhs=xqfeat.ap()[:, dch, tt * 512:(tt + 1) * 512],
                                  start=(dch == 0), stop=(dch == DCH - 1))
                    for dch in range(DCH):
                        i = te.matmul(out=banks[ub].ap()[:, 0:512],
                                      lhsT=wus.ap()[:, wb, dch, io:io + 128],
                                      rhs=xqfeat.ap()[:, dch, tt * 512:(tt + 1) * 512],
                                      start=(dch == 0), stop=(dch == DCH - 1))
                    ctl.inc(i, "s_te")
                t_gu = ctl.tok("s_te")

                @block.scalar
                def _(se, ii=ii, gb=gb, t_gu=t_gu, prev_ttr=prev_ttr):
                    ctl.waits(se, [t_gu, toks["sgx"], prev_ttr], "scalar")
                    i = se.activation(out=silut.ap(), in_=banks[gb].ap()[:, 0:512],
                                      func=AF.Silu, scale=sgxcol.ap()[:, ii:ii + 1])
                    ctl.inc(i, "s_se")
                t_si = ctl.tok("s_se")
                bank_free[gb] = t_si

                @block.vector
                def _(v, ii=ii, tt=tt, ub=ub, t_gu=t_gu, t_si=t_si):
                    ctl.waits(v, [t_gu, t_si, prev_h2spill[0]], "vector")
                    v.tensor_scalar(out=upt.ap(), in0=banks[ub].ap()[:, 0:512],
                                    scalar1=suxcol.ap()[:, ii:ii + 1], scalar2=None,
                                    op0=ALU.mult)
                    vsp(v)
                    v.tensor_tensor(out=h2st.ap(), in0=silut.ap(), in1=upt.ap(),
                                    op=ALU.mult)
                    vsp(v)
                    v.tensor_scalar(out=scratch.ap()[:, 0:512], in0=h2st.ap(),
                                    scalar1=0.0, scalar2=None, op0=ALU.bypass, op1=ALU.max,
                                    accum_out=statmu.ap()[:, 0:1])
                    v.tensor_scalar(out=scratch.ap()[:, 0:512],
                                    in0=h2st.ap(),
                                    scalar1=0.0, scalar2=None, op0=ALU.bypass, op1=ALU.min,
                                    accum_out=statmu.ap()[:, 1:2])
                    vsp(v)
                    v.tensor_tensor(out=hmaxcol.ap()[:, 0:1], in0=hmaxcol.ap()[:, 0:1],
                                    in1=statmu.ap()[:, 0:1], op=ALU.max)
                    v.tensor_scalar(out=statmu.ap()[:, 1:2], in0=statmu.ap()[:, 1:2],
                                    scalar1=-1.0, scalar2=None, op0=ALU.mult)
                    vsp(v)
                    i = v.tensor_tensor(out=hmaxcol.ap()[:, 1:2], in0=hmaxcol.ap()[:, 1:2],
                                        in1=statmu.ap()[:, 1:2], op=ALU.max)
                    ctl.inc(i, "s_ve")
                prev_ttr = ctl.tok("s_ve")
                t_q = ctl.tok("s_ve")
                bank_free[ub] = t_q

                @block.sync
                def _(sync, ii=ii, tt=tt, t_q=t_q):
                    ctl.wait(sync, t_q)
                    toks[("h2sp", ii, tt)] = ctl.inc_dma(sync.dma_start(
                        out=h2_spill[ii, :, tt * 512:(tt + 1) * 512],
                        in_=h2st.ap()), "s_ld")
                prev_h2spill[0] = toks[("h2sp", ii, tt)]
            toks[("h2sp", ii)] = toks[("h2sp", ii, TT - 1)]
        wbuf_free[wb] = ctl.tok("s_te")
    toks["gu_done"] = ctl.tok("s_te")

    @block.vector
    def _(v):
        ctl.wait(v, ctl.tok("s_ve"))
        vsp(v, 4)
        i = v.tensor_tensor(out=hmaxcol.ap()[:, 0:1], in0=hmaxcol.ap()[:, 0:1],
                            in1=hmaxcol.ap()[:, 1:2], op=ALU.max)
        ctl.inc(i, "s_ve")
    t_gm2 = ctl.tok("s_ve")

    @block.tensor
    def _(te):
        ctl.waits(te, [t_gm2, bank_free[0]], "tensor")
        i = te.transpose(out=banks[0].ap()[0:1, 0:128], in_=hmaxcol.ap()[:, 0:1],
                         identity=eyef.ap())
        ctl.inc(i, "s_te")
    t_gm2T = ctl.tok("s_te")

    @block.vector
    def _(v):
        ctl.wait(v, t_gm2T, "vector")
        i = v.tensor_reduce(out=cells.ap()[0:1, 8:9], in_=banks[0].ap()[0:1, 0:128],
                            axis=AX.X, op=ALU.max)
        ctl.inc(i, "s_ve")
    t_gmax2 = ctl.tok("s_ve")
    bank_free[0] = t_gmax2

    @block.gpsimd
    def _(g):
        ctl.wait(g, t_gmax2)
        toks["m2o"] = ctl.inc_dma(g.dma_start(out=cc_m2_in[None, :], in_=cells.ap()[0:1, 8:9]), "s_gd")
        ctl.wait(g, toks["m2o"])
        toks["ar_m2"] = ctl.inc(g.collective_compute(
            "AllReduce", ALU.max, replica_groups=[list(range(8))],
            ins=[cc_m2_in], outs=[cc_m2_out]), "s_cc")
        ctl.wait(g, toks["ar_m2"])
        toks["m2i"] = ctl.inc_dma(g.dma_start(out=cells.ap()[0:1, 9:10], in_=cc_m2_out[None, :]), "s_gd")

    @block.vector
    def _(v):
        ctl.wait(v, toks["m2i"], "vector")
        v.scalar_tensor_tensor(out=cells.ap()[0:1, 10:11], in0=cells.ap()[0:1, 3:4],
                               scalar=1.0 / 127.0, in1=cells.ap()[0:1, 9:10],
                               op0=ALU.mult, op1=ALU.mult)
        vsp(v)
        i = v.reciprocal(out=cells.ap()[0:1, 11:12], in_=cells.ap()[0:1, 10:11])
        ctl.inc(i, "s_ve")
    t_s2c = ctl.tok("s_ve")

    @block.tensor
    def _(te):
        ctl.waits(te, [t_s2c, bank_free[1]], "tensor")
        i = te.matmul(out=banks[1].ap()[:, 0:1], lhsT=onesf.ap(), rhs=cells.ap()[0:8, 11:12],
                      start=True, stop=True)
        ctl.inc(i, "s_te")
    t_bc2 = ctl.tok("s_te")

    @block.scalar
    def _(se):
        ctl.wait(se, t_bc2, "scalar")
        i = se.activation(out=gcol.ap()[:, 2:3], in_=banks[1].ap()[:, 0:1], func=AF.Copy)
        ctl.inc(i, "s_se")
    toks["rs2"] = ctl.tok("s_se")
    bank_free[1] = toks["rs2"]

    @block.gpsimd
    def _(g):
        toks["gb_row"] = ctl.inc_dma(g.dma_start(
            out=garow.ap()[0:1, :], in_=cc_mod_out[None, 5120:6144]), "s_gd")

    @block.scalar
    def _(se):
        ctl.wait(se, toks["gb_row"], "scalar")
        se.activation(out=scratch.ap()[0:1, 0:D], in_=garow.ap()[0:1, :], func=AF.Sign)
        i = se.activation(out=garow.ap()[0:1, :], in_=garow.ap()[0:1, :], func=AF.Abs)
        ctl.inc(i, "s_se")
    t_gbc = ctl.tok("s_se")

    @block.vector
    def _(v):
        ctl.waits(v, [t_gbc, toks["sdrow"], t_s2c], "vector")
        v.tensor_scalar(out=garow.ap()[0:1, :], in0=garow.ap()[0:1, :],
                        scalar1=0.25, scalar2=None, op0=ALU.max)
        vsp(v)
        v.tensor_tensor(out=garow.ap()[0:1, :], in0=garow.ap()[0:1, :],
                        in1=scratch.ap()[0:1, 0:D], op=ALU.mult)
        vsp(v)
        v.tensor_tensor(out=sdrow.ap()[0:1, :], in0=sdrow.ap()[0:1, :],
                        in1=garow.ap()[0:1, :], op=ALU.mult)
        vsp(v)
        i = v.tensor_scalar(out=sdrow.ap()[0:1, :], in0=sdrow.ap()[0:1, :],
                            scalar1=cells.ap()[0:1, 10:11], scalar2=None, op0=ALU.mult)
        ctl.inc(i, "s_ve")
    t_gbsd = ctl.tok("s_ve")

    t_gb_mm = []
    @block.tensor
    def _(te):
        ctl.waits(te, [t_gbsd, bank_free[2], bank_free[3]], "tensor")
        for nh in range(2):
            ns = slice(nh * 256, (nh + 1) * 256)
            te.matmul(out=banks[2].ap()[:, ns], lhsT=onesf.ap(),
                      rhs=sdrow.ap()[0:8, nh * 256:(nh + 1) * 256], start=True, stop=True)
            i = te.matmul(out=banks[3].ap()[:, ns], lhsT=onesf.ap(),
                          rhs=sdrow.ap()[0:8, 512 + nh * 256:512 + (nh + 1) * 256],
                          start=True, stop=True)
        ctl.inc(i, "s_te")
    t_gr = ctl.tok("s_te")

    @block.scalar
    def _(se):
        ctl.wait(se, t_gr, "scalar")
        se.activation(out=modrepA.ap()[:, 0:512], in_=banks[2].ap()[:, 0:512], func=AF.Copy)
        i = se.activation(out=modrepA.ap()[:, 512:1024], in_=banks[3].ap()[:, 0:512], func=AF.Copy)
        ctl.inc(i, "s_se")
    toks["gbsd"] = ctl.tok("s_se")
    bank_free[2] = toks["gbsd"]
    bank_free[3] = toks["gbsd"]

    prev_wd = [toks["gu_done"], toks["gu_done"]]
    prev_rld2 = [toks["gu_done"], toks["gu_done"]]
    for q in range(TT):
        for ii in range(ICH):
            rb = ii % 2
            wb = ii % 2

            @block.sync
            def _(sync, q=q, ii=ii, wb=wb):
                ctl.waits(sync, [toks["ag_wd"], prev_wd[wb]])
                toks[("wdld", q, ii)] = ctl.inc_dma(sync.dma_start(
                    out=wdTs.ap()[:, wb, :].rearrange("p (c d) -> p c d", d=128),
                    in_=cc_wd_out.rearrange("(c r) d -> r c d", r=II)[
                        ii * 128:(ii + 1) * 128, :, :]), "s_ld")

            @block.scalar
            def _(se, q=q, ii=ii, rb=rb):
                ctl.waits(se, [toks[("h2sp", ii, q)], prev_rld2[rb]], "scalar")
                t_r = ctl.inc_dma(se.dma_start(
                    out=rld.ap()[:, rb, :], in_=h2_spill[ii, :, q * 512:(q + 1) * 512]), "s_sc")
                ctl.wait(se, t_r, "scalar")
                ctl.wait(se, toks["rs2"], "scalar")
                i = se.activation(out=x2f.ap()[:, rb, :], in_=rld.ap()[:, rb, :],
                                  func=AF.Copy, scale=gcol.ap()[:, 2:3])
                ctl.inc(i, "s_se")
            t_u2q = ctl.tok("s_se")

            @block.vector
            def _(v, rb=rb, t_u2q=t_u2q, prev_mm=prev_wd[rb]):
                ctl.waits(v, [t_u2q, prev_mm], "vector")
                if QUANT_MODE == "cast":
                    v.tensor_copy(out=x28.ap()[:, rb, :], in_=x2f.ap()[:, rb, :])
                    vsp(v)
                    v.tensor_copy(out=nxq2.ap()[:, rb, :], in_=x28.ap()[:, rb, :])
                    i = vflush(v)
                else:
                    v.tensor_copy(out=x216.ap()[:, rb, :], in_=x2f.ap()[:, rb, :])
                    vsp(v)
                    v.tensor_copy(out=rld.ap()[:, rb, :], in_=x216.ap()[:, rb, :])
                    vsp(v)
                    v.tensor_tensor(out=u2.ap()[:, rb, :], in0=x2f.ap()[:, rb, :],
                                    in1=rld.ap()[:, rb, :], op=ALU.subtract)
                    vsp(v)
                    v.tensor_scalar(out=x2g.ap()[:, rb, :], in0=u2.ap()[:, rb, :],
                                    scalar1=0.5, scalar2=None, op0=ALU.is_ge)
                    v.tensor_scalar(out=u2.ap()[:, rb, :], in0=u2.ap()[:, rb, :],
                                    scalar1=-0.5, scalar2=-1.0, op0=ALU.is_ge, op1=ALU.add)
                    vsp(v)
                    v.tensor_tensor(out=x2g.ap()[:, rb, :], in0=x2g.ap()[:, rb, :],
                                    in1=u2.ap()[:, rb, :], op=ALU.add)
                    vsp(v)
                    v.tensor_tensor(out=rld.ap()[:, rb, :], in0=rld.ap()[:, rb, :],
                                    in1=x2g.ap()[:, rb, :], op=ALU.add)
                    vsp(v)
                    v.tensor_scalar(out=nxq2.ap()[:, rb, :], in0=rld.ap()[:, rb, :],
                                    scalar1=-128.0, scalar2=127.0, op0=ALU.max, op1=ALU.min)
                    i = vflush(v)
                ctl.inc(i, "s_ve")
            t_nx2 = ctl.tok("s_ve")
            prev_rld2[rb] = t_nx2

            @block.tensor
            def _(te, q=q, ii=ii, rb=rb, wb=wb, t_nx2=t_nx2):
                ctl.waits(te, [t_nx2, toks[("wdld", q, ii)]], "tensor")
                if ii == 0:
                    for b in range(8):
                        ctl.wait(te, bank_free[b], "tensor")
                for tc in range(4):
                    for hv in range(2):
                        i = te.matmul(out=banks[tc * 2 + hv].ap()[:, 0:512],
                                      lhsT=nxq2.ap()[:, rb, tc * 128:(tc + 1) * 128],
                                      rhs=wdTs.ap()[:, wb, hv * 512:(hv + 1) * 512],
                                      start=(ii == 0), stop=(ii == ICH - 1))
                ctl.inc(i, "s_te")
            prev_wd[wb] = ctl.tok("s_te")
        t_down = ctl.tok("s_te")

        for tc in range(4):
            c = q * 4 + tc

            @block.vector
            def _(v, tc=tc, c=c, t_down=t_down):
                ctl.waits(v, [t_down, toks["gbsd"]], "vector")
                for hv in range(2):
                    sl = slice(hv * 512, (hv + 1) * 512)
                    v.tensor_tensor(out=scratch.ap()[:, sl], in0=banks[tc * 2 + hv].ap()[:, 0:512],
                                    in1=modrepA.ap()[:, sl], op=ALU.mult)
                vsp(v, 1)
                for hv in range(2):
                    sl = slice(hv * 512, (hv + 1) * 512)
                    i = v.tensor_tensor(out=xres.ap()[:, c, sl], in0=xres.ap()[:, c, sl],
                                        in1=scratch.ap()[:, sl], op=ALU.add)
                ctl.inc(i, "s_ve")
            t_res = ctl.tok("s_ve")
            bank_free[tc * 2] = t_res
            bank_free[tc * 2 + 1] = t_res

            @block.gpsimd
            def _(g, c=c, t_res=t_res):
                ctl.wait(g, t_res)
                ctl.inc_dma(g.dma_start(
                    out=y_out.rearrange("(c p) d -> p c d", p=128)[:, c, :],
                    in_=xres.ap()[:, c, :]), "s_gd")

    if DEBUG_TAP == "h":
        @block.gpsimd
        def _(g):
            g.wait_ge(ctl.sems["s_gd"], ctl.cnt["s_gd"])
            ctl.wait(g, ctl.tok("s_ld"))
            ctl.inc_dma(g.dma_start(
                out=y_out.rearrange("(c p) d -> c p d", p=128), in_=h_spill), "s_gd")
    elif DEBUG_TAP == "mod":
        @block.gpsimd
        def _(g):
            g.wait_ge(ctl.sems["s_gd"], ctl.cnt["s_gd"])
            ctl.wait(g, ctl.tok("s_ld"))
            ctl.inc_dma(g.dma_start(
                out=y_out[0:6, :], in_=cc_mod_out.rearrange("(v d) -> v d", v=6)), "s_gd")
    elif DEBUG_TAP == "qtok":
        @block.gpsimd
        def _(g):
            g.wait_ge(ctl.sems["s_gd"], ctl.cnt["s_gd"])
            ctl.wait(g, ctl.tok("s_ld"))
            ctl.inc_dma(g.dma_start(
                out=y_out.rearrange("(c p) d -> p c d", p=128)[:, 0:1, :],
                in_=qtok.ap().bitcast(F32)[:, 0:16, :].rearrange("p c d -> p (c d)")[:, 0:1024][:, None, :]), "s_gd")
    elif DEBUG_TAP == "prep":
        @block.gpsimd
        def _(g):
            g.wait_ge(ctl.sems["s_gd"], ctl.cnt["s_gd"])
            ctl.wait(g, ctl.tok("s_ld"))
            ctl.inc_dma(g.dma_start(out=y_out[0:128, 0:16], in_=gamcol.ap()), "s_gd")
            ctl.inc_dma(g.dma_start(out=y_out[128:256, 0:4],
                                    in_=siluct.ap().bitcast(F32)[:, 0:4]), "s_gd")
            ctl.inc_dma(g.dma_start(out=y_out[256:384, 0:64],
                                    in_=wqb.ap().bitcast(F32)[:, 0, 0:64]), "s_gd")
            ctl.inc_dma(g.dma_start(out=y_out[384:512, 0:1024], in_=modrepA.ap()), "s_gd")
            ctl.inc_dma(g.dma_start(out=y_out[512:640, 0:1024], in_=modrepB.ap()), "s_gd")
            ctl.inc_dma(g.dma_start(out=y_out[640:656, 0:512],
                                    in_=woutb.ap().bitcast(F32)[32:48 if False else slice(32, 48), :, 0:512].rearrange("p h d -> p (h d)")[:, 0:1024] if False else woutb.ap().bitcast(F32)[32:48, 0, 0:512]), "s_gd")
    elif DEBUG_TAP == "stats":
        @block.gpsimd
        def _(g):
            g.wait_ge(ctl.sems["s_gd"], ctl.cnt["s_gd"])
            ctl.wait(g, ctl.tok("s_ld"))
            ctl.inc_dma(g.dma_start(out=y_out[0:128, 0:TCH], in_=statsum.ap()), "s_gd")
            ctl.inc_dma(g.dma_start(out=y_out[128:256, 0:TCH], in_=statssq.ap()), "s_gd")
            ctl.inc_dma(g.dma_start(out=y_out[256:384, 0:TCH], in_=statrstd.ap()), "s_gd")
            ctl.inc_dma(g.dma_start(out=y_out[384:512, 0:TCH], in_=statB.ap()), "s_gd")
            ctl.inc_dma(g.dma_start(out=y_out[512:640, 0:TCH], in_=statr.ap()), "s_gd")
    elif DEBUG_TAP == "stacc":
        @block.gpsimd
        def _(g):
            g.wait_ge(ctl.sems["s_gd"], ctl.cnt["s_gd"])
            ctl.wait(g, ctl.tok("s_ld"))
            ctl.inc_dma(g.dma_start(
                out=y_out[0:113, 0:819], in_=stacc.ap()), "s_gd")
    elif DEBUG_TAP == "ph2":
        @block.gpsimd
        def _(g):
            g.wait_ge(ctl.sems["s_gd"], ctl.cnt["s_gd"])
            ctl.wait(g, ctl.tok("s_ld"))
            ctl.inc_dma(g.dma_start(
                out=y_out.rearrange("(a p) d -> a p d", p=128)[0:4],
                in_=tb_tap.rearrange("p (a d) -> a p d", d=1024)), "s_gd")
    elif DEBUG_TAP == "ffn":
        @block.gpsimd
        def _(g):
            g.wait_ge(ctl.sems["s_gd"], ctl.cnt["s_gd"])
            ctl.wait(g, ctl.tok("s_ld"))
            ctl.inc_dma(g.dma_start(out=y_out[0:1024, 0:352], in_=cc_wg_out[0:1024, :]), "s_gd")
            ctl.inc_dma(g.dma_start(out=y_out[1024:1152, 0:22], in_=sgcol.ap()), "s_gd")
            ctl.inc_dma(g.dma_start(out=y_out[1152:1280, 0:22], in_=sucol.ap()), "s_gd")
            ctl.inc_dma(g.dma_start(out=y_out[1280:1408, 0:4], in_=gcol.ap()), "s_gd")
            ctl.inc_dma(g.dma_start(out=y_out[1408:1416, 0:16], in_=cells.ap()), "s_gd")
            ctl.inc_dma(g.dma_start(out=y_out[1416:1544, 0:22], in_=sgxcol.ap()), "s_gd")
            ctl.inc_dma(g.dma_start(out=y_out[1544:1896, 0:1024],
                                    in_=cc_wg_in.rearrange("d r -> r d")), "s_gd")
    elif DEBUG_TAP == "h2a":
        @block.gpsimd
        def _(g):
            g.wait_ge(ctl.sems["s_gd"], ctl.cnt["s_gd"])
            ctl.wait(g, ctl.tok("s_ld"))
            for u in range(2):
                ctl.inc_dma(g.dma_start(
                    out=y_out.rearrange("(a w) d -> a w d", w=256)[0:8, u * 128:(u + 1) * 128, :],
                    in_=h2_spill[0:8, :, u * 1024:(u + 1) * 1024]), "s_gd")

    @block.gpsimd
    def _(g):
        g.wait_ge(ctl.sems["s_gd"], ctl.cnt["s_gd"])

    ctx.close()
    return nc


def _shard_inputs(L0_attn_x, L0_attn_gamma, L0_attn_w_q, L0_attn_w_kv, L0_attn_w_out,
                  L0_ffn_clip_x, L0_ffn_clip_down, L0_ffn_w_gate, L0_ffn_w_up,
                  L0_ffn_w_down, L0_c, L0_wmod, T):
    f32 = lambda a: np.ascontiguousarray(np.asarray(a, dtype=np.float32))
    x = f32(L0_attn_x)
    b, n, d = x.shape
    assert n == 4 * T and d == D
    gamma = f32(L0_attn_gamma); wq = f32(L0_attn_w_q); wkv = f32(L0_attn_w_kv)
    wout = f32(L0_attn_w_out); clipx = f32(L0_ffn_clip_x); clipd = f32(L0_ffn_clip_down)
    wg = f32(L0_ffn_w_gate); wu = f32(L0_ffn_w_up); wd = f32(L0_ffn_w_down)
    cc = f32(L0_c); wmod = f32(L0_wmod)
    in_maps = []
    for core in range(N_CORES):
        bb = core // 4
        qq = core % 4
        rg = slice(core * 352, (core + 1) * 352)
        in_maps.append({
            "x": np.ascontiguousarray(x[bb, qq * T:(qq + 1) * T, :]),
            "gamma": gamma, "wq": wq, "wkv": wkv, "wout": wout,
            "clipx": clipx, "clipd": clipd,
            "wg_sh": np.ascontiguousarray(wg[rg, :]),
            "wu_sh": np.ascontiguousarray(wu[rg, :]),
            "wd_sh": np.ascontiguousarray(wd[core * 128:(core + 1) * 128, :]),
            "cvec": np.ascontiguousarray(cc[bb].reshape(8, 128)),
            "wmod_sh": np.ascontiguousarray(wmod[:, qq * 1536:(qq + 1) * 1536]),
        })
    return in_maps


def kernel(L0_attn_x, L0_attn_gamma, L0_attn_w_q, L0_attn_w_kv, L0_attn_w_out,
           L0_ffn_clip_x, L0_ffn_clip_down, L0_ffn_w_gate, L0_ffn_w_up,
           L0_ffn_w_down, L0_c, L0_wmod):
    T = 2048
    in_maps = _shard_inputs(L0_attn_x, L0_attn_gamma, L0_attn_w_q, L0_attn_w_kv,
                            L0_attn_w_out, L0_ffn_clip_x, L0_ffn_clip_down,
                            L0_ffn_w_gate, L0_ffn_w_up, L0_ffn_w_down, L0_c, L0_wmod, T)
    nc = build_nc(T)
    res = run_bass_kernel_spmd(nc, in_maps, core_ids=list(range(N_CORES)))
    out = np.empty((2, 4 * T, D), dtype=np.float32)
    for core in range(N_CORES):
        bb = core // 4
        qq = core % 4
        out[bb, qq * T:(qq + 1) * T, :] = res.results[core]["y"]
    return out


# version 22 (final)
# speedup vs baseline: 1.0949x; 1.0723x over previous
import numpy as np
import ml_dtypes
from contextlib import ExitStack
from collections import defaultdict

import concourse.bass as bass
import concourse.mybir as mybir
from concourse.bass_utils import run_bass_kernel_spmd

F32 = mybir.dt.float32
BF16 = mybir.dt.bfloat16
I8 = mybir.dt.int8
I16 = mybir.dt.int16
QUANT_MODE = "cast"
DEBUG_TAP = None
AF = mybir.ActivationFunctionType
ALU = mybir.AluOpType
AX = mybir.AxisListType

D = 1024
DCH = 8
HEADS = 8
DH = 16
II = 2816
ICH = 22
NEG_BIG = -3.0e38
N_CORES = 8


class Ctl:
    def __init__(self, nc, ctx):
        self.nc = nc
        self.sems = {}
        self.cnt = defaultdict(int)
        self.wm = defaultdict(int)
        for name in ("s_ld", "s_sc", "s_gd", "s_cc", "s_te", "s_ve", "s_se", "s_gp"):
            self.sems[name] = ctx.enter_context(nc.semaphore(name))
        self.eng_sem = {"tensor": "s_te", "vector": "s_ve", "scalar": "s_se",
                        "gpsimd": "s_gp"}

    def inc(self, instr, name, n=1):
        instr.then_inc(self.sems[name], n)
        self.cnt[name] += n
        return (name, self.cnt[name])

    def inc_dma(self, instr, name):
        return self.inc(instr, name, 16)

    def tok(self, name):
        return (name, self.cnt[name])

    def wait(self, eng, tok, eng_name=None):
        if tok is None:
            return
        name, v = tok
        if v <= 0:
            return
        if eng_name is not None and self.eng_sem.get(eng_name) == name:
            return
        key = (id(eng), name)
        if self.wm[key] >= v:
            return
        eng.wait_ge(self.sems[name], v)
        self.wm[key] = v

    def waits(self, eng, toks, eng_name=None):
        for t in toks:
            self.wait(eng, t, eng_name)


def build_nc(T=2048):
    TCH = T // 128
    TT = T // 512
    assert T % 512 == 0

    nc = bass.Bass("TRN2", target_bir_lowering=False, debug=False,
                   detect_race_conditions=False)

    x_in = nc.dram_tensor("x", [T, D], F32, kind="ExternalInput").ap()
    gamma_in = nc.dram_tensor("gamma", [D], F32, kind="ExternalInput").ap()
    wq_in = nc.dram_tensor("wq", [D, 128], F32, kind="ExternalInput").ap()
    wkv_in = nc.dram_tensor("wkv", [D, 256], F32, kind="ExternalInput").ap()
    wout_in = nc.dram_tensor("wout", [128, D], F32, kind="ExternalInput").ap()
    clipx_in = nc.dram_tensor("clipx", [1], F32, kind="ExternalInput").ap()
    clipd_in = nc.dram_tensor("clipd", [1], F32, kind="ExternalInput").ap()
    wg_sh_in = nc.dram_tensor("wg_sh", [352, D], F32, kind="ExternalInput").ap()
    wu_sh_in = nc.dram_tensor("wu_sh", [352, D], F32, kind="ExternalInput").ap()
    wd_sh_in = nc.dram_tensor("wd_sh", [128, II], F32, kind="ExternalInput").ap()
    cvec_in = nc.dram_tensor("cvec", [8, 128], F32, kind="ExternalInput").ap()
    wmod_sh_in = nc.dram_tensor("wmod_sh", [D, 1536], F32, kind="ExternalInput").ap()
    y_out = nc.dram_tensor("y", [T, D], F32, kind="ExternalOutput").ap()

    tb_tap = nc.dram_tensor("tb_tap", [128, 4096], F32, kind="Internal").ap()
    h_spill = nc.dram_tensor("h_spill", [TCH, 128, D], F32, kind="Internal").ap()
    h2_spill = nc.dram_tensor("h2_spill", [ICH, 128, T], F32, kind="Internal").ap()

    cc_mod_in = nc.dram_tensor("cc_mod_in", [1536], F32, kind="Internal").ap()
    cc_mod_out = nc.dram_tensor("cc_mod_out", [6144], F32, kind="Internal").ap()
    cc_wg_in = nc.dram_tensor("cc_wg_in", [D, 352], BF16, kind="Internal").ap()
    cc_wg_out = nc.dram_tensor("cc_wg_out", [8 * D, 352], BF16, kind="Internal").ap()
    cc_wu_in = nc.dram_tensor("cc_wu_in", [D, 352], BF16, kind="Internal").ap()
    cc_wu_out = nc.dram_tensor("cc_wu_out", [8 * D, 352], BF16, kind="Internal").ap()
    cc_wd_in = nc.dram_tensor("cc_wd_in", [II, 128], BF16, kind="Internal").ap()
    cc_wd_out = nc.dram_tensor("cc_wd_out", [8 * II, 128], BF16, kind="Internal").ap()
    cc_sg_in = nc.dram_tensor("cc_sg_in", [352], F32, kind="Internal").ap()
    cc_sg_out = nc.dram_tensor("cc_sg_out", [II], F32, kind="Internal").ap()
    cc_su_in = nc.dram_tensor("cc_su_in", [352], F32, kind="Internal").ap()
    cc_su_out = nc.dram_tensor("cc_su_out", [II], F32, kind="Internal").ap()
    cc_sd_in = nc.dram_tensor("cc_sd_in", [128], F32, kind="Internal").ap()
    cc_sd_out = nc.dram_tensor("cc_sd_out", [D], F32, kind="Internal").ap()
    cc_st_in = nc.dram_tensor("cc_st_in", [113, 819], F32, kind="Internal").ap()
    cc_st_out = nc.dram_tensor("cc_st_out", [113, 819], F32, kind="Internal").ap()
    cc_m1_in = nc.dram_tensor("cc_m1_in", [1], F32, kind="Internal").ap()
    cc_m1_out = nc.dram_tensor("cc_m1_out", [1], F32, kind="Internal").ap()
    cc_m2_in = nc.dram_tensor("cc_m2_in", [1], F32, kind="Internal").ap()
    cc_m2_out = nc.dram_tensor("cc_m2_out", [1], F32, kind="Internal").ap()

    eyef_d = nc.inline_tensor(np.eye(128, dtype=np.float32), "eyef_c").ap()
    eyeb_d = nc.inline_tensor(np.eye(128, dtype=ml_dtypes.bfloat16), "eyeb_c").ap()
    onesf_d = nc.inline_tensor(np.ones((8, 128), dtype=np.float32), "onesf_c").ap()
    onesb_d = nc.inline_tensor(np.ones((8, 512), dtype=ml_dtypes.bfloat16), "onesb_c").ap()

    top = nc.sbuf_top
    off = [nc.sbuf_base]
    tensors = {}

    def alloc(name, shape, dtype, cursor=None):
        cur = off if cursor is None else cursor
        per_part = int(np.prod(shape[1:])) * mybir.dt.size(dtype)
        o = (cur[0] + 31) // 32 * 32
        cur[0] = o + per_part
        assert cur[0] <= top, f"SBUF overflow at {name}: {cur[0]} > {top}"
        h = nc.alloc_sbuf_tensor_at(name, list(shape), dtype, offset=o)
        tensors[name] = h
        return h

    xres = alloc("xres", [128, TCH, D], F32)
    eyef = alloc("eyef", [128, 128], F32)
    eyeb = alloc("eyeb", [128, 128], BF16)
    onesf = alloc("onesf", [8, 128], F32)
    onesb = alloc("onesb", [8, 512], BF16)
    cells = alloc("cells", [8, 16], F32)
    gcol = alloc("gcol", [128, 4], F32)
    gamcol = alloc("gamcol", [128, 16], F32)
    siluct = alloc("siluct", [128, 8], BF16)
    statsum = alloc("statsum", [128, TCH], F32)
    statssq = alloc("statssq", [128, TCH], F32)
    statmu = alloc("statmu", [128, TCH], F32)
    statrstd = alloc("statrstd", [128, TCH], F32)
    statB = alloc("statB", [128, TCH], F32)
    statr = alloc("statr", [128, TCH], F32)
    maxs = alloc("maxs", [128, TCH], F32)
    mins = alloc("mins", [128, TCH], F32)
    hmaxcol = alloc("hmaxcol", [128, 2], F32)
    sgcol = alloc("sgcol", [128, ICH], F32)
    sucol = alloc("sucol", [128, ICH], F32)
    sdrow = alloc("sdrow", [8, D], F32)
    sgxcol = alloc("sgxcol", [128, ICH], F32)
    suxcol = alloc("suxcol", [128, ICH], F32)
    wqb = alloc("wqb", [128, DCH, 128], BF16)
    wkvb = alloc("wkvb", [128, DCH, 256], BF16)
    woutb = alloc("woutb", [48, HEADS, D], BF16)
    mrow = alloc("mrow", [8, D], F32)
    garow = alloc("garow", [8, D], F32)
    modrepA = alloc("modrepA", [128, D], F32)
    modrepB = alloc("modrepB", [128, D], F32)
    qtok = alloc("qtok", [128, TCH, 128], BF16)
    stacc = alloc("stacc", [113, 819], F32)
    stA = alloc("stA", [16, HEADS, 48], BF16)
    stB = alloc("stB", [128, HEADS, 48], BF16)
    stC = alloc("stC", [128, HEADS, 48], BF16)
    stD = alloc("stD", [8, HEADS, 48], BF16)
    scratch = alloc("scratch", [128, D], F32)
    scrows = alloc("scrows", [22, 128], F32)
    junkpad = alloc("junkpad", [1, 4], F32)

    def vsp(v, n=2):
        for _ in range(n):
            v.memset(junkpad.ap(), 0.0)

    def vflush(v):
        vsp(v, 2)
        return v.memset(junkpad.ap(), 0.0)

    arena0 = off[0]

    a = [arena0]
    gamrow = alloc("gamrow", [1, D], F32, a)
    crow = alloc("crow", [8, 128], F32, a)
    wqf = alloc("wqf", [128, DCH, 128], F32, a)
    wkvf = alloc("wkvf", [128, DCH, 256], F32, a)
    wmodb = alloc("wmodb", [128, DCH, 384], BF16, a)
    wshf = alloc("wshf", [128, D], F32, a)
    wshu = alloc("wshu", [128, D], F32, a)
    wshb = alloc("wshb", [128, D], BF16, a)
    wsh8 = alloc("wsh8", [128, D], I8, a)
    wsh16 = alloc("wsh16", [128, D], I16, a)
    wshr = alloc("wshr", [128, D], F32, a)
    wdf = alloc("wdf", [128, II], F32, a)
    wdu = alloc("wdu", [128, II // 2], F32, a)
    wdb = alloc("wdb", [128, II // 2], BF16, a)
    wd8 = alloc("wd8", [128, II // 2], I8, a)
    wd16 = alloc("wd16", [128, II // 2], I16, a)
    wdr = alloc("wdr", [128, II // 2], F32, a)
    wdg = alloc("wdg", [128, II // 2], F32, a)
    modstage = alloc("modstage", [1, 1536], F32, a)
    ptile = alloc("ptile", [128, 1024], BF16, a)
    p1_end = a[0]

    a = [arena0]
    hstage = alloc("hstage", [128, 2, D], F32, a)
    hfeat = alloc("hfeat", [128, 2, DCH, 512], BF16, a)
    kfeat = alloc("kfeat", [128, 2, 512], BF16, a)
    vfeat = alloc("vfeat", [128, 2, 512], BF16, a)
    qfeatr = alloc("qfeatr", [128, 2, 512], BF16, a)
    ktok = alloc("ktok", [128, 2, 128], BF16, a)
    phiaug = alloc("phiaug", [128, 2, HEADS, 273], BF16, a)
    vaug = alloc("vaug", [128, 2, HEADS, 17], BF16, a)
    p2_end = a[0]

    a = [arena0]
    phi2q4 = alloc("phi2q4", [128, 4, 256], BF16, a)
    phiqA = alloc("phiqA", [128, 512], BF16, a)
    phiqB = alloc("phiqB", [128, 512], BF16, a)
    qf16 = alloc("qf16", [16, 512], BF16, a)
    ofd = alloc("ofd", [48, HEADS, 512], BF16, a)
    denr = alloc("denr", [8, HEADS, 512], F32, a)
    denr2 = alloc("denr2", [1, HEADS, 512], F32, a)
    p5_end = a[0]

    a = [arena0]
    xqfeat = alloc("xqfeat", [128, DCH, T], BF16, a)
    hq = alloc("hq", [128, D], F32, a)
    xq8 = alloc("xq8", [128, D], I8, a)
    if QUANT_MODE != "cast":
        xq16 = alloc("xq16", [128, D], I16, a)
    xqf = alloc("xqf", [128, D], F32, a)
    nxq = alloc("nxq", [128, D], BF16, a)
    wgs = alloc("wgs", [128, 2, DCH, 512], BF16, a)
    wus = alloc("wus", [128, 2, DCH, 512], BF16, a)
    silut = alloc("silut", [128, 512], F32, a)
    upt = alloc("upt", [128, 512], F32, a)
    h2st = alloc("h2st", [128, 2, 512], F32, a)
    p8_end = a[0]

    a = [arena0]
    wdTs = alloc("wdTs", [128, 2, D], BF16, a)
    rld = alloc("rld", [128, 2, 512], F32, a)
    x28 = alloc("x28", [128, 2, 512], I8, a)
    x216 = alloc("x216", [128, 2, 512], I16, a)
    x2f = alloc("x2f", [128, 2, 512], F32, a)
    x2g = alloc("x2g", [128, 2, 512], F32, a)
    u2 = alloc("u2", [128, 2, 512], F32, a)
    nxq2 = alloc("nxq2", [128, 2, 512], BF16, a)
    p10_end = a[0]

    banks = [nc.alloc_psum_tensor(f"bank{i}", [128, 512], F32) for i in range(8)]

    ctx = ExitStack()
    ctl = Ctl(nc, ctx)
    block = ctx.enter_context(nc.Block())

    bank_free = [None] * 8
    toks = {}

    @block.sync
    def _(sync):
        toks["x"] = ctl.inc_dma(sync.dma_start(
            out=xres.ap(), in_=x_in.rearrange("(c p) d -> p c d", p=128)), "s_ld")
        toks["eyef"] = ctl.inc_dma(sync.dma_start(out=eyef.ap(), in_=eyef_d), "s_ld")
        toks["eyeb"] = ctl.inc_dma(sync.dma_start(out=eyeb.ap(), in_=eyeb_d), "s_ld")
        toks["onesf"] = ctl.inc_dma(sync.dma_start(out=onesf.ap(), in_=onesf_d), "s_ld")
        toks["onesb"] = ctl.inc_dma(sync.dma_start(out=onesb.ap(), in_=onesb_d), "s_ld")
        toks["gam"] = ctl.inc_dma(sync.dma_start(out=gamrow.ap(), in_=gamma_in[None, :]), "s_ld")
        toks["crow"] = ctl.inc_dma(sync.dma_start(out=crow.ap(), in_=cvec_in), "s_ld")
        toks["wqf"] = ctl.inc_dma(sync.dma_start(
            out=wqf.ap(), in_=wq_in.rearrange("(c p) m -> p c m", p=128)), "s_ld")
        toks["wkvf"] = ctl.inc_dma(sync.dma_start(
            out=wkvf.ap(), in_=wkv_in.rearrange("(c p) m -> p c m", p=128)), "s_ld")
        toks["wdf"] = ctl.inc_dma(sync.dma_start(out=wdf.ap(), in_=wd_sh_in), "s_ld")

    @block.gpsimd
    def _(g):
        toks["wmod0"] = ctl.inc_dma(g.dma_start(
            out=wmodb.ap(),
            in_=wmod_sh_in.rearrange("(c p) m -> p c m", p=128)[:, :, 0:384]), "s_gd")
        toks["woutb"] = ctl.inc_dma(g.dma_start(
            out=woutb.ap()[32:48, :, :],
            in_=wout_in.rearrange("(h e) d -> e h d", h=8)), "s_gd")

    @block.vector
    def _(v):
        v.memset(stacc.ap(), 0.0)
        v.memset(stA.ap(), 0.0)
        v.memset(stB.ap(), 0.0)
        v.memset(stC.ap(), 0.0)
        v.memset(stD.ap(), 0.0)
        v.memset(mrow.ap(), 0.0)
        v.memset(garow.ap(), 0.0)
        v.memset(sdrow.ap(), 0.0)
        v.memset(cells.ap(), 0.0)
        v.memset(hmaxcol.ap(), NEG_BIG)
        for b in (5, 6, 7):
            i = v.memset(banks[b].ap()[:, 0:512], 0.0)
        ctl.inc(i, "s_ve")
    toks["memset"] = ctl.tok("s_ve")

    @block.sync
    def _(sync):
        ctl.wait(sync, toks["memset"])
        toks["clip"] = ctl.inc_dma(sync.dma_start(out=cells.ap()[0:1, 0:1],
                                                  in_=clipx_in[None, :]), "s_ld")
        toks["clip2"] = ctl.inc_dma(sync.dma_start(out=cells.ap()[0:1, 1:2],
                                                   in_=clipd_in[None, :]), "s_ld")

    @block.scalar
    def _(se):
        ctl.waits(se, [toks["crow"], toks["clip"], toks["clip2"]], "scalar")
        se.activation(out=crow.ap(), in_=crow.ap(), func=AF.Silu)
        se.activation(out=cells.ap()[0:1, 2:3], in_=cells.ap()[0:1, 0:1], func=AF.Sigmoid)
        i = se.activation(out=cells.ap()[0:1, 3:4], in_=cells.ap()[0:1, 1:2], func=AF.Sigmoid)
        ctl.inc(i, "s_se")
    toks["siluc"] = ctl.tok("s_se")

    @block.tensor
    def _(te):
        ctl.waits(te, [toks["gam"], toks["eyef"], toks["siluc"]], "tensor")
        for ch in range(DCH):
            te.transpose(out=banks[0].ap()[:, ch:ch + 1],
                         in_=gamrow.ap()[0:1, ch * 128:(ch + 1) * 128],
                         identity=eyef.ap()[0:1, 0:1])
        i = te.transpose(out=banks[0].ap()[:, 8:16], in_=crow.ap(),
                         identity=eyef.ap()[0:8, 0:8])
        ctl.inc(i, "s_te")
    toks["cols_mm"] = ctl.tok("s_te")

    @block.scalar
    def _(se):
        ctl.wait(se, toks["cols_mm"], "scalar")
        se.activation(out=gamcol.ap()[:, 0:8], in_=banks[0].ap()[:, 0:8], func=AF.Copy)
        se.activation(out=gamcol.ap()[:, 8:16], in_=banks[0].ap()[:, 0:8],
                      func=AF.Copy, scale=0.25)
        i = se.activation(out=siluct.ap(), in_=banks[0].ap()[:, 8:16], func=AF.Copy)
        ctl.inc(i, "s_se")
    toks["cols"] = ctl.tok("s_se")
    bank_free[0] = toks["cols"]

    @block.vector
    def _(v):
        ctl.waits(v, [toks["wqf"], toks["wkvf"], toks["cols"]], "vector")
        for ch in range(DCH):
            v.tensor_scalar(out=wqb.ap()[:, ch, :], in0=wqf.ap()[:, ch, :],
                            scalar1=gamcol.ap()[:, 8 + ch:9 + ch], scalar2=None, op0=ALU.mult)
            i = v.tensor_scalar(out=wkvb.ap()[:, ch, :], in0=wkvf.ap()[:, ch, :],
                                scalar1=gamcol.ap()[:, ch:ch + 1], scalar2=None, op0=ALU.mult)
        ctl.inc(i, "s_ve")
    toks["wproj"] = ctl.tok("s_ve")

    t_mvq = toks["wmod0"]
    for qd in range(4):
        bq = 1 + (qd % 2)

        @block.tensor
        def _(te, qd=qd, bq=bq, t_mvq=t_mvq):
            ctl.waits(te, [toks["cols"], t_mvq, bank_free[bq]], "tensor")
            for ch in range(DCH):
                i = te.matmul(out=banks[bq].ap()[0:1, 0:384], lhsT=siluct.ap()[:, ch:ch + 1],
                              rhs=wmodb.ap()[:, ch, 0:384],
                              start=(ch == 0), stop=(ch == DCH - 1))
            ctl.inc(i, "s_te")
        t_mv = ctl.tok("s_te")

        @block.scalar
        def _(se, qd=qd, bq=bq, t_mv=t_mv):
            ctl.wait(se, t_mv, "scalar")
            i = se.activation(out=modstage.ap()[0:1, qd * 384:(qd + 1) * 384],
                              in_=banks[bq].ap()[0:1, 0:384], func=AF.Copy)
            ctl.inc(i, "s_se")
        bank_free[bq] = ctl.tok("s_se")

        if qd < 3:
            @block.gpsimd
            def _(g, qd=qd, t_mv=t_mv):
                ctl.wait(g, t_mv)
                toks["wmodq"] = ctl.inc_dma(g.dma_start(
                    out=wmodb.ap(),
                    in_=wmod_sh_in.rearrange("(c p) m -> p c m", p=128)[
                        :, :, (qd + 1) * 384:(qd + 2) * 384]), "s_gd")
            t_mvq = toks["wmodq"]

    @block.gpsimd
    def _(g):
        ctl.wait(g, ctl.tok("s_se"))
        toks["modst"] = ctl.inc_dma(g.dma_start(out=cc_mod_in[None, :], in_=modstage.ap()[0:1, :]), "s_gd")
        ctl.wait(g, toks["modst"])
        toks["ag_mod"] = ctl.inc(g.collective_compute(
            "AllGather", ALU.bypass, replica_groups=[[0, 1, 2, 3], [4, 5, 6, 7]],
            ins=[cc_mod_in], outs=[cc_mod_out]), "s_cc")

    prev_wsh_read = None
    prev_wsh_store = None

    def emit_wscale(rows, wt_ap, wtok, prev_store):
        @block.vector
        def _(v):
            ctl.waits(v, [wtok, prev_store], "vector")
            i = v.tensor_reduce(out=scratch.ap()[0:rows, 0:1], in_=wt_ap, axis=AX.X,
                                op=ALU.max, apply_absolute_value=True)
            ctl.inc(i, "s_ve")
        t_red = ctl.tok("s_ve")

        @block.scalar
        def _(se):
            ctl.wait(se, t_red, "scalar")
            i = se.activation(out=scratch.ap()[0:rows, 3:4], in_=scratch.ap()[0:rows, 0:1],
                              func=AF.Copy, scale=1.0 / 127.0)
            ctl.inc(i, "s_se")
        t_c3 = ctl.tok("s_se")

        @block.vector
        def _(v):
            ctl.wait(v, t_c3, "vector")
            i = v.reciprocal(out=scratch.ap()[0:rows, 2:3], in_=scratch.ap()[0:rows, 3:4])
            ctl.inc(i, "s_ve")
        return ctl.tok("s_ve")

    def emit_quant(rows, cols, wt_ap, wu_ap, wb_ap, w8_ap, w16_ap, wr_ap, wg_ap,
                   t_rs):

        @block.scalar
        def _(se):
            ctl.wait(se, t_rs, "scalar")
            i = se.activation(out=wr_ap, in_=wt_ap, func=AF.Copy,
                              scale=scratch.ap()[0:rows, 2:3])
            ctl.inc(i, "s_se")
        t_u = ctl.tok("s_se")

        if QUANT_MODE == "cast":
            @block.vector
            def _(v):
                ctl.wait(v, t_u, "vector")
                v.tensor_copy(out=w8_ap, in_=wr_ap)
                vsp(v)
                v.tensor_copy(out=wb_ap, in_=w8_ap)
                i = vflush(v)
                ctl.inc(i, "s_ve")
        else:
            @block.vector
            def _(v):
                ctl.wait(v, t_u, "vector")
                v.tensor_copy(out=w16_ap, in_=wr_ap)
                vsp(v)
                v.tensor_copy(out=wu_ap, in_=w16_ap)
                vsp(v)
                v.tensor_tensor(out=wr_ap, in0=wr_ap, in1=wu_ap, op=ALU.subtract)
                vsp(v)
                v.tensor_scalar(out=wg_ap, in0=wr_ap, scalar1=0.5, scalar2=None, op0=ALU.is_ge)
                v.tensor_scalar(out=wr_ap, in0=wr_ap, scalar1=-0.5, scalar2=-1.0,
                                op0=ALU.is_ge, op1=ALU.add)
                vsp(v)
                v.tensor_tensor(out=wg_ap, in0=wg_ap, in1=wr_ap, op=ALU.add)
                vsp(v)
                v.tensor_tensor(out=wu_ap, in0=wu_ap, in1=wg_ap, op=ALU.add)
                vsp(v)
                v.tensor_scalar(out=wb_ap, in0=wu_ap, scalar1=-128.0, scalar2=127.0,
                                op0=ALU.max, op1=ALU.min)
                i = vflush(v)
                ctl.inc(i, "s_ve")
        return t_u, ctl.tok("s_ve")

    wg_tiles = [(0, 128), (128, 256), (256, 352)]
    prev_ptile = None
    for w_in, cc_w, cc_s in ((wg_sh_in, cc_wg_in, cc_sg_in),
                             (wu_sh_in, cc_wu_in, cc_su_in)):
        for (r0, r1) in wg_tiles:
            rows = r1 - r0

            @block.sync
            def _(sync, w_in=w_in, r0=r0, r1=r1, rows=rows):
                ctl.waits(sync, [prev_wsh_read])
                toks["wsh_ld"] = ctl.inc_dma(sync.dma_start(
                    out=wshf.ap()[0:rows, :], in_=w_in[r0:r1, :]), "s_ld")

            t_rs = emit_wscale(rows, wshf.ap()[0:rows, :], toks["wsh_ld"], prev_wsh_store)
            t_u, t_q = emit_quant(rows, D, wshf.ap()[0:rows, :], wshu.ap()[0:rows, :],
                                  wshb.ap()[0:rows, :], wsh8.ap()[0:rows, :],
                                  wsh16.ap()[0:rows, :], wshr.ap()[0:rows, :],
                                  wshf.ap()[0:rows, :], t_rs)
            prev_wsh_read = t_q

            @block.tensor
            def _(te, rows=rows, t_q=t_q, prev_ptile=prev_ptile):
                ctl.waits(te, [t_q, toks["eyeb"], bank_free[3], prev_ptile], "tensor")
                bv = banks[3].ap().bitcast(BF16)
                for dch in range(DCH):
                    i = te.transpose(out=bv[:, dch * 128:dch * 128 + rows],
                                     in_=wshb.ap()[0:rows, dch * 128:(dch + 1) * 128],
                                     identity=eyeb.ap()[0:rows, 0:rows])
                ctl.inc(i, "s_te")
            t_wT = ctl.tok("s_te")

            @block.scalar
            def _(se, rows=rows, t_wT=t_wT, prev_ptile=prev_ptile):
                ctl.waits(se, [t_wT, prev_ptile], "scalar")
                bv = banks[3].ap().bitcast(BF16)
                for dch in range(DCH):
                    i = se.activation(out=ptile.ap()[:, dch * 128:dch * 128 + rows],
                                      in_=bv[:, dch * 128:dch * 128 + rows], func=AF.Copy)
                ctl.inc(i, "s_se")
            t_wTe = ctl.tok("s_se")
            bank_free[3] = t_wTe

            @block.sync
            def _(sync, cc_w=cc_w, cc_s=cc_s, r0=r0, r1=r1, rows=rows, t_q=t_q,
                  t_wTe=t_wTe):
                ctl.waits(sync, [t_q, t_wTe])
                for dch in range(DCH):
                    ctl.inc_dma(sync.dma_start(
                        out=cc_w[dch * 128:(dch + 1) * 128, r0:r1],
                        in_=ptile.ap()[:, dch * 128:dch * 128 + rows]), "s_ld")
                ctl.inc_dma(sync.dma_start(out=cc_s[r0:r1],
                                           in_=scratch.ap()[0:rows, 3:4]), "s_ld")
            prev_wsh_store = ctl.tok("s_ld")
            prev_ptile = prev_wsh_store

    t_rs = emit_wscale(128, wdf.ap(), toks["wdf"], prev_wsh_store)
    prev_half_store = None
    for half in range(2):
        hsl = slice(half * (II // 2), (half + 1) * (II // 2))
        if prev_half_store is not None:
            @block.vector
            def _(v, prev_half_store=prev_half_store):
                ctl.wait(v, prev_half_store, "vector")
                ctl.inc(v.engine_nop(), "s_ve")
            t_rs2 = ctl.tok("s_ve")
        else:
            t_rs2 = t_rs
        t_u, t_qd = emit_quant(128, II // 2, wdf.ap()[:, hsl], wdu.ap(), wdb.ap(),
                               wd8.ap(), wd16.ap(), wdr.ap(), wdg.ap(), t_rs2)

        h0 = half * (II // 2)
        for grp in range(2):
            j0, j1 = (0, 8) if grp == 0 else (8, 11)

            @block.tensor
            def _(te, j0=j0, j1=j1, t_qd=t_qd, prev_ptile=prev_ptile):
                ctl.waits(te, [t_qd, toks["eyeb"], bank_free[3], prev_ptile], "tensor")
                bv = banks[3].ap().bitcast(BF16)
                for j in range(j0, j1):
                    i = te.transpose(out=bv[:, (j - j0) * 128:(j - j0 + 1) * 128],
                                     in_=wdb.ap()[:, j * 128:(j + 1) * 128],
                                     identity=eyeb.ap())
                ctl.inc(i, "s_te")
            t_wdT = ctl.tok("s_te")

            @block.scalar
            def _(se, j0=j0, j1=j1, t_wdT=t_wdT, prev_ptile=prev_ptile):
                ctl.waits(se, [t_wdT, prev_ptile], "scalar")
                bv = banks[3].ap().bitcast(BF16)
                for j in range(j0, j1):
                    i = se.activation(out=ptile.ap()[:, (j - j0) * 128:(j - j0 + 1) * 128],
                                      in_=bv[:, (j - j0) * 128:(j - j0 + 1) * 128],
                                      func=AF.Copy)
                ctl.inc(i, "s_se")
            t_wdTe = ctl.tok("s_se")
            bank_free[3] = t_wdTe

            @block.sync
            def _(sync, h0=h0, j0=j0, j1=j1, t_wdTe=t_wdTe):
                ctl.wait(sync, t_wdTe)
                for j in range(j0, j1):
                    ctl.inc_dma(sync.dma_start(
                        out=cc_wd_in[h0 + j * 128:h0 + (j + 1) * 128, :],
                        in_=ptile.ap()[:, (j - j0) * 128:(j - j0 + 1) * 128]), "s_ld")
            prev_ptile = ctl.tok("s_ld")
        prev_half_store = ctl.tok("s_ld")

    @block.sync
    def _(sync):
        ins = sync.dma_start(out=cc_sd_in, in_=scratch.ap()[0:128, 3:4])
        ctl.inc_dma(ins, "s_ld")
    toks["wq_stores"] = ctl.tok("s_ld")

    @block.gpsimd
    def _(g):
        ctl.wait(g, toks["wq_stores"])
        toks["ag_wg"] = ctl.inc(g.collective_compute(
            "AllGather", ALU.bypass, replica_groups=[list(range(8))],
            ins=[cc_wg_in], outs=[cc_wg_out]), "s_cc")
        toks["ag_wu"] = ctl.inc(g.collective_compute(
            "AllGather", ALU.bypass, replica_groups=[list(range(8))],
            ins=[cc_wu_in], outs=[cc_wu_out]), "s_cc")
        toks["ag_wd"] = ctl.inc(g.collective_compute(
            "AllGather", ALU.bypass, replica_groups=[list(range(8))],
            ins=[cc_wd_in], outs=[cc_wd_out]), "s_cc")
        toks["ag_sg"] = ctl.inc(g.collective_compute(
            "AllGather", ALU.bypass, replica_groups=[list(range(8))],
            ins=[cc_sg_in], outs=[cc_sg_out]), "s_cc")
        toks["ag_su"] = ctl.inc(g.collective_compute(
            "AllGather", ALU.bypass, replica_groups=[list(range(8))],
            ins=[cc_su_in], outs=[cc_su_out]), "s_cc")
        toks["ag_sd"] = ctl.inc(g.collective_compute(
            "AllGather", ALU.bypass, replica_groups=[list(range(8))],
            ins=[cc_sd_in], outs=[cc_sd_out]), "s_cc")

    @block.vector
    def _(v):
        ctl.wait(v, toks["wq_stores"], "vector")
        v.memset(vaug.ap()[:, :, :, 16:17], 1.0)
        i = v.memset(phiaug.ap()[:, :, :, 272:273], 1.0)
        ctl.inc(i, "s_ve")
    toks["arena2"] = ctl.tok("s_ve")

    @block.gpsimd
    def _(g):
        ctl.waits(g, [toks["ag_mod"], toks["memset"]])
        toks["ca_row"] = ctl.inc_dma(g.dma_start(
            out=mrow.ap()[0:1, :], in_=cc_mod_out[None, 1024:2048]), "s_gd")
        toks["ga_row"] = ctl.inc_dma(g.dma_start(
            out=garow.ap()[0:1, :], in_=cc_mod_out[None, 2048:3072]), "s_gd")

    @block.scalar
    def _(se):
        ctl.wait(se, toks["ga_row"], "scalar")
        se.activation(out=scratch.ap()[0:1, 0:D], in_=garow.ap()[0:1, :], func=AF.Sign)
        i = se.activation(out=garow.ap()[0:1, :], in_=garow.ap()[0:1, :], func=AF.Abs)
        ctl.inc(i, "s_se")
    t_cl = ctl.tok("s_se")

    @block.vector
    def _(v):
        ctl.wait(v, t_cl, "vector")
        v.tensor_scalar(out=garow.ap()[0:1, :], in0=garow.ap()[0:1, :],
                        scalar1=0.25, scalar2=None, op0=ALU.max)
        vsp(v)
        i = v.tensor_tensor(out=garow.ap()[0:1, :], in0=garow.ap()[0:1, :],
                            in1=scratch.ap()[0:1, 0:D], op=ALU.mult)
        ctl.inc(i, "s_ve")
    toks["clamp"] = ctl.tok("s_ve")

    for half in range(2):
        sl = slice(half * 512, (half + 1) * 512)

        @block.tensor
        def _(te, sl=sl):
            ctl.waits(te, [toks["clamp"], toks["ca_row"], toks["onesf"],
                           bank_free[0], bank_free[2]], "tensor")
            for nh in range(2):
                ns = slice(nh * 256, (nh + 1) * 256)
                nsl = slice(sl.start + nh * 256, sl.start + (nh + 1) * 256)
                te.matmul(out=banks[0].ap()[:, ns], lhsT=onesf.ap(),
                          rhs=mrow.ap()[0:8, nsl], start=True, stop=True)
                i = te.matmul(out=banks[2].ap()[32:48, ns], lhsT=onesf.ap()[0:8, 0:16],
                              rhs=garow.ap()[0:8, nsl], start=True, stop=True)
            ctl.inc(i, "s_te")
        t_rep_mm = ctl.tok("s_te")

        @block.scalar
        def _(se, sl=sl, t_rep_mm=t_rep_mm):
            ctl.wait(se, t_rep_mm, "scalar")
            i = se.activation(out=modrepA.ap()[:, sl], in_=banks[0].ap()[:, 0:512],
                              func=AF.Copy, bias=1.0)
            ctl.inc(i, "s_se")
        t_repe = ctl.tok("s_se")

        @block.vector
        def _(v, sl=sl, t_rep_mm=t_rep_mm):
            ctl.waits(v, [t_rep_mm, toks["woutb"]], "vector")
            for hh in range(HEADS):
                i = v.tensor_tensor(out=woutb.ap()[32:48, hh, sl],
                                    in0=woutb.ap()[32:48, hh, sl],
                                    in1=banks[2].ap()[32:48, 0:512], op=ALU.mult)
            ctl.inc(i, "s_ve")
        bank_free[0] = t_repe
        bank_free[2] = ctl.tok("s_ve")

    @block.gpsimd
    def _(g):
        ctl.wait(g, ctl.tok("s_te"))
        toks["sa_row"] = ctl.inc_dma(g.dma_start(
            out=mrow.ap()[0:1, :], in_=cc_mod_out[None, 0:1024]), "s_gd")

    for half in range(2):
        sl = slice(half * 512, (half + 1) * 512)

        @block.tensor
        def _(te, sl=sl):
            ctl.waits(te, [toks["sa_row"], bank_free[1]], "tensor")
            for nh in range(2):
                i = te.matmul(out=banks[1].ap()[:, nh * 256:(nh + 1) * 256], lhsT=onesf.ap(),
                              rhs=mrow.ap()[0:8, sl.start + nh * 256:sl.start + (nh + 1) * 256],
                              start=True, stop=True)
            ctl.inc(i, "s_te")
        t_rep_mm = ctl.tok("s_te")

        @block.scalar
        def _(se, sl=sl, t_rep_mm=t_rep_mm):
            ctl.wait(se, t_rep_mm, "scalar")
            i = se.activation(out=modrepB.ap()[:, sl], in_=banks[1].ap()[:, 0:512], func=AF.Copy)
            ctl.inc(i, "s_se")
        bank_free[1] = ctl.tok("s_se")
    toks["modrep"] = ctl.tok("s_se")

    st_banks = (5, 6, 7)
    prev_state_add = None
    prev_tokuse2 = [None, None]
    prev_hT_evac = [None, None]
    prev_projevac = None
    prev_tokuse = None
    prev_hfeat_use = [None, None]

    for tt in range(TT):
        ch0 = tt * 4

        @block.scalar
        def _(se, ch0=ch0):
            ctl.waits(se, [toks["x"], toks["modrep"], toks["wq_stores"]], "scalar")
            for c in range(ch0, ch0 + 4):
                se.activation(out=scratch.ap(), in_=xres.ap()[:, c, :], func=AF.Square,
                              accum_out=statssq.ap()[:, c:c + 1])
            for c in range(ch0, ch0 + 4):
                i = se.activation(out=scratch.ap(), in_=xres.ap()[:, c, :], func=AF.Copy,
                                  accum_out=statsum.ap()[:, c:c + 1])
            ctl.inc(i, "s_se")
        t_stats = ctl.tok("s_se")

        @block.vector
        def _(v, ch0=ch0, t_stats=t_stats):
            ctl.wait(v, t_stats, "vector")
            sl = slice(ch0, ch0 + 4)
            v.tensor_scalar(out=statmu.ap()[:, sl], in0=statsum.ap()[:, sl],
                            scalar1=1.0 / D, scalar2=None, op0=ALU.mult)
            vsp(v)
            for cc_ in range(ch0, ch0 + 4):
                v.tensor_scalar(out=statB.ap()[:, cc_:cc_ + 1],
                                in0=statmu.ap()[:, cc_:cc_ + 1],
                                scalar1=statmu.ap()[:, cc_:cc_ + 1], scalar2=None,
                                op0=ALU.mult)
            vsp(v)
            v.scalar_tensor_tensor(out=statB.ap()[:, sl], in0=statssq.ap()[:, sl],
                                   scalar=1.0 / D, in1=statB.ap()[:, sl],
                                   op0=ALU.mult, op1=ALU.subtract)
            vsp(v)
            v.tensor_scalar(out=statB.ap()[:, sl], in0=statB.ap()[:, sl],
                            scalar1=1e-5, scalar2=None, op0=ALU.add)
            vsp(v)
            i = v.reciprocal(out=statrstd.ap()[:, sl], in_=statB.ap()[:, sl])
            ctl.inc(i, "s_ve")
        t_chain = ctl.tok("s_ve")

        @block.scalar
        def _(se, ch0=ch0, t_chain=t_chain):
            ctl.wait(se, t_chain, "scalar")
            sl = slice(ch0, ch0 + 4)
            i = se.activation(out=statrstd.ap()[:, sl], in_=statrstd.ap()[:, sl], func=AF.Sqrt)
            ctl.inc(i, "s_se")
        t_rstd = ctl.tok("s_se")

        @block.vector
        def _(v, ch0=ch0, t_rstd=t_rstd):
            ctl.wait(v, t_rstd, "vector")
            sl = slice(ch0, ch0 + 4)
            i = v.scalar_tensor_tensor(out=statB.ap()[:, sl], in0=statmu.ap()[:, sl],
                                       scalar=-1.0, in1=statrstd.ap()[:, sl],
                                       op0=ALU.mult, op1=ALU.mult)
            ctl.inc(i, "s_ve")
        t_B = ctl.tok("s_ve")

        for ci in range(4):
            c = ch0 + ci
            hb = ci % 2

            @block.vector
            def _(v, c=c, hb=hb, t_B=t_B):
                ctl.waits(v, [t_B, prev_hT_evac[hb], toks["wq_stores"]], "vector")
                hv = hstage.ap()[:, hb, :]
                vsp(v)
                v.tensor_scalar(out=hv, in0=xres.ap()[:, c, :],
                                scalar1=statrstd.ap()[:, c:c + 1],
                                scalar2=statB.ap()[:, c:c + 1],
                                op0=ALU.mult, op1=ALU.add)
                vsp(v)
                v.tensor_tensor(out=hv, in0=hv, in1=modrepA.ap(), op=ALU.mult)
                vsp(v)
                i = v.tensor_tensor(out=hv, in0=hv, in1=modrepB.ap(), op=ALU.add)
                ctl.inc(i, "s_ve")
            t_h = ctl.tok("s_ve")

            @block.scalar
            def _(se, c=c, hb=hb, t_h=t_h):
                ctl.wait(se, t_h, "scalar")
                i = se.activation(out=scratch.ap(), in_=hstage.ap()[:, hb, :],
                                  func=AF.Square, accum_out=statr.ap()[:, c:c + 1])
                ctl.inc(i, "s_se")
            t_hsq = ctl.tok("s_se")

            @block.tensor
            def _(te, c=c, hb=hb, t_h=t_h):
                ctl.waits(te, [t_h, bank_free[0], bank_free[1]], "tensor")
                for dch in range(DCH):
                    bk = 0 if dch < 4 else 1
                    i = te.transpose(out=banks[bk].ap()[:, (dch % 4) * 128:(dch % 4 + 1) * 128],
                                     in_=hstage.ap()[:, hb, dch * 128:(dch + 1) * 128],
                                     identity=eyef.ap())
                ctl.inc(i, "s_te")
            t_hT = ctl.tok("s_te")

            @block.scalar
            def _(se, c=c, ci=ci, tt=tt, t_hT=t_hT):
                ctl.waits(se, [t_hT, prev_hfeat_use[tt % 2]], "scalar")
                for dch in range(DCH):
                    bk = 0 if dch < 4 else 1
                    i = se.activation(out=hfeat.ap()[:, tt % 2, dch, ci * 128:(ci + 1) * 128],
                                      in_=banks[bk].ap()[:, (dch % 4) * 128:(dch % 4 + 1) * 128],
                                      func=AF.Copy)
                ctl.inc(i, "s_se")
            prev_hT_evac[hb] = ctl.tok("s_se")
            bank_free[0] = ctl.tok("s_se")
            bank_free[1] = ctl.tok("s_se")
        t_hfeat = ctl.tok("s_se")

        @block.scalar
        def _(se, ch0=ch0, t_hsq=t_hsq):
            ctl.wait(se, t_hsq, "scalar")
            sl = slice(ch0, ch0 + 4)
            i = se.activation(out=statr.ap()[:, sl], in_=statr.ap()[:, sl], func=AF.Sqrt)
            ctl.inc(i, "s_se")
        t_nn = ctl.tok("s_se")

        @block.vector
        def _(v, ch0=ch0, t_nn=t_nn):
            ctl.wait(v, t_nn, "vector")
            sl = slice(ch0, ch0 + 4)
            v.tensor_scalar(out=statB.ap()[:, sl], in0=statr.ap()[:, sl],
                            scalar1=1e-12, scalar2=1.0 / 32.0,
                            op0=ALU.max, op1=ALU.mult)
            vsp(v)
            i = v.reciprocal(out=statr.ap()[:, sl], in_=statB.ap()[:, sl])
            ctl.inc(i, "s_ve")
        toks[("rt", tt)] = ctl.tok("s_ve")

        @block.tensor
        def _(te, tt=tt, t_hfeat=t_hfeat):
            ctl.waits(te, [t_hfeat, toks["wproj"], prev_projevac,
                           bank_free[2], bank_free[3], bank_free[4]], "tensor")
            for (bk, w_t, c0, c1) in ((2, wqb, 0, 128), (3, wkvb, 0, 128), (4, wkvb, 128, 256)):
                for ch in range(DCH):
                    i = te.matmul(out=banks[bk].ap()[:, 0:512],
                                  lhsT=w_t.ap()[:, ch, c0:c1],
                                  rhs=hfeat.ap()[:, tt % 2, ch, :],
                                  start=(ch == 0), stop=(ch == DCH - 1))
            ctl.inc(i, "s_te")
        t_proj = ctl.tok("s_te")
        prev_hfeat_use[tt % 2] = t_proj

        @block.scalar
        def _(se, tt=tt, t_proj=t_proj):
            ctl.waits(se, [t_proj, prev_tokuse], "scalar")
            se.activation(out=qfeatr.ap()[:, tt % 2, :], in_=banks[2].ap()[:, 0:512], func=AF.Copy)
            se.activation(out=kfeat.ap()[:, tt % 2, :], in_=banks[3].ap()[:, 0:512], func=AF.Copy)
            i = se.activation(out=vfeat.ap()[:, tt % 2, :], in_=banks[4].ap()[:, 0:512],
                              func=AF.Copy)
            ctl.inc(i, "s_se")
        t_pe = ctl.tok("s_se")
        prev_projevac = t_pe
        bank_free[2] = t_pe
        bank_free[3] = t_pe
        bank_free[4] = t_pe

        for ci in range(4):
            c = ch0 + ci
            cb2 = c % 2

            @block.tensor
            def _(te, tt=tt, ci=ci, cb2=cb2, t_pe=t_pe):
                ctl.waits(te, [t_pe, toks["eyeb"], bank_free[cb2], prev_tokuse2[cb2]],
                          "tensor")
                kp = banks[cb2].ap().bitcast(BF16)
                te.transpose(out=kp[:, 0:128], in_=kfeat.ap()[:, tt % 2, ci * 128:(ci + 1) * 128],
                             identity=eyeb.ap())
                te.transpose(out=kp[:, 128:256], in_=vfeat.ap()[:, tt % 2, ci * 128:(ci + 1) * 128],
                             identity=eyeb.ap())
                i = te.transpose(out=kp[:, 256:384],
                                 in_=qfeatr.ap()[:, tt % 2, ci * 128:(ci + 1) * 128],
                                 identity=eyeb.ap())
                ctl.inc(i, "s_te")
            t_tokT = ctl.tok("s_te")

            @block.scalar
            def _(se, c=c, tt=tt, cb2=cb2, t_tokT=t_tokT):
                ctl.waits(se, [t_tokT, toks[("rt", tt)], toks["arena2"],
                               prev_tokuse2[cb2]], "scalar")
                kp = banks[cb2].ap().bitcast(BF16)
                rr = statr.ap()[:, c:c + 1]
                se.activation(out=ktok.ap()[:, cb2, :], in_=kp[:, 0:128], func=AF.Copy, scale=rr)
                se.activation(out=vaug.ap()[:, cb2, :, 0:16], in_=kp[:, 128:256],
                              func=AF.Copy, scale=rr)
                se.activation(out=phiaug.ap()[:, cb2, :, 0:16], in_=kp[:, 0:128],
                              func=AF.Copy, scale=rr)
                i = se.activation(out=qtok.ap()[:, c, :], in_=kp[:, 256:384], func=AF.Copy, scale=rr)
                ctl.inc(i, "s_se")
            t_tok = ctl.tok("s_se")
            bank_free[cb2] = t_tok

            @block.vector
            def _(v, cb2=cb2, t_tok=t_tok):
                ctl.wait(v, t_tok, "vector")
                for hh in range(HEADS):
                    ks = ktok.ap()[:, cb2, hh * 16:(hh + 1) * 16]
                    i = v.tensor_tensor(
                        out=phiaug.ap()[:, cb2, hh, 16:272].rearrange("p (i j) -> p i j", i=16),
                        in0=ks.unsqueeze(2).broadcast_to([128, 16, 16]),
                        in1=ks.unsqueeze(1).broadcast_to([128, 16, 16]), op=ALU.mult)
                ctl.inc(i, "s_ve")
            t_phi = ctl.tok("s_ve")

            @block.tensor
            def _(te, cb2=cb2, t_phi=t_phi):
                ctl.waits(te, [t_phi, prev_state_add, toks["memset"]], "tensor")
                for hh in range(HEADS):
                    bk = st_banks[hh // 3]
                    po = 32 * (hh % 3)
                    i = te.matmul(out=banks[bk].ap()[po:po + 17, 0:273],
                                  lhsT=vaug.ap()[:, cb2, hh, :],
                                  rhs=phiaug.ap()[:, cb2, hh, :], start=True, stop=True)
                ctl.inc(i, "s_te")
            t_smm = ctl.tok("s_te")
            prev_tokuse = t_smm
            prev_tokuse2[cb2] = t_smm

            @block.vector
            def _(v, t_smm=t_smm):
                ctl.wait(v, t_smm, "vector")
                for k in range(3):
                    i = v.tensor_tensor(out=stacc.ap()[:, k * 273:(k + 1) * 273],
                                        in0=stacc.ap()[:, k * 273:(k + 1) * 273],
                                        in1=banks[st_banks[k]].ap()[0:113, 0:273], op=ALU.add)
                ctl.inc(i, "s_ve")
            prev_state_add = ctl.tok("s_ve")

        if DEBUG_TAP == "ph2" and tt == 0:
            @block.gpsimd
            def _(g, prev_state_add=prev_state_add):
                ctl.wait(g, prev_state_add)
                ctl.inc_dma(g.dma_start(out=tb_tap[:, 0:1024], in_=hstage.ap()[:, 1, :]), "s_gd")
                ctl.inc_dma(g.dma_start(out=tb_tap[:, 1024:1152], in_=ktok.ap()[:, 1, :]), "s_gd")
                ctl.inc_dma(g.dma_start(out=tb_tap[:, 1152:1425], in_=phiaug.ap()[:, 1, 0, :]), "s_gd")
                ctl.inc_dma(g.dma_start(out=tb_tap[:, 1425:1442], in_=vaug.ap()[:, 1, 0, :]), "s_gd")
                ctl.inc_dma(g.dma_start(out=tb_tap[:, 1442:1446], in_=statr.ap()[:, 0:4]), "s_gd")
                ctl.inc_dma(g.dma_start(out=tb_tap[:, 1446:1450], in_=statrstd.ap()[:, 0:4]), "s_gd")
                ctl.inc_dma(g.dma_start(out=tb_tap[:, 1450:1454], in_=statsum.ap()[:, 0:4]), "s_gd")
                ctl.inc_dma(g.dma_start(out=tb_tap[:, 1454:1458], in_=statssq.ap()[:, 0:4]), "s_gd")
                ctl.inc_dma(g.dma_start(out=tb_tap[:, 2048:2560], in_=kfeat.ap()[:, 0, :]), "s_gd")
                ctl.inc_dma(g.dma_start(out=tb_tap[:, 2560:3072], in_=qfeatr.ap()[:, 0, :]), "s_gd")
                ctl.inc_dma(g.dma_start(out=tb_tap[:, 3072:3584], in_=hfeat.ap()[:, 0, 0, :]), "s_gd")
                ctl.inc_dma(g.dma_start(out=tb_tap[0:113, 3584:4096], in_=stacc.ap()[:, 0:512]), "s_gd")
            t_tap = ctl.tok("s_gd")
            prev_state_add = t_tap
            prev_tokuse = t_tap
            prev_hT_evac = [t_tap, t_tap]
            prev_projevac = t_tap
            prev_hfeat_use = [t_tap, t_tap]

    @block.gpsimd
    def _(g):
        ctl.wait(g, prev_state_add)
        toks["st_o"] = ctl.inc_dma(g.dma_start(out=cc_st_in, in_=stacc.ap()), "s_gd")
        ctl.wait(g, toks["st_o"])
        toks["ar_st"] = ctl.inc(g.collective_compute(
            "AllReduce", ALU.add, replica_groups=[[0, 1, 2, 3], [4, 5, 6, 7]],
            ins=[cc_st_in], outs=[cc_st_out]), "s_cc")
        ctl.wait(g, toks["ar_st"])
        toks["st_i"] = ctl.inc_dma(g.dma_start(out=stacc.ap(), in_=cc_st_out), "s_gd")

    for hh in range(HEADS):
        co = (hh // 3) * 273
        po = 32 * (hh % 3)
        bk = hh % 2
        idm = eyef.ap()[po:po + 17, po:po + 17]

        @block.tensor
        def _(te, hh=hh, co=co, po=po, bk=bk, idm=idm):
            ctl.waits(te, [toks["st_i"], bank_free[bk]], "tensor")
            src = stacc.ap()
            te.transpose(out=banks[bk].ap()[0:16, 0:17], in_=src[po:po + 17, co:co + 16],
                         identity=idm)
            te.transpose(out=banks[bk].ap()[0:128, 32:49], in_=src[po:po + 17, co + 16:co + 144],
                         identity=idm)
            te.transpose(out=banks[bk].ap()[0:128, 64:81], in_=src[po:po + 17, co + 144:co + 272],
                         identity=idm)
            i = te.transpose(out=banks[bk].ap()[0:1, 96:113], in_=src[po:po + 17, co + 272:co + 273],
                             identity=idm)
            ctl.inc(i, "s_te")
        t_sT = ctl.tok("s_te")

        @block.scalar
        def _(se, hh=hh, bk=bk, t_sT=t_sT):
            ctl.wait(se, t_sT, "scalar")
            se.activation(out=stA.ap()[:, hh, 0:1], in_=banks[bk].ap()[0:16, 16:17], func=AF.Copy)
            se.activation(out=stA.ap()[:, hh, 32:48], in_=banks[bk].ap()[0:16, 0:16], func=AF.Copy)
            se.activation(out=stB.ap()[:, hh, 0:1], in_=banks[bk].ap()[0:128, 48:49],
                          func=AF.Copy, scale=0.5)
            se.activation(out=stB.ap()[:, hh, 32:48], in_=banks[bk].ap()[0:128, 32:48],
                          func=AF.Copy, scale=0.5)
            se.activation(out=stC.ap()[:, hh, 0:1], in_=banks[bk].ap()[0:128, 80:81],
                          func=AF.Copy, scale=0.5)
            se.activation(out=stC.ap()[:, hh, 32:48], in_=banks[bk].ap()[0:128, 64:80],
                          func=AF.Copy, scale=0.5)
            se.activation(out=stD.ap()[0:1, hh, 0:1], in_=banks[bk].ap()[0:1, 112:113], func=AF.Copy)
            i = se.activation(out=stD.ap()[0:1, hh, 32:48], in_=banks[bk].ap()[0:1, 96:112],
                              func=AF.Copy)
            ctl.inc(i, "s_se")
        bank_free[bk] = ctl.tok("s_se")
    toks["stprep"] = ctl.tok("s_se")

    @block.vector
    def _(v):
        ctl.wait(v, toks["st_i"], "vector")
        i = v.memset(denr.ap(), 0.0)
        ctl.inc(i, "s_ve")

    prev_ofd_use = None
    prev_od = None
    prev_p2T = None
    for tt in range(TT):
        for hh in range(HEADS):
            tq = 2 + (hh % 2)
            ob = 4 + (hh % 2)

            @block.tensor
            def _(te, tt=tt, hh=hh):
                ctl.waits(te, [toks["stprep"], bank_free[7], prev_ofd_use], "tensor")
                for ci in range(4):
                    c = tt * 4 + ci
                    i = te.transpose(out=banks[7].ap().bitcast(BF16)[0:16, ci * 128:(ci + 1) * 128],
                                     in_=qtok.ap()[:, c, hh * 16:(hh + 1) * 16],
                                     identity=eyeb.ap())
                ctl.inc(i, "s_te")
            t_qfT = ctl.tok("s_te")

            @block.scalar
            def _(se, t_qfT=t_qfT, prev_od=prev_od):
                ctl.waits(se, [t_qfT, prev_od], "scalar")
                i = se.activation(out=qf16.ap(), in_=banks[7].ap().bitcast(BF16)[0:16, 0:512],
                                  func=AF.Copy)
                ctl.inc(i, "s_se")
            t_qf = ctl.tok("s_se")
            bank_free[7] = t_qf

            @block.vector
            def _(v, hh=hh, tt=tt, prev_p2T=prev_p2T):
                ctl.waits(v, [prev_p2T], "vector")
                for ci in range(4):
                    c = tt * 4 + ci
                    qs = qtok.ap()[:, c, hh * 16:(hh + 1) * 16]
                    i = v.tensor_tensor(
                        out=phi2q4.ap()[:, ci, :].rearrange("p (i j) -> p i j", i=16),
                        in0=qs.unsqueeze(2).broadcast_to([128, 16, 16]),
                        in1=qs.unsqueeze(1).broadcast_to([128, 16, 16]), op=ALU.mult)
                ctl.inc(i, "s_ve")
            t_p2 = ctl.tok("s_ve")

            @block.tensor
            def _(te, tq=tq, t_p2=t_p2):
                ctl.waits(te, [t_p2, bank_free[tq]], "tensor")
                bv = banks[tq].ap().bitcast(BF16)
                for ci in range(4):
                    te.transpose(out=bv[:, ci * 128:(ci + 1) * 128],
                                 in_=phi2q4.ap()[:, ci, 0:128], identity=eyeb.ap())
                    i = te.transpose(out=bv[:, 512 + ci * 128:512 + (ci + 1) * 128],
                                     in_=phi2q4.ap()[:, ci, 128:256], identity=eyeb.ap())
                ctl.inc(i, "s_te")
            prev_p2T = ctl.tok("s_te")

            @block.scalar
            def _(se, tq=tq, prev_p2T=prev_p2T, prev_od=prev_od):
                ctl.waits(se, [prev_p2T, prev_od], "scalar")
                bv = banks[tq].ap().bitcast(BF16)
                se.activation(out=phiqA.ap(), in_=bv[:, 0:512], func=AF.Copy)
                i = se.activation(out=phiqB.ap(), in_=bv[:, 512:1024], func=AF.Copy)
                ctl.inc(i, "s_se")
            bank_free[tq] = ctl.tok("s_se")
            t_phiq = ctl.tok("s_se")

            @block.tensor
            def _(te, hh=hh, tt=tt, ob=ob, t_phiq=t_phiq, t_qf=t_qf):
                ctl.waits(te, [t_phiq, t_qf, toks["onesb"], bank_free[ob]], "tensor")
                te.matmul(out=banks[ob].ap()[0:48, 0:512], lhsT=stA.ap()[:, hh, :],
                          rhs=qf16.ap(), start=True, stop=False)
                te.matmul(out=banks[ob].ap()[0:48, 0:512], lhsT=stB.ap()[:, hh, :],
                          rhs=phiqA.ap(), start=False, stop=False)
                te.matmul(out=banks[ob].ap()[0:48, 0:512], lhsT=stC.ap()[:, hh, :],
                          rhs=phiqB.ap(), start=False, stop=False)
                i = te.matmul(out=banks[ob].ap()[0:48, 0:512], lhsT=stD.ap()[:, hh, :],
                              rhs=onesb.ap(), start=False, stop=True)
                ctl.inc(i, "s_te")
            t_od = ctl.tok("s_te")
            prev_od = t_od

            @block.scalar
            def _(se, hh=hh, ob=ob, t_od=t_od):
                ctl.wait(se, t_od, "scalar")
                se.activation(out=denr.ap()[0:1, hh, :], in_=banks[ob].ap()[0:1, 0:512],
                              func=AF.Copy)
                i = se.activation(out=ofd.ap()[32:48, hh, :], in_=banks[ob].ap()[32:48, 0:512],
                                  func=AF.Copy)
                ctl.inc(i, "s_se")
            t_ode = ctl.tok("s_se")
            bank_free[ob] = t_ode
        t_odall = ctl.tok("s_se")

        @block.vector
        def _(v, t_odall=t_odall):
            ctl.wait(v, t_odall, "vector")
            v.tensor_scalar(out=denr2.ap()[0:1, :, :], in0=denr.ap()[0:1, :, :],
                            scalar1=1e-5, scalar2=None, op0=ALU.max)
            vsp(v)
            i = v.reciprocal(out=denr.ap()[0:1, :, :], in_=denr2.ap()[0:1, :, :])
            ctl.inc(i, "s_ve")
        t_den = ctl.tok("s_ve")

        qk_banks = (2, 3, 6, 7)
        for rnd in range(2):
            @block.tensor
            def _(te, rnd=rnd, t_den=t_den):
                ctl.waits(te, [t_den] + [bank_free[b] for b in qk_banks], "tensor")
                for hi in range(4):
                    hh = rnd * 4 + hi
                    for nh in range(2):
                        i = te.matmul(out=banks[qk_banks[hi]].ap()[32:48, nh * 256:(nh + 1) * 256],
                                      lhsT=onesf.ap()[0:8, 0:16],
                                      rhs=denr.ap()[0:8, hh, nh * 256:(nh + 1) * 256],
                                      start=True, stop=True)
                ctl.inc(i, "s_te")
            t_qkr = ctl.tok("s_te")

            @block.vector
            def _(v, rnd=rnd, t_qkr=t_qkr):
                ctl.wait(v, t_qkr, "vector")
                for hi in range(4):
                    hh = rnd * 4 + hi
                    i = v.tensor_tensor(out=ofd.ap()[32:48, hh, :],
                                        in0=ofd.ap()[32:48, hh, :],
                                        in1=banks[qk_banks[hi]].ap()[32:48, 0:512],
                                        op=ALU.mult)
                ctl.inc(i, "s_ve")
            for b in qk_banks:
                bank_free[b] = ctl.tok("s_ve")
        t_qkall = ctl.tok("s_ve")

        for ci in range(4):
            c = tt * 4 + ci
            b0, b1 = (0, 1) if ci % 2 == 0 else (2, 3)

            @block.tensor
            def _(te, c=c, ci=ci, b0=b0, b1=b1, t_qkall=t_qkall):
                ctl.waits(te, [t_qkall, toks["woutb"], bank_free[b0], bank_free[b1]], "tensor")
                for hh in range(HEADS):
                    te.matmul(out=banks[b0].ap()[:, 0:512],
                              lhsT=ofd.ap()[32:48, hh, ci * 128:(ci + 1) * 128],
                              rhs=woutb.ap()[32:48, hh, 0:512],
                              start=(hh == 0), stop=(hh == HEADS - 1))
                for hh in range(HEADS):
                    i = te.matmul(out=banks[b1].ap()[:, 0:512],
                                  lhsT=ofd.ap()[32:48, hh, ci * 128:(ci + 1) * 128],
                                  rhs=woutb.ap()[32:48, hh, 512:1024],
                                  start=(hh == 0), stop=(hh == HEADS - 1))
                ctl.inc(i, "s_te")
            t_op = ctl.tok("s_te")

            @block.vector
            def _(v, c=c, b0=b0, b1=b1, t_op=t_op):
                ctl.wait(v, t_op, "vector")
                v.tensor_tensor(out=xres.ap()[:, c, 0:512], in0=xres.ap()[:, c, 0:512],
                                in1=banks[b0].ap()[:, 0:512], op=ALU.add)
                i = v.tensor_tensor(out=xres.ap()[:, c, 512:1024], in0=xres.ap()[:, c, 512:1024],
                                    in1=banks[b1].ap()[:, 0:512], op=ALU.add)
                ctl.inc(i, "s_ve")
            bank_free[b0] = ctl.tok("s_ve")
            bank_free[b1] = ctl.tok("s_ve")
        prev_ofd_use = ctl.tok("s_te")
    toks["resid1"] = ctl.tok("s_ve")

    @block.gpsimd
    def _(g):
        toks["cb_row"] = ctl.inc_dma(g.dma_start(
            out=mrow.ap()[0:1, :], in_=cc_mod_out[None, 4096:5120]), "s_gd")

    for half in range(2):
        sl = slice(half * 512, (half + 1) * 512)
        ba = 4 if half == 0 else 5

        @block.tensor
        def _(te, sl=sl, ba=ba):
            ctl.waits(te, [toks["cb_row"], toks["resid1"], bank_free[ba]], "tensor")
            for nh in range(2):
                i = te.matmul(out=banks[ba].ap()[:, nh * 256:(nh + 1) * 256], lhsT=onesf.ap(),
                              rhs=mrow.ap()[0:8, sl.start + nh * 256:sl.start + (nh + 1) * 256],
                              start=True, stop=True)
            ctl.inc(i, "s_te")
        t_rep2 = ctl.tok("s_te")

        @block.scalar
        def _(se, sl=sl, ba=ba, t_rep2=t_rep2):
            ctl.wait(se, t_rep2, "scalar")
            i = se.activation(out=modrepA.ap()[:, sl], in_=banks[ba].ap()[:, 0:512],
                              func=AF.Copy, bias=1.0)
            ctl.inc(i, "s_se")
        bank_free[ba] = ctl.tok("s_se")
    t_cb_mm = ctl.tok("s_te")

    @block.gpsimd
    def _(g):
        ctl.wait(g, t_cb_mm)
        toks["sb_row"] = ctl.inc_dma(g.dma_start(
            out=mrow.ap()[0:1, :], in_=cc_mod_out[None, 3072:4096]), "s_gd")

    for half in range(2):
        sl = slice(half * 512, (half + 1) * 512)
        ba = 6 if half == 0 else 7

        @block.tensor
        def _(te, sl=sl, ba=ba):
            ctl.waits(te, [toks["sb_row"], bank_free[ba]], "tensor")
            for nh in range(2):
                i = te.matmul(out=banks[ba].ap()[:, nh * 256:(nh + 1) * 256], lhsT=onesf.ap(),
                              rhs=mrow.ap()[0:8, sl.start + nh * 256:sl.start + (nh + 1) * 256],
                              start=True, stop=True)
            ctl.inc(i, "s_te")
        t_rep2 = ctl.tok("s_te")

        @block.scalar
        def _(se, sl=sl, ba=ba, t_rep2=t_rep2):
            ctl.wait(se, t_rep2, "scalar")
            i = se.activation(out=modrepB.ap()[:, sl], in_=banks[ba].ap()[:, 0:512], func=AF.Copy)
            ctl.inc(i, "s_se")
        bank_free[ba] = ctl.tok("s_se")
    toks["modrep2"] = ctl.tok("s_se")

    prev_spill = [None, None]
    for c in range(TCH):
        @block.scalar
        def _(se, c=c):
            ctl.waits(se, [toks["resid1"], toks["modrep2"]], "scalar")
            se.activation(out=scratch.ap(), in_=xres.ap()[:, c, :], func=AF.Square,
                          accum_out=statssq.ap()[:, c:c + 1])
            i = se.activation(out=scratch.ap(), in_=xres.ap()[:, c, :], func=AF.Copy,
                              accum_out=statsum.ap()[:, c:c + 1])
            ctl.inc(i, "s_se")
        t_s2 = ctl.tok("s_se")

        @block.vector
        def _(v, c=c, t_s2=t_s2):
            ctl.wait(v, t_s2, "vector")
            sl = slice(c, c + 1)
            v.tensor_scalar(out=statmu.ap()[:, sl], in0=statsum.ap()[:, sl],
                            scalar1=1.0 / D, scalar2=None, op0=ALU.mult)
            vsp(v)
            v.tensor_scalar(out=statB.ap()[:, sl], in0=statmu.ap()[:, sl],
                            scalar1=statmu.ap()[:, sl], scalar2=None, op0=ALU.mult)
            vsp(v)
            v.scalar_tensor_tensor(out=statB.ap()[:, sl], in0=statssq.ap()[:, sl],
                                   scalar=1.0 / D, in1=statB.ap()[:, sl],
                                   op0=ALU.mult, op1=ALU.subtract)
            vsp(v)
            v.tensor_scalar(out=statB.ap()[:, sl], in0=statB.ap()[:, sl],
                            scalar1=1e-5, scalar2=None, op0=ALU.add)
            vsp(v)
            i = v.reciprocal(out=statrstd.ap()[:, sl], in_=statB.ap()[:, sl])
            ctl.inc(i, "s_ve")
        t_c2 = ctl.tok("s_ve")

        @block.scalar
        def _(se, c=c, t_c2=t_c2):
            ctl.wait(se, t_c2, "scalar")
            i = se.activation(out=statrstd.ap()[:, c:c + 1], in_=statrstd.ap()[:, c:c + 1],
                              func=AF.Sqrt)
            ctl.inc(i, "s_se")
        t_r2 = ctl.tok("s_se")

        @block.vector
        def _(v, c=c, t_r2=t_r2):
            ctl.wait(v, t_r2, "vector")
            sl = slice(c, c + 1)
            i = v.scalar_tensor_tensor(out=statB.ap()[:, sl], in0=statmu.ap()[:, sl],
                                       scalar=-1.0, in1=statrstd.ap()[:, sl],
                                       op0=ALU.mult, op1=ALU.mult)
            ctl.inc(i, "s_ve")
        t_B2 = ctl.tok("s_ve")

        if True:
            @block.vector
            def _(v, c=c, t_B2=t_B2):
                ctl.waits(v, [t_B2, prev_spill[0]], "vector")
                hv = hq.ap()
                vsp(v)
                v.tensor_scalar(out=hv, in0=xres.ap()[:, c, :],
                                scalar1=statrstd.ap()[:, c:c + 1],
                                scalar2=statB.ap()[:, c:c + 1],
                                op0=ALU.mult, op1=ALU.add)
                vsp(v)
                v.tensor_tensor(out=hv, in0=hv, in1=modrepA.ap(), op=ALU.mult)
                vsp(v)
                v.tensor_tensor(out=hv, in0=hv, in1=modrepB.ap(), op=ALU.add)
                vsp(v)
                v.tensor_scalar(out=scratch.ap(), in0=hv, scalar1=0.0, scalar2=None,
                                op0=ALU.bypass, op1=ALU.max, accum_out=maxs.ap()[:, c:c + 1])
                i = v.tensor_scalar(out=scratch.ap(), in0=hv, scalar1=0.0, scalar2=None,
                                    op0=ALU.bypass, op1=ALU.min, accum_out=mins.ap()[:, c:c + 1])
                ctl.inc(i, "s_ve")
            t_h2 = ctl.tok("s_ve")

            @block.sync
            def _(sync, c=c, t_h2=t_h2):
                ctl.wait(sync, t_h2)
                toks[("spill", c)] = ctl.inc_dma(sync.dma_start(
                    out=h_spill[c, :, :], in_=hq.ap()), "s_ld")
            prev_spill[0] = toks[("spill", c)]
    t_ln2 = ctl.tok("s_ve")

    @block.vector
    def _(v):
        ctl.wait(v, t_ln2)
        vsp(v, 4)
        v.tensor_reduce(out=hmaxcol.ap()[:, 0:1], in_=maxs.ap(), axis=AX.X, op=ALU.max)
        i = v.tensor_reduce(out=hmaxcol.ap()[:, 1:2], in_=mins.ap(), axis=AX.X, op=ALU.min)
        ctl.inc(i, "s_ve")
    t_gmr = ctl.tok("s_ve")

    @block.scalar
    def _(se):
        ctl.wait(se, t_gmr, "scalar")
        i = se.activation(out=hmaxcol.ap()[:, 1:2], in_=hmaxcol.ap()[:, 1:2],
                          func=AF.Copy, scale=-1.0)
        ctl.inc(i, "s_se")
    t_gmse = ctl.tok("s_se")

    @block.vector
    def _(v):
        ctl.wait(v, t_gmse, "vector")
        i = v.tensor_tensor(out=hmaxcol.ap()[:, 0:1], in0=hmaxcol.ap()[:, 0:1],
                            in1=hmaxcol.ap()[:, 1:2], op=ALU.max)
        ctl.inc(i, "s_ve")
    t_gm = ctl.tok("s_ve")

    @block.tensor
    def _(te):
        ctl.waits(te, [t_gm, bank_free[6]], "tensor")
        i = te.transpose(out=banks[6].ap()[0:1, 0:128], in_=hmaxcol.ap()[:, 0:1],
                         identity=eyef.ap())
        ctl.inc(i, "s_te")
    t_gmT = ctl.tok("s_te")

    @block.vector
    def _(v):
        ctl.wait(v, t_gmT, "vector")
        i = v.tensor_reduce(out=cells.ap()[0:1, 4:5], in_=banks[6].ap()[0:1, 0:128],
                            axis=AX.X, op=ALU.max)
        ctl.inc(i, "s_ve")
    t_gmax = ctl.tok("s_ve")
    bank_free[6] = t_gmax

    @block.gpsimd
    def _(g):
        ctl.wait(g, t_gmax)
        toks["m1o"] = ctl.inc_dma(g.dma_start(out=cc_m1_in[None, :], in_=cells.ap()[0:1, 4:5]), "s_gd")
        ctl.wait(g, toks["m1o"])
        toks["ar_m1"] = ctl.inc(g.collective_compute(
            "AllReduce", ALU.max, replica_groups=[list(range(8))],
            ins=[cc_m1_in], outs=[cc_m1_out]), "s_cc")
        ctl.wait(g, toks["ar_m1"])
        toks["m1i"] = ctl.inc_dma(g.dma_start(out=cells.ap()[0:1, 5:6], in_=cc_m1_out[None, :]), "s_gd")

    @block.vector
    def _(v):
        ctl.wait(v, toks["m1i"], "vector")
        v.scalar_tensor_tensor(out=cells.ap()[0:1, 6:7], in0=cells.ap()[0:1, 2:3],
                               scalar=1.0 / 127.0, in1=cells.ap()[0:1, 5:6],
                               op0=ALU.mult, op1=ALU.mult)
        vsp(v)
        i = v.reciprocal(out=cells.ap()[0:1, 7:8], in_=cells.ap()[0:1, 6:7])
        ctl.inc(i, "s_ve")
    t_s1c = ctl.tok("s_ve")

    @block.tensor
    def _(te):
        ctl.waits(te, [t_s1c, bank_free[7]], "tensor")
        te.matmul(out=banks[7].ap()[:, 0:1], lhsT=onesf.ap(), rhs=cells.ap()[0:8, 7:8],
                  start=True, stop=True)
        i = te.matmul(out=banks[7].ap()[:, 1:2], lhsT=onesf.ap(), rhs=cells.ap()[0:8, 6:7],
                      start=True, stop=True)
        ctl.inc(i, "s_te")
    t_bc1 = ctl.tok("s_te")

    @block.scalar
    def _(se):
        ctl.wait(se, t_bc1, "scalar")
        i = se.activation(out=gcol.ap()[:, 0:2], in_=banks[7].ap()[:, 0:2], func=AF.Copy)
        ctl.inc(i, "s_se")
    toks["rs1"] = ctl.tok("s_se")
    bank_free[7] = toks["rs1"]

    @block.gpsimd
    def _(g):
        ctl.wait(g, toks["ag_sg"])
        toks["sgrows"] = ctl.inc_dma(g.dma_start(
            out=scrows.ap(), in_=cc_sg_out.rearrange("(a b) -> a b", a=22)), "s_gd")

    @block.tensor
    def _(te):
        ctl.waits(te, [toks["sgrows"], bank_free[6]], "tensor")
        i = te.transpose(out=banks[6].ap()[:, 0:22], in_=scrows.ap(),
                         identity=eyef.ap()[0:22, 0:22])
        ctl.inc(i, "s_te")
    t_sgT = ctl.tok("s_te")

    @block.scalar
    def _(se):
        ctl.wait(se, t_sgT, "scalar")
        i = se.activation(out=sgcol.ap(), in_=banks[6].ap()[:, 0:22], func=AF.Copy)
        ctl.inc(i, "s_se")
    t_sgc = ctl.tok("s_se")
    bank_free[6] = t_sgc

    @block.gpsimd
    def _(g):
        ctl.waits(g, [toks["ag_su"], t_sgc])
        toks["surows"] = ctl.inc_dma(g.dma_start(
            out=scrows.ap(), in_=cc_su_out.rearrange("(a b) -> a b", a=22)), "s_gd")

    @block.tensor
    def _(te):
        ctl.waits(te, [toks["surows"], bank_free[6]], "tensor")
        i = te.transpose(out=banks[6].ap()[:, 0:22], in_=scrows.ap(),
                         identity=eyef.ap()[0:22, 0:22])
        ctl.inc(i, "s_te")
    t_suT = ctl.tok("s_te")

    @block.scalar
    def _(se):
        ctl.wait(se, t_suT, "scalar")
        i = se.activation(out=sucol.ap(), in_=banks[6].ap()[:, 0:22], func=AF.Copy)
        ctl.inc(i, "s_se")
    t_suc = ctl.tok("s_se")
    bank_free[6] = t_suc

    @block.gpsimd
    def _(g):
        ctl.waits(g, [toks["ag_sd"], t_suc])
        toks["sdrow"] = ctl.inc_dma(g.dma_start(out=sdrow.ap()[0:1, :], in_=cc_sd_out[None, :]), "s_gd")

    @block.vector
    def _(v):
        ctl.waits(v, [t_sgc, t_suc, toks["rs1"]], "vector")
        v.tensor_scalar(out=sgxcol.ap(), in0=sgcol.ap(), scalar1=gcol.ap()[:, 1:2],
                        scalar2=None, op0=ALU.mult)
        i = v.tensor_scalar(out=suxcol.ap(), in0=sucol.ap(), scalar1=gcol.ap()[:, 1:2],
                            scalar2=None, op0=ALU.mult)
        ctl.inc(i, "s_ve")
    toks["sgx"] = ctl.tok("s_ve")

    final_spill = toks[("spill", TCH - 1)]
    prev_stt = None
    prev_xT = None
    for c in range(TCH):
        @block.sync
        def _(sync, c=c, prev_stt=prev_stt):
            ctl.waits(sync, [toks[("spill", c)], final_spill, prev_stt])
            toks[("rld", c)] = ctl.inc_dma(sync.dma_start(
                out=hq.ap(), in_=h_spill[c, :, :]), "s_ld")

        @block.scalar
        def _(se, c=c):
            ctl.waits(se, [toks[("rld", c)], toks["rs1"]], "scalar")
            i = se.activation(out=xqf.ap(), in_=hq.ap(), func=AF.Copy,
                              scale=gcol.ap()[:, 0:1])
            ctl.inc(i, "s_se")
        t_uq = ctl.tok("s_se")

        if QUANT_MODE == "cast":
            @block.vector
            def _(v, t_uq=t_uq, prev_xT=prev_xT):
                ctl.waits(v, [t_uq, prev_xT], "vector")
                v.tensor_copy(out=xq8.ap(), in_=xqf.ap())
                vsp(v)
                v.tensor_copy(out=nxq.ap(), in_=xq8.ap())
                i = vflush(v)
                ctl.inc(i, "s_ve")
        else:
            @block.vector
            def _(v, t_uq=t_uq, prev_xT=prev_xT):
                ctl.waits(v, [t_uq, prev_xT], "vector")
                v.tensor_copy(out=xq16.ap(), in_=xqf.ap())
                vsp(v)
                v.tensor_copy(out=hq.ap(), in_=xq16.ap())
                vsp(v)
                v.tensor_tensor(out=scratch.ap(), in0=xqf.ap(), in1=hq.ap(),
                                op=ALU.subtract)
                vsp(v)
                v.tensor_scalar(out=xqf.ap(), in0=scratch.ap(), scalar1=0.5, scalar2=None,
                                op0=ALU.is_ge)
                v.tensor_scalar(out=scratch.ap(), in0=scratch.ap(), scalar1=-0.5,
                                scalar2=-1.0, op0=ALU.is_ge, op1=ALU.add)
                vsp(v)
                v.tensor_tensor(out=xqf.ap(), in0=xqf.ap(), in1=scratch.ap(), op=ALU.add)
                vsp(v)
                v.tensor_tensor(out=hq.ap(), in0=hq.ap(), in1=xqf.ap(), op=ALU.add)
                vsp(v)
                v.tensor_scalar(out=nxq.ap(), in0=hq.ap(), scalar1=-128.0, scalar2=127.0,
                                op0=ALU.max, op1=ALU.min)
                i = vflush(v)
                ctl.inc(i, "s_ve")
        t_nx = ctl.tok("s_ve")
        prev_stt = t_nx

        @block.tensor
        def _(te, c=c, t_nx=t_nx):
            ctl.waits(te, [t_nx, bank_free[0], bank_free[1]], "tensor")
            for dch in range(DCH):
                bk = 0 if dch < 4 else 1
                bv = banks[bk].ap().bitcast(BF16)
                i = te.transpose(out=bv[:, (dch % 4) * 128:(dch % 4 + 1) * 128],
                                 in_=nxq.ap()[:, dch * 128:(dch + 1) * 128], identity=eyeb.ap())
            ctl.inc(i, "s_te")
        t_xT = ctl.tok("s_te")
        prev_xT = t_xT

        @block.scalar
        def _(se, c=c, t_xT=t_xT):
            ctl.wait(se, t_xT, "scalar")
            for dch in range(DCH):
                bk = 0 if dch < 4 else 1
                bv = banks[bk].ap().bitcast(BF16)
                i = se.activation(out=xqfeat.ap()[:, dch, c * 128:(c + 1) * 128],
                                  in_=bv[:, (dch % 4) * 128:(dch % 4 + 1) * 128], func=AF.Copy)
            ctl.inc(i, "s_se")
        bank_free[0] = ctl.tok("s_se")
        bank_free[1] = ctl.tok("s_se")
    toks["xqfeat"] = ctl.tok("s_se")

    sblocks = [(s * 4, min(ICH, s * 4 + 4)) for s in range(6)]
    gu_banks = [(2, 3), (4, 5), (6, 7)]
    prev_h2spill = [None, None]
    prev_ttr = None
    wbuf_free = [None, None]
    sb_ld = {}
    for sbi, (i0, i1) in enumerate(sblocks):
        wb = sbi % 2
        ncols = (i1 - i0) * 128

        @block.sync
        def _(sync, sbi=sbi, i0=i0, i1=i1, wb=wb, ncols=ncols):
            ctl.waits(sync, [toks["ag_wg"], toks["ag_wu"], wbuf_free[wb]])
            gi0, gi1 = i0 * 128, i1 * 128
            for c in range(8):
                s = max(gi0, c * 352)
                e = min(gi1, (c + 1) * 352)
                if s >= e:
                    continue
                for buf, cc_o in ((wgs, cc_wg_out), (wus, cc_wu_out)):
                    ctl.inc_dma(sync.dma_start(
                        out=buf.ap()[:, wb, :, s - gi0:e - gi0],
                        in_=cc_o[c * D:(c + 1) * D, s - c * 352:e - c * 352].rearrange(
                            "(dch p) w -> p dch w", p=128)), "s_ld")
            sb_ld[sbi] = ctl.tok("s_ld")

        for ii in range(i0, i1):
            io = (ii - i0) * 128
            for tt in range(TT):
                gb, ub = gu_banks[(ii * TT + tt) % 3]

                @block.tensor
                def _(te, sbi=sbi, io=io, tt=tt, gb=gb, ub=ub, wb=wb):
                    ctl.waits(te, [sb_ld[sbi], toks["xqfeat"], bank_free[gb], bank_free[ub]],
                              "tensor")
                    for dch in range(DCH):
                        te.matmul(out=banks[gb].ap()[:, 0:512],
                                  lhsT=wgs.ap()[:, wb, dch, io:io + 128],
                                  rhs=xqfeat.ap()[:, dch, tt * 512:(tt + 1) * 512],
                                  start=(dch == 0), stop=(dch == DCH - 1))
                    for dch in range(DCH):
                        i = te.matmul(out=banks[ub].ap()[:, 0:512],
                                      lhsT=wus.ap()[:, wb, dch, io:io + 128],
                                      rhs=xqfeat.ap()[:, dch, tt * 512:(tt + 1) * 512],
                                      start=(dch == 0), stop=(dch == DCH - 1))
                    ctl.inc(i, "s_te")
                t_gu = ctl.tok("s_te")

                @block.scalar
                def _(se, ii=ii, gb=gb, t_gu=t_gu, prev_ttr=prev_ttr):
                    ctl.waits(se, [t_gu, toks["sgx"], prev_ttr], "scalar")
                    i = se.activation(out=silut.ap(), in_=banks[gb].ap()[:, 0:512],
                                      func=AF.Silu, scale=sgxcol.ap()[:, ii:ii + 1])
                    ctl.inc(i, "s_se")
                t_si = ctl.tok("s_se")
                bank_free[gb] = t_si

                hb2 = (ii * TT + tt) % 2

                @block.vector
                def _(v, ii=ii, tt=tt, ub=ub, hb2=hb2, t_gu=t_gu, t_si=t_si):
                    ctl.waits(v, [t_gu, t_si, prev_h2spill[hb2]], "vector")
                    v.tensor_scalar(out=upt.ap(), in0=banks[ub].ap()[:, 0:512],
                                    scalar1=suxcol.ap()[:, ii:ii + 1], scalar2=None,
                                    op0=ALU.mult)
                    vsp(v)
                    v.tensor_tensor(out=h2st.ap()[:, hb2, :], in0=silut.ap(), in1=upt.ap(),
                                    op=ALU.mult)
                    vsp(v)
                    v.tensor_scalar(out=scratch.ap()[:, 0:512], in0=h2st.ap()[:, hb2, :],
                                    scalar1=0.0, scalar2=None, op0=ALU.bypass, op1=ALU.max,
                                    accum_out=statmu.ap()[:, 0:1])
                    v.tensor_scalar(out=scratch.ap()[:, 0:512],
                                    in0=h2st.ap()[:, hb2, :],
                                    scalar1=0.0, scalar2=None, op0=ALU.bypass, op1=ALU.min,
                                    accum_out=statmu.ap()[:, 1:2])
                    vsp(v)
                    v.tensor_tensor(out=hmaxcol.ap()[:, 0:1], in0=hmaxcol.ap()[:, 0:1],
                                    in1=statmu.ap()[:, 0:1], op=ALU.max)
                    v.tensor_scalar(out=statmu.ap()[:, 1:2], in0=statmu.ap()[:, 1:2],
                                    scalar1=-1.0, scalar2=None, op0=ALU.mult)
                    vsp(v)
                    i = v.tensor_tensor(out=hmaxcol.ap()[:, 1:2], in0=hmaxcol.ap()[:, 1:2],
                                        in1=statmu.ap()[:, 1:2], op=ALU.max)
                    ctl.inc(i, "s_ve")
                prev_ttr = ctl.tok("s_ve")
                t_q = ctl.tok("s_ve")
                bank_free[ub] = t_q

                @block.gpsimd
                def _(g, ii=ii, tt=tt, hb2=hb2, t_q=t_q):
                    ctl.wait(g, t_q)
                    toks[("h2sp", ii, tt)] = ctl.inc_dma(g.dma_start(
                        out=h2_spill[ii, :, tt * 512:(tt + 1) * 512],
                        in_=h2st.ap()[:, hb2, :]), "s_gd")
                prev_h2spill[hb2] = toks[("h2sp", ii, tt)]
            toks[("h2sp", ii)] = toks[("h2sp", ii, TT - 1)]
        wbuf_free[wb] = ctl.tok("s_te")
    toks["gu_done"] = ctl.tok("s_te")

    @block.vector
    def _(v):
        ctl.wait(v, ctl.tok("s_ve"))
        vsp(v, 4)
        i = v.tensor_tensor(out=hmaxcol.ap()[:, 0:1], in0=hmaxcol.ap()[:, 0:1],
                            in1=hmaxcol.ap()[:, 1:2], op=ALU.max)
        ctl.inc(i, "s_ve")
    t_gm2 = ctl.tok("s_ve")

    @block.tensor
    def _(te):
        ctl.waits(te, [t_gm2, bank_free[0]], "tensor")
        i = te.transpose(out=banks[0].ap()[0:1, 0:128], in_=hmaxcol.ap()[:, 0:1],
                         identity=eyef.ap())
        ctl.inc(i, "s_te")
    t_gm2T = ctl.tok("s_te")

    @block.vector
    def _(v):
        ctl.wait(v, t_gm2T, "vector")
        i = v.tensor_reduce(out=cells.ap()[0:1, 8:9], in_=banks[0].ap()[0:1, 0:128],
                            axis=AX.X, op=ALU.max)
        ctl.inc(i, "s_ve")
    t_gmax2 = ctl.tok("s_ve")
    bank_free[0] = t_gmax2

    @block.gpsimd
    def _(g):
        ctl.wait(g, t_gmax2)
        toks["m2o"] = ctl.inc_dma(g.dma_start(out=cc_m2_in[None, :], in_=cells.ap()[0:1, 8:9]), "s_gd")
        ctl.wait(g, toks["m2o"])
        toks["ar_m2"] = ctl.inc(g.collective_compute(
            "AllReduce", ALU.max, replica_groups=[list(range(8))],
            ins=[cc_m2_in], outs=[cc_m2_out]), "s_cc")
        ctl.wait(g, toks["ar_m2"])
        toks["m2i"] = ctl.inc_dma(g.dma_start(out=cells.ap()[0:1, 9:10], in_=cc_m2_out[None, :]), "s_gd")

    @block.vector
    def _(v):
        ctl.wait(v, toks["m2i"], "vector")
        v.scalar_tensor_tensor(out=cells.ap()[0:1, 10:11], in0=cells.ap()[0:1, 3:4],
                               scalar=1.0 / 127.0, in1=cells.ap()[0:1, 9:10],
                               op0=ALU.mult, op1=ALU.mult)
        vsp(v)
        i = v.reciprocal(out=cells.ap()[0:1, 11:12], in_=cells.ap()[0:1, 10:11])
        ctl.inc(i, "s_ve")
    t_s2c = ctl.tok("s_ve")

    @block.tensor
    def _(te):
        ctl.waits(te, [t_s2c, bank_free[1]], "tensor")
        i = te.matmul(out=banks[1].ap()[:, 0:1], lhsT=onesf.ap(), rhs=cells.ap()[0:8, 11:12],
                      start=True, stop=True)
        ctl.inc(i, "s_te")
    t_bc2 = ctl.tok("s_te")

    @block.scalar
    def _(se):
        ctl.wait(se, t_bc2, "scalar")
        i = se.activation(out=gcol.ap()[:, 2:3], in_=banks[1].ap()[:, 0:1], func=AF.Copy)
        ctl.inc(i, "s_se")
    toks["rs2"] = ctl.tok("s_se")
    bank_free[1] = toks["rs2"]

    @block.gpsimd
    def _(g):
        toks["gb_row"] = ctl.inc_dma(g.dma_start(
            out=garow.ap()[0:1, :], in_=cc_mod_out[None, 5120:6144]), "s_gd")

    @block.scalar
    def _(se):
        ctl.wait(se, toks["gb_row"], "scalar")
        se.activation(out=scratch.ap()[0:1, 0:D], in_=garow.ap()[0:1, :], func=AF.Sign)
        i = se.activation(out=garow.ap()[0:1, :], in_=garow.ap()[0:1, :], func=AF.Abs)
        ctl.inc(i, "s_se")
    t_gbc = ctl.tok("s_se")

    @block.vector
    def _(v):
        ctl.waits(v, [t_gbc, toks["sdrow"], t_s2c], "vector")
        v.tensor_scalar(out=garow.ap()[0:1, :], in0=garow.ap()[0:1, :],
                        scalar1=0.25, scalar2=None, op0=ALU.max)
        vsp(v)
        v.tensor_tensor(out=garow.ap()[0:1, :], in0=garow.ap()[0:1, :],
                        in1=scratch.ap()[0:1, 0:D], op=ALU.mult)
        vsp(v)
        v.tensor_tensor(out=sdrow.ap()[0:1, :], in0=sdrow.ap()[0:1, :],
                        in1=garow.ap()[0:1, :], op=ALU.mult)
        vsp(v)
        i = v.tensor_scalar(out=sdrow.ap()[0:1, :], in0=sdrow.ap()[0:1, :],
                            scalar1=cells.ap()[0:1, 10:11], scalar2=None, op0=ALU.mult)
        ctl.inc(i, "s_ve")
    t_gbsd = ctl.tok("s_ve")

    t_gb_mm = []
    @block.tensor
    def _(te):
        ctl.waits(te, [t_gbsd, bank_free[2], bank_free[3]], "tensor")
        for nh in range(2):
            ns = slice(nh * 256, (nh + 1) * 256)
            te.matmul(out=banks[2].ap()[:, ns], lhsT=onesf.ap(),
                      rhs=sdrow.ap()[0:8, nh * 256:(nh + 1) * 256], start=True, stop=True)
            i = te.matmul(out=banks[3].ap()[:, ns], lhsT=onesf.ap(),
                          rhs=sdrow.ap()[0:8, 512 + nh * 256:512 + (nh + 1) * 256],
                          start=True, stop=True)
        ctl.inc(i, "s_te")
    t_gr = ctl.tok("s_te")

    @block.scalar
    def _(se):
        ctl.wait(se, t_gr, "scalar")
        se.activation(out=modrepA.ap()[:, 0:512], in_=banks[2].ap()[:, 0:512], func=AF.Copy)
        i = se.activation(out=modrepA.ap()[:, 512:1024], in_=banks[3].ap()[:, 0:512], func=AF.Copy)
        ctl.inc(i, "s_se")
    toks["gbsd"] = ctl.tok("s_se")
    bank_free[2] = toks["gbsd"]
    bank_free[3] = toks["gbsd"]

    prev_wd = [toks["gu_done"], toks["gu_done"]]
    prev_rld2 = [toks["gu_done"], toks["gu_done"]]
    for q in range(TT):
        for ii in range(ICH):
            rb = ii % 2
            wb = ii % 2

            @block.sync
            def _(sync, q=q, ii=ii, wb=wb):
                ctl.waits(sync, [toks["ag_wd"], prev_wd[wb]])
                toks[("wdld", q, ii)] = ctl.inc_dma(sync.dma_start(
                    out=wdTs.ap()[:, wb, :].rearrange("p (c d) -> p c d", d=128),
                    in_=cc_wd_out.rearrange("(c r) d -> r c d", r=II)[
                        ii * 128:(ii + 1) * 128, :, :]), "s_ld")

            @block.scalar
            def _(se, q=q, ii=ii, rb=rb):
                ctl.waits(se, [toks[("h2sp", ii, q)], prev_rld2[rb]], "scalar")
                t_r = ctl.inc_dma(se.dma_start(
                    out=rld.ap()[:, rb, :], in_=h2_spill[ii, :, q * 512:(q + 1) * 512]), "s_sc")
                ctl.wait(se, t_r, "scalar")
                ctl.wait(se, toks["rs2"], "scalar")
                i = se.activation(out=x2f.ap()[:, rb, :], in_=rld.ap()[:, rb, :],
                                  func=AF.Copy, scale=gcol.ap()[:, 2:3])
                ctl.inc(i, "s_se")
            t_u2q = ctl.tok("s_se")

            @block.vector
            def _(v, rb=rb, t_u2q=t_u2q, prev_mm=prev_wd[rb]):
                ctl.waits(v, [t_u2q, prev_mm], "vector")
                if QUANT_MODE == "cast":
                    v.tensor_copy(out=x28.ap()[:, rb, :], in_=x2f.ap()[:, rb, :])
                    vsp(v)
                    v.tensor_copy(out=nxq2.ap()[:, rb, :], in_=x28.ap()[:, rb, :])
                    i = vflush(v)
                else:
                    v.tensor_copy(out=x216.ap()[:, rb, :], in_=x2f.ap()[:, rb, :])
                    vsp(v)
                    v.tensor_copy(out=rld.ap()[:, rb, :], in_=x216.ap()[:, rb, :])
                    vsp(v)
                    v.tensor_tensor(out=u2.ap()[:, rb, :], in0=x2f.ap()[:, rb, :],
                                    in1=rld.ap()[:, rb, :], op=ALU.subtract)
                    vsp(v)
                    v.tensor_scalar(out=x2g.ap()[:, rb, :], in0=u2.ap()[:, rb, :],
                                    scalar1=0.5, scalar2=None, op0=ALU.is_ge)
                    v.tensor_scalar(out=u2.ap()[:, rb, :], in0=u2.ap()[:, rb, :],
                                    scalar1=-0.5, scalar2=-1.0, op0=ALU.is_ge, op1=ALU.add)
                    vsp(v)
                    v.tensor_tensor(out=x2g.ap()[:, rb, :], in0=x2g.ap()[:, rb, :],
                                    in1=u2.ap()[:, rb, :], op=ALU.add)
                    vsp(v)
                    v.tensor_tensor(out=rld.ap()[:, rb, :], in0=rld.ap()[:, rb, :],
                                    in1=x2g.ap()[:, rb, :], op=ALU.add)
                    vsp(v)
                    v.tensor_scalar(out=nxq2.ap()[:, rb, :], in0=rld.ap()[:, rb, :],
                                    scalar1=-128.0, scalar2=127.0, op0=ALU.max, op1=ALU.min)
                    i = vflush(v)
                ctl.inc(i, "s_ve")
            t_nx2 = ctl.tok("s_ve")
            prev_rld2[rb] = t_nx2

            @block.tensor
            def _(te, q=q, ii=ii, rb=rb, wb=wb, t_nx2=t_nx2):
                ctl.waits(te, [t_nx2, toks[("wdld", q, ii)]], "tensor")
                if ii == 0:
                    for b in range(8):
                        ctl.wait(te, bank_free[b], "tensor")
                for tc in range(4):
                    for hv in range(2):
                        i = te.matmul(out=banks[tc * 2 + hv].ap()[:, 0:512],
                                      lhsT=nxq2.ap()[:, rb, tc * 128:(tc + 1) * 128],
                                      rhs=wdTs.ap()[:, wb, hv * 512:(hv + 1) * 512],
                                      start=(ii == 0), stop=(ii == ICH - 1))
                ctl.inc(i, "s_te")
            prev_wd[wb] = ctl.tok("s_te")
        t_down = ctl.tok("s_te")

        for tc in range(4):
            c = q * 4 + tc

            @block.vector
            def _(v, tc=tc, c=c, t_down=t_down):
                ctl.waits(v, [t_down, toks["gbsd"]], "vector")
                for hv in range(2):
                    sl = slice(hv * 512, (hv + 1) * 512)
                    v.tensor_tensor(out=scratch.ap()[:, sl], in0=banks[tc * 2 + hv].ap()[:, 0:512],
                                    in1=modrepA.ap()[:, sl], op=ALU.mult)
                vsp(v, 1)
                for hv in range(2):
                    sl = slice(hv * 512, (hv + 1) * 512)
                    i = v.tensor_tensor(out=xres.ap()[:, c, sl], in0=xres.ap()[:, c, sl],
                                        in1=scratch.ap()[:, sl], op=ALU.add)
                ctl.inc(i, "s_ve")
            t_res = ctl.tok("s_ve")
            bank_free[tc * 2] = t_res
            bank_free[tc * 2 + 1] = t_res

            @block.gpsimd
            def _(g, c=c, t_res=t_res):
                ctl.wait(g, t_res)
                ctl.inc_dma(g.dma_start(
                    out=y_out.rearrange("(c p) d -> p c d", p=128)[:, c, :],
                    in_=xres.ap()[:, c, :]), "s_gd")

    if DEBUG_TAP == "h":
        @block.gpsimd
        def _(g):
            g.wait_ge(ctl.sems["s_gd"], ctl.cnt["s_gd"])
            ctl.wait(g, ctl.tok("s_ld"))
            ctl.inc_dma(g.dma_start(
                out=y_out.rearrange("(c p) d -> c p d", p=128), in_=h_spill), "s_gd")
    elif DEBUG_TAP == "mod":
        @block.gpsimd
        def _(g):
            g.wait_ge(ctl.sems["s_gd"], ctl.cnt["s_gd"])
            ctl.wait(g, ctl.tok("s_ld"))
            ctl.inc_dma(g.dma_start(
                out=y_out[0:6, :], in_=cc_mod_out.rearrange("(v d) -> v d", v=6)), "s_gd")
    elif DEBUG_TAP == "qtok":
        @block.gpsimd
        def _(g):
            g.wait_ge(ctl.sems["s_gd"], ctl.cnt["s_gd"])
            ctl.wait(g, ctl.tok("s_ld"))
            ctl.inc_dma(g.dma_start(
                out=y_out.rearrange("(c p) d -> p c d", p=128)[:, 0:1, :],
                in_=qtok.ap().bitcast(F32)[:, 0:16, :].rearrange("p c d -> p (c d)")[:, 0:1024][:, None, :]), "s_gd")
    elif DEBUG_TAP == "prep":
        @block.gpsimd
        def _(g):
            g.wait_ge(ctl.sems["s_gd"], ctl.cnt["s_gd"])
            ctl.wait(g, ctl.tok("s_ld"))
            ctl.inc_dma(g.dma_start(out=y_out[0:128, 0:16], in_=gamcol.ap()), "s_gd")
            ctl.inc_dma(g.dma_start(out=y_out[128:256, 0:4],
                                    in_=siluct.ap().bitcast(F32)[:, 0:4]), "s_gd")
            ctl.inc_dma(g.dma_start(out=y_out[256:384, 0:64],
                                    in_=wqb.ap().bitcast(F32)[:, 0, 0:64]), "s_gd")
            ctl.inc_dma(g.dma_start(out=y_out[384:512, 0:1024], in_=modrepA.ap()), "s_gd")
            ctl.inc_dma(g.dma_start(out=y_out[512:640, 0:1024], in_=modrepB.ap()), "s_gd")
            ctl.inc_dma(g.dma_start(out=y_out[640:656, 0:512],
                                    in_=woutb.ap().bitcast(F32)[32:48 if False else slice(32, 48), :, 0:512].rearrange("p h d -> p (h d)")[:, 0:1024] if False else woutb.ap().bitcast(F32)[32:48, 0, 0:512]), "s_gd")
    elif DEBUG_TAP == "stats":
        @block.gpsimd
        def _(g):
            g.wait_ge(ctl.sems["s_gd"], ctl.cnt["s_gd"])
            ctl.wait(g, ctl.tok("s_ld"))
            ctl.inc_dma(g.dma_start(out=y_out[0:128, 0:TCH], in_=statsum.ap()), "s_gd")
            ctl.inc_dma(g.dma_start(out=y_out[128:256, 0:TCH], in_=statssq.ap()), "s_gd")
            ctl.inc_dma(g.dma_start(out=y_out[256:384, 0:TCH], in_=statrstd.ap()), "s_gd")
            ctl.inc_dma(g.dma_start(out=y_out[384:512, 0:TCH], in_=statB.ap()), "s_gd")
            ctl.inc_dma(g.dma_start(out=y_out[512:640, 0:TCH], in_=statr.ap()), "s_gd")
    elif DEBUG_TAP == "stacc":
        @block.gpsimd
        def _(g):
            g.wait_ge(ctl.sems["s_gd"], ctl.cnt["s_gd"])
            ctl.wait(g, ctl.tok("s_ld"))
            ctl.inc_dma(g.dma_start(
                out=y_out[0:113, 0:819], in_=stacc.ap()), "s_gd")
    elif DEBUG_TAP == "ph2":
        @block.gpsimd
        def _(g):
            g.wait_ge(ctl.sems["s_gd"], ctl.cnt["s_gd"])
            ctl.wait(g, ctl.tok("s_ld"))
            ctl.inc_dma(g.dma_start(
                out=y_out.rearrange("(a p) d -> a p d", p=128)[0:4],
                in_=tb_tap.rearrange("p (a d) -> a p d", d=1024)), "s_gd")
    elif DEBUG_TAP == "ffn":
        @block.gpsimd
        def _(g):
            g.wait_ge(ctl.sems["s_gd"], ctl.cnt["s_gd"])
            ctl.wait(g, ctl.tok("s_ld"))
            ctl.inc_dma(g.dma_start(out=y_out[0:1024, 0:352], in_=cc_wg_out[0:1024, :]), "s_gd")
            ctl.inc_dma(g.dma_start(out=y_out[1024:1152, 0:22], in_=sgcol.ap()), "s_gd")
            ctl.inc_dma(g.dma_start(out=y_out[1152:1280, 0:22], in_=sucol.ap()), "s_gd")
            ctl.inc_dma(g.dma_start(out=y_out[1280:1408, 0:4], in_=gcol.ap()), "s_gd")
            ctl.inc_dma(g.dma_start(out=y_out[1408:1416, 0:16], in_=cells.ap()), "s_gd")
            ctl.inc_dma(g.dma_start(out=y_out[1416:1544, 0:22], in_=sgxcol.ap()), "s_gd")
            ctl.inc_dma(g.dma_start(out=y_out[1544:1896, 0:1024],
                                    in_=cc_wg_in.rearrange("d r -> r d")), "s_gd")
    elif DEBUG_TAP == "h2a":
        @block.gpsimd
        def _(g):
            g.wait_ge(ctl.sems["s_gd"], ctl.cnt["s_gd"])
            ctl.wait(g, ctl.tok("s_ld"))
            for u in range(2):
                ctl.inc_dma(g.dma_start(
                    out=y_out.rearrange("(a w) d -> a w d", w=256)[0:8, u * 128:(u + 1) * 128, :],
                    in_=h2_spill[0:8, :, u * 1024:(u + 1) * 1024]), "s_gd")

    @block.gpsimd
    def _(g):
        g.wait_ge(ctl.sems["s_gd"], ctl.cnt["s_gd"])

    ctx.close()
    return nc


def _shard_inputs(L0_attn_x, L0_attn_gamma, L0_attn_w_q, L0_attn_w_kv, L0_attn_w_out,
                  L0_ffn_clip_x, L0_ffn_clip_down, L0_ffn_w_gate, L0_ffn_w_up,
                  L0_ffn_w_down, L0_c, L0_wmod, T):
    f32 = lambda a: np.ascontiguousarray(np.asarray(a, dtype=np.float32))
    x = f32(L0_attn_x)
    b, n, d = x.shape
    assert n == 4 * T and d == D
    gamma = f32(L0_attn_gamma); wq = f32(L0_attn_w_q); wkv = f32(L0_attn_w_kv)
    wout = f32(L0_attn_w_out); clipx = f32(L0_ffn_clip_x); clipd = f32(L0_ffn_clip_down)
    wg = f32(L0_ffn_w_gate); wu = f32(L0_ffn_w_up); wd = f32(L0_ffn_w_down)
    cc = f32(L0_c); wmod = f32(L0_wmod)
    in_maps = []
    for core in range(N_CORES):
        bb = core // 4
        qq = core % 4
        rg = slice(core * 352, (core + 1) * 352)
        in_maps.append({
            "x": np.ascontiguousarray(x[bb, qq * T:(qq + 1) * T, :]),
            "gamma": gamma, "wq": wq, "wkv": wkv, "wout": wout,
            "clipx": clipx, "clipd": clipd,
            "wg_sh": np.ascontiguousarray(wg[rg, :]),
            "wu_sh": np.ascontiguousarray(wu[rg, :]),
            "wd_sh": np.ascontiguousarray(wd[core * 128:(core + 1) * 128, :]),
            "cvec": np.ascontiguousarray(cc[bb].reshape(8, 128)),
            "wmod_sh": np.ascontiguousarray(wmod[:, qq * 1536:(qq + 1) * 1536]),
        })
    return in_maps


def kernel(L0_attn_x, L0_attn_gamma, L0_attn_w_q, L0_attn_w_kv, L0_attn_w_out,
           L0_ffn_clip_x, L0_ffn_clip_down, L0_ffn_w_gate, L0_ffn_w_up,
           L0_ffn_w_down, L0_c, L0_wmod):
    T = 2048
    in_maps = _shard_inputs(L0_attn_x, L0_attn_gamma, L0_attn_w_q, L0_attn_w_kv,
                            L0_attn_w_out, L0_ffn_clip_x, L0_ffn_clip_down,
                            L0_ffn_w_gate, L0_ffn_w_up, L0_ffn_w_down, L0_c, L0_wmod, T)
    nc = build_nc(T)
    res = run_bass_kernel_spmd(nc, in_maps, core_ids=list(range(N_CORES)))
    out = np.empty((2, 4 * T, D), dtype=np.float32)
    for core in range(N_CORES):
        bb = core // 4
        qq = core % 4
        out[bb, qq * T:(qq + 1) * T, :] = res.results[core]["y"]
    return out
